# version 06
# baseline (speedup 1.0000x reference)
import numpy as np
import concourse.bass as bass
import concourse.mybir as mybir
from concourse.bass_utils import run_bass_kernel_spmd

F32 = mybir.dt.float32
BF16 = mybir.dt.bfloat16
I32 = mybir.dt.int32
AF = mybir.ActivationFunctionType
ALU = mybir.AluOpType
AX = mybir.AxisListType

B, S, D = 4, 2048, 2048
TT = 1024
DT = 16
NA = 16
FFH = 5120
INTER = 4096
NH = 64
PDIM = 64
NB = 128
LCH = 256
NCH = 8
EPS = 1e-5
STAGES = ("prep", "attna", "ffn1", "attnb", "ffn2")


class P:

    def __init__(self, nc):
        self.nc = nc
        self.cnt = {}
        self.sems = {}

    def sem(self, name):
        if name not in self.sems:
            self.sems[name] = self.nc.semaphore(name).__enter__()
            self.cnt[name] = 0
        return self.sems[name]

    def inc(self, instr, name, amt):
        instr.then_inc(self.sem(name), amt)
        self.cnt[name] += amt

    def wait(self, eng, name):
        eng.wait_ge(self.sem(name), self.cnt[name])

    def wait_lag(self, eng, name, lag):
        eng.wait_ge(self.sem(name), max(0, self.cnt[name] - lag))


def build_program(upto="ffn2", taps=()):
    nc = bass.Bass("TRN2", target_bir_lowering=False)
    p = P(nc)
    g, v, sc, te, sy = nc.gpsimd, nc.vector, nc.scalar, nc.tensor, nc.sync

    inp = {}

    def din(name, shape, dt=F32):
        inp[name] = nc.dram_tensor(name, shape, dt, kind="ExternalInput").ap()
        return inp[name]

    din("xin", [D, TT])
    din("meta", [1, 8], I32)
    din("wprj_cat", [D, 112])
    din("wdt_outT_own", [64, 1024])
    din("bdt_own", [128, 8])
    din("Da_own", [128, 8])
    din("alog_own", [128, 128])
    din("inpT", [D, 8512])
    din("convw_own", [128, 18 * 4])
    din("convb_own", [128, 18])
    din("alogb_own", [32, 1])
    din("dtbias_own", [32, 1])
    din("Db_own", [128, 16])
    din("normw_r", [128, 32])
    din("outT", [INTER, D])
    din("fcT_half", [D // 2, FFH])
    din("projT_half", [FFH // 2, D])

    xout = nc.dram_tensor("xout", [D, TT], F32, kind="ExternalOutput").ap()
    tap = {}
    for tname, tshape in taps:
        tap[tname] = nc.dram_tensor("tap_" + tname, tshape, F32,
                                    kind="ExternalOutput").ap()

    shd = lambda n, s, dt: nc.dram_tensor(n, s, dt, kind="Internal",
                                          addr_space="Shared").ap()
    sh_q2fc = shd("sh_q2fc", [2, D // 2, FFH], BF16)
    sh_q2pj = shd("sh_q2pj", [2, FFH // 2, D], BF16)
    sh_u = shd("sh_u", [2, D, TT], BF16)
    sh_ya = shd("sh_ya", [2, 1024, S], BF16)
    sh_hbc = shd("sh_hbc", [2, 4416, TT], BF16)
    sh_yb = shd("sh_yb", [2, 2048, S], BF16)
    sh_flag = shd("sh_flag", [2, 16], I32)

    gate_dram = nc.dram_tensor("gate_dram", [INTER, TT], BF16, kind="Internal").ap()
    q2fc_loc = nc.dram_tensor("q2fc_loc", [D // 2, FFH], BF16, kind="Internal").ap()
    q2pj_loc = nc.dram_tensor("q2pj_loc", [FFH // 2, D], BF16, kind="Internal").ap()
    hbc_loc = nc.dram_tensor("hbc_loc", [4416, TT], BF16, kind="Internal").ap()
    yb_loc2 = nc.dram_tensor("yb_loc2", [2048, S], BF16, kind="Internal").ap()
    zconv_dram = nc.dram_tensor("zconv_dram", [2, 2048, TT], BF16, kind="Internal").ap()
    ytok_dram = nc.dram_tensor("ytok_dram", [2, 2048, TT], BF16, kind="Internal").ap()
    acum_dram = nc.dram_tensor("acum_dram", [NCH, 32, LCH], F32, kind="Internal").ap()
    h_dram = nc.dram_tensor("h_dram", [FFH, TT], BF16, kind="Internal").ap()

    sbufs = []

    def sbuf(name, shape, dt):
        t = nc.sbuf_tensor(name, shape, dt).__enter__()
        sbufs.append(t)
        return t

    mp = sbuf("mp", [1, 8], I32)
    identB = sbuf("identB", [128, 128], BF16)
    ident64 = sbuf("ident64", [64, 64], BF16)
    ones_col = sbuf("ones_col", [128, 1], BF16)
    ones_colF = sbuf("ones_colF", [128, 1], F32)
    ones_row = sbuf("ones_row", [1, 128], BF16)
    ones_rowF = sbuf("ones_rowF", [1, 128], F32)
    ph_sb = sbuf("ph_sb", [1, 16], I32)
    pf_sb = sbuf("pf_sb", [1, 16], I32)
    A_a = sbuf("A_a", [128, 128], F32)
    convw = sbuf("convw", [128, 72], F32)
    convb = sbuf("convb", [128, 18], F32)
    bdt = sbuf("bdt", [128, 8], F32)
    Da = sbuf("Da", [128, 8], F32)
    d64b = sbuf("d64b", [64, 2048], BF16)
    brawb = sbuf("brawb", [16, 2048], BF16)
    crawb = sbuf("crawb", [16, 2048], BF16)
    wpj = sbuf("wpj", [128, 1792], BF16)
    Ab_col = sbuf("Ab_col", [32, 1], F32)
    dtb_col = sbuf("dtb_col", [32, 1], F32)
    Dbr = sbuf("Dbr", [128, 16], F32)
    nwr = sbuf("nwr", [128, 32], F32)
    srow = sbuf("srow", [1, 64], F32)
    thr_c = sbuf("thr_c", [128, 2], F32)
    sc_ffn = sbuf("sc_ffn", [128, 1], F32)

    xr = sbuf("xr", [128, DT * TT], F32)
    ub = sbuf("ub", [128, DT * TT], BF16)
    cb = sbuf("cb", [128, 16384], BF16)
    ar = sbuf("ar", [128, 16384], BF16)
    ar2 = sbuf("ar2", [128, 4096], BF16)
    wouT = ar2[0:64, 3072:4096]
    wk = sbuf("wk", [128, 2048], F32)
    wk2 = sbuf("wk2", [128, 1024], F32)
    zb = sbuf("zb", [128, 2048], BF16)
    zb2 = sbuf("zb2", [128, 2048], BF16)
    rows = wk[0:1, :]
    mub = cb[:, 0:TT]
    rsb = cb[:, TT:2 * TT]
    st_a = wk2[:, 0:128]

    QS, QG, EV, EP, EA, ET, PO = "qs", "qg", "ev", "ep", "ea", "et", "po"

    p.inc(sy.dma_start(out=mp[:], in_=inp["meta"]), QS, 16)
    p.inc(sy.dma_start(out=bdt[:], in_=inp["bdt_own"]), QS, 16)
    p.inc(sy.dma_start(out=Da[:], in_=inp["Da_own"]), QS, 16)
    p.inc(sy.dma_start(out=wk[:, 0:128], in_=inp["alog_own"]), QS, 16)
    p.inc(sy.dma_start(out=convw[:], in_=inp["convw_own"]), QS, 16)
    p.inc(sy.dma_start(out=convb[:], in_=inp["convb_own"]), QS, 16)
    p.inc(sy.dma_start(out=wk[0:32, 256:257], in_=inp["alogb_own"]), QS, 16)
    p.inc(sy.dma_start(out=dtb_col[:], in_=inp["dtbias_own"]), QS, 16)
    p.inc(sy.dma_start(out=Dbr[:], in_=inp["Db_own"]), QS, 16)
    p.inc(sy.dma_start(out=nwr[:], in_=inp["normw_r"]), QS, 16)

    p.inc(g.memset(identB[:], 1.0), EP, 1)
    p.inc(g.memset(ident64[:], 1.0), EP, 1)
    p.wait(g, EP)
    p.inc(g.affine_select(identB[:], identB[:], [[1, 128]], ALU.is_equal, 0.0,
                          base=0, channel_multiplier=-1), EP, 1)
    p.inc(g.affine_select(ident64[:], ident64[:], [[1, 64]], ALU.is_equal,
                          0.0, base=0, channel_multiplier=-1), EP, 1)
    p.inc(v.memset(ph_sb[:].bitcast(F32), 0.0), EV, 1)
    p.inc(v.memset(ones_col[:], 1.0), EV, 1)
    p.inc(v.memset(ones_colF[:], 1.0), EV, 1)
    p.inc(v.memset(ones_row[:], 1.0), EV, 1)
    p.inc(v.memset(ones_rowF[:], 1.0), EV, 1)

    p.wait(sc, QS)
    p.inc(sc.activation(out=A_a[:], in_=wk[:, 0:128], func=AF.Exp), EA, 1)
    p.inc(sc.activation(out=wk2[0:32, 0:1], in_=wk[0:32, 256:257], func=AF.Exp),
          EA, 1)
    p.wait(v, EA)
    p.inc(v.tensor_scalar_mul(A_a[:], A_a[:], -1.0), EV, 1)
    p.inc(v.tensor_scalar_mul(Ab_col[:], wk2[0:32, 0:1], -1.0), EV, 1)

    p.wait(g, QS)
    r_me = g.alloc_register("r_me")
    r_peer = g.alloc_register("r_peer")
    r_phase = g.alloc_register("r_phase")
    r_want = g.alloc_register("r_want")
    r_poll = g.alloc_register("r_poll")
    r_fl = g.alloc_register("r_fl")
    g.reg_load(r_me, mp[0:1, 0:1])
    g.reg_load(r_peer, mp[0:1, 1:2])
    g.reg_load(r_phase, mp[0:1, 2:3])
    g.reg_load(r_want, mp[0:1, 2:3])
    g.reg_mov(r_poll, 16)
    me_v = g.snap(r_me, min_val=0, max_val=1)
    peer_v = g.snap(r_peer, min_val=0, max_val=1)

    spin_id = [0]
    n_sends = [0]
    flush_id = [0]

    def allwait():
        for e in (v, sc, te, g):
            for snm in (EV, EA, ET, EP, QG):
                p.wait(e, snm)

    def bb_flush():
        flush_id[0] += 1
        nm = f"flush{flush_id[0]}"
        g.br(nm)
        nc.switch_bb(nm)

    def flag_send():
        n_sends[0] += 1
        g.reg_add(r_phase, r_phase, 1)
        g.reg_save(ph_sb[0:1, 0:1], r_phase)
        p.inc(g.dma_start(out=sh_flag[bass.ds(me_v, 1), :], in_=ph_sb[:]), QG, 16)
        return n_sends[0]

    def flag_wait(expect=None):
        k = n_sends[0] if expect is None else expect
        g.reg_load(r_want, mp[0:1, 2:3])
        g.reg_add(r_want, r_want, k)
        sid = spin_id[0]
        spin_id[0] += 1
        lname, cname, ename = f"spin{sid}", f"spin{sid}_c", f"spin{sid}_end"
        g.br(lname)
        with nc.body(lname, valid_engines=None):
            g.dma_start(out=pf_sb[:], in_=sh_flag[bass.ds(peer_v, 1), :]).then_inc(
                p.sem(PO), 16)
            g.wait_ge(p.sem(PO), r_poll)
            g.reg_add(r_poll, r_poll, 16)
            g.reg_load(r_fl, pf_sb[0:1, 0:1])
            g.br_lt(r_fl, r_want, on_true=lname, on_false=cname)
        with nc.body(cname, valid_engines=None):
            g.reg_add(r_fl, r_fl, -99)
            g.br_lt(r_fl, r_want, on_true=ename, on_false=lname)
        nc.switch_bb(ename)

    def finish():
        p.wait(sy, EV)
        p.wait(sy, EA)
        p.wait(sy, EP)
        p.wait(sy, ET)
        for i in range(DT):
            p.inc(sy.dma_start(out=xout[i * 128:(i + 1) * 128, :],
                               in_=xr[:, i * TT:(i + 1) * TT]), QS, 16)
        p.wait(sy, QS)
        p.wait(g, QG)

    def tap_out(name, src_f32_fn, ntiles, tcols):
        if name not in tap:
            return
        for i in range(ntiles):
            p.wait(sy, EV)
            p.wait(sy, EA)
            p.wait(sy, ET)
            src = src_f32_fn(i)
            p.inc(sy.dma_start(out=tap[name][i * 128:(i + 1) * 128, :], in_=src),
                  QS, 16)
        p.wait(sy, QS)

    with nc.psum_tensor("ps_tz", [128, 512], F32) as ps:
        specs = [(inp["fcT_half"], sh_q2fc, D, FFH, 0),
                 (inp["projT_half"], sh_q2pj, FFH, D, 1)]
        p.wait(v, EV)
        p.inc(v.memset(wk2[:, 0:4], 0.0), EV, 1)
        chunks = []
        for wsrc, _, rn, cn, wi in specs:
            half = rn // 2
            for i in range(half // 128):
                for cc in range((cn + 2047) // 2048):
                    c0, c1 = cc * 2048, min(cn, (cc + 1) * 2048)
                    chunks.append((wsrc, i, c0, c1, wi))
        prev_incs = 0
        for k, (wsrc, i, c0, c1, wi) in enumerate(chunks):
            w_ = c1 - c0
            nsub = (w_ + 511) // 512
            buf = xr[:, (k % 2) * 2048:(k % 2) * 2048 + w_]
            p.wait_lag(g, EV, prev_incs if k > 0 else 0)
            p.inc(g.dma_start(out=buf, in_=wsrc[i * 128:(i + 1) * 128, c0:c1]),
                  QG, 16)
            p.wait(v, QG)
            for s8 in range(nsub):
                a, b = s8 * 512, min(w_, (s8 + 1) * 512)
                p.inc(v.tensor_reduce(wk2[:, 4 + s8:5 + s8], buf[:, a:b],
                                      AX.X, ALU.add,
                                      apply_absolute_value=True), EV, 1)
            p.wait(v, EV)
            p.inc(v.tensor_reduce(wk2[:, 2:3], wk2[:, 4:4 + nsub], AX.X,
                                  ALU.add), EV, 1)
            p.wait(v, EV)
            p.inc(v.tensor_tensor(wk2[:, wi:wi + 1], wk2[:, wi:wi + 1],
                                  wk2[:, 2:3], ALU.add), EV, 1)
            p.wait(v, EV)
            prev_incs = nsub + 2
        p.wait(te, EV)
        p.inc(te.matmul(ps[0:1, 0:2], ones_colF[:], wk2[:, 0:2], start=True,
                        stop=True), ET, 1)
        p.wait(v, ET)
        p.inc(v.tensor_copy(srow[0:1, 0:2], ps[0:1, 0:2]), EV, 1)
        p.wait(v, EV)
        p.inc(v.tensor_copy(ph_sb[0:1, 4:6].bitcast(F32), srow[0:1, 0:2]), EV, 1)
        p.wait(g, EV)
        flag_send()
        flag_wait()
        p.inc(g.engine_nop(), EP, 1)
        p.wait(v, EP)
        p.inc(v.tensor_tensor(srow[0:1, 4:6], srow[0:1, 0:2],
                              pf_sb[0:1, 4:6].bitcast(F32), ALU.add), EV, 1)
        p.wait(v, EV)
        p.inc(v.tensor_scalar_mul(srow[0:1, 6:7], srow[0:1, 4:5], 1.0 / (D * FFH)),
              EV, 1)
        p.inc(v.tensor_scalar_mul(srow[0:1, 7:8], srow[0:1, 5:6], 1.0 / (D * FFH)),
              EV, 1)
        p.wait(v, EV)
        p.inc(v.tensor_scalar_max(srow[0:1, 6:8], srow[0:1, 6:8], 1e-5), EV, 1)
        if "scal" in tap:
            p.wait(g, EV)
            p.inc(g.dma_start(out=tap["scal"][0:1, 0:8], in_=srow[0:1, 0:8]), QG, 16)
            p.inc(g.dma_start(out=tap["scal"][0:1, 8:24], in_=pf_sb[:].bitcast(F32)), QG, 16)
            p.inc(g.dma_start(out=tap["scal"][0:1, 24:32], in_=ph_sb[0:1, 0:8].bitcast(F32)), QG, 16)
            p.wait(g, QG)
        p.wait(v, EV)
        p.inc(v.tensor_scalar_mul(srow[0:1, 8:10], srow[0:1, 6:8], 0.7), EV, 1)
        p.wait(v, EV)
        p.inc(v.tensor_tensor(srow[0:1, 10:11], srow[0:1, 6:7], srow[0:1, 6:7],
                              ALU.mult), EV, 1)
        p.wait(v, EV)
        p.inc(v.tensor_tensor(srow[0:1, 10:11], srow[0:1, 10:11], srow[0:1, 7:8],
                              ALU.mult), EV, 1)
        p.wait(v, EV)
        p.inc(v.tensor_scalar_mul(srow[0:1, 10:11], srow[0:1, 10:11], 0.125), EV, 1)
        p.wait(te, EV)
        p.inc(te.matmul(ps[:, 0:4], ones_rowF[:], srow[0:1, 8:12], start=True,
                        stop=True), ET, 1)
        p.wait(v, ET)
        p.inc(v.tensor_copy(sc_ffn[:], ps[:, 2:3]), EV, 1)

        bounce = {0: q2fc_loc, 1: q2pj_loc}
        for wsrc, wdst, rn, cn, wi in specs:
            p.wait(v, ET)
            p.inc(v.tensor_copy(thr_c[:, 0:1], ps[:, wi:wi + 1]), EV, 1)
            p.wait(v, EV)
            p.inc(v.tensor_scalar_mul(thr_c[:, 1:2], thr_c[:, 0:1], -1.0), EV, 1)
            half = rn // 2
            for i in range(half // 128):
                nch_c = (cn + 2047) // 2048
                for cc in range(nch_c):
                    c0, c1 = cc * 2048, min(cn, (cc + 1) * 2048)
                    w_ = c1 - c0
                    p.wait(g, EV)
                    p.wait(g, EA)
                    p.inc(g.dma_start(out=wk[:, 0:w_],
                                      in_=wsrc[i * 128:(i + 1) * 128, c0:c1]),
                          QG, 16)
                    p.wait(sc, QG)
                    p.wait(sc, EV)
                    p.inc(sc.activation(out=zb[:, 0:w_], in_=wk[:, 0:w_],
                                        func=AF.Sign, bias=thr_c[:, 0:1]), EA, 1)
                    p.inc(sc.activation(out=zb2[:, 0:w_], in_=wk[:, 0:w_],
                                        func=AF.Sign, bias=thr_c[:, 1:2]), EA, 1)
                    p.wait(v, EA)
                    p.inc(v.tensor_tensor(zb[:, 0:w_], zb[:, 0:w_], zb2[:, 0:w_],
                                          ALU.add), EV, 1)
                    p.wait(g, EV)
                    p.inc(g.dma_start(out=bounce[wi][i * 128:(i + 1) * 128, c0:c1],
                                      in_=zb[:, 0:w_]), QG, 16)
        p.wait(g, QG)
        p.inc(g.dma_start(out=sh_q2fc[bass.ds(me_v, 1), :, :], in_=q2fc_loc),
              QG, 16)
        p.inc(g.dma_start(out=sh_q2pj[bass.ds(me_v, 1), :, :], in_=q2pj_loc),
              QG, 16)
        p.wait(g, QG)
        flag_send()

    if upto == "tern":
        if "q2fc" in tap:
            p.wait(sy, QG)
            flag_wait()
            for cc in range(2):
                p.wait(sy, EV)
                p.inc(sy.dma_start(out=zb[:, 0:2048],
                                   in_=sh_q2fc[0, 0:128, cc * 2048:(cc + 1) * 2048]),
                      QS, 16)
                p.wait(v, QS)
                p.inc(v.tensor_copy(wk[:, 0:2048], zb[:, 0:2048]), EV, 1)
                p.wait(sy, EV)
                p.inc(sy.dma_start(out=tap["q2fc"][:, cc * 2048:(cc + 1) * 2048],
                                   in_=wk[:, 0:2048]), QS, 16)
        if "scffn" in tap:
            p.wait(sy, EV)
            p.inc(sy.dma_start(out=tap["scffn"], in_=sc_ffn[:]), QS, 16)
        finish()
        return nc

    _ln_n = [0]

    def ln_to_u(x_tile_fn, u_arena, send_shared):
        _ln_n[0] += 1
        with nc.psum_tensor("ps_ln%d" % _ln_n[0], [128, 2048], F32) as ps_ln:
            p.wait(v, QG)
            for i in range(DT):
                xi = x_tile_fn(i)
                p.inc(v.tensor_copy(u_arena[:, i * TT:(i + 1) * TT], xi), EV, 1)
                p.wait(sc, ET)
                p.wait(sc, EV)
                p.inc(sc.activation(out=zb[:, 0:TT], in_=xi,
                                    func=AF.Square), EA, 1)
                p.wait(te, EV)
                p.wait(te, EA)
                sl = u_arena[:, i * TT:(i + 1) * TT]
                for tch in range(2):
                    c = slice(tch * 512, (tch + 1) * 512)
                    p.inc(te.matmul(ps_ln[0:1, tch * 512:(tch + 1) * 512],
                                    ones_col[:], sl[:, c], start=(i == 0),
                                    stop=(i == DT - 1)), ET, 1)
                    p.inc(te.matmul(ps_ln[0:1, 1024 + tch * 512:1536 + tch * 512],
                                    ones_col[:], zb[:, 0:TT][:, c],
                                    start=(i == 0), stop=(i == DT - 1)), ET, 1)
            p.wait(v, ET)
            p.inc(v.tensor_scalar_mul(rows[0:1, 0:TT], ps_ln[0:1, 0:TT], 1.0 / D),
                  EV, 1)
            p.inc(v.tensor_scalar_mul(rows[0:1, TT:2 * TT],
                                      ps_ln[0:1, 1024:1024 + TT], 1.0 / D), EV, 1)
            p.wait(v, EV)
            p.inc(v.tensor_tensor(wk2[0:1, 0:TT], rows[0:1, 0:TT],
                                  rows[0:1, 0:TT], ALU.mult), EV, 1)
            p.wait(v, EV)
            p.inc(v.tensor_tensor(rows[0:1, TT:2 * TT], rows[0:1, TT:2 * TT],
                                  wk2[0:1, 0:TT], ALU.subtract), EV, 1)
            p.wait(v, EV)
            p.inc(v.tensor_scalar_add(rows[0:1, TT:2 * TT], rows[0:1, TT:2 * TT],
                                      EPS), EV, 1)
            p.wait(v, EV)
            p.inc(v.reciprocal(wk2[0:1, 0:TT], rows[0:1, TT:2 * TT]), EV, 1)
            p.wait(sc, EV)
            p.inc(sc.activation(out=rows[0:1, TT:2 * TT], in_=wk2[0:1, 0:TT],
                                func=AF.Sqrt), EA, 1)
            p.wait(v, EA)
            p.inc(v.tensor_copy(zb[0:1, 0:TT], rows[0:1, 0:TT]), EV, 1)
            p.inc(v.tensor_copy(zb[0:1, TT:2 * TT], rows[0:1, TT:2 * TT]),
                  EV, 1)
            p.wait(te, EV)
            for tch in range(2):
                p.inc(te.matmul(ps_ln[:, tch * 512:(tch + 1) * 512], ones_row[:],
                                zb[0:1, tch * 512:(tch + 1) * 512], start=True,
                                stop=True), ET, 1)
                p.inc(te.matmul(ps_ln[:, 1024 + tch * 512:1536 + tch * 512],
                                ones_row[:],
                                zb[0:1, TT + tch * 512:TT + (tch + 1) * 512],
                                start=True, stop=True), ET, 1)
            p.wait(v, ET)
            p.inc(v.tensor_copy(mub[:], ps_ln[:, 0:TT]), EV, 1)
            p.inc(v.tensor_copy(rsb[:], ps_ln[:, 1024:1024 + TT]), EV, 1)
            for i in range(DT):
                sl = u_arena[:, i * TT:(i + 1) * TT]
                p.wait(v, EV)
                p.inc(v.tensor_tensor(zb2[:, 0:TT], sl, mub[:],
                                      ALU.subtract), EV, 1)
                p.wait(v, EV)
                p.inc(v.tensor_tensor(sl, zb2[:, 0:TT], rsb[:],
                                      ALU.mult), EV, 1)


    def xin_tile(i):
        p.wait(g, EV)
        p.wait(g, EA)
        slot = (i % 2) * TT
        p.inc(g.dma_start(out=wk[:, slot:slot + TT],
                          in_=inp["xin"][i * 128:(i + 1) * 128, :]), QG, 16)
        p.wait(v, QG)
        p.wait(sc, QG)
        return wk[:, slot:slot + TT]

    ln_to_u(xin_tile, ub, None)
    p.wait(g, EV)
    u_dst = sh_u[bass.ds(me_v, 1), :, :].rearrange("a (i q) t -> (a q) i t", q=128)
    p.inc(g.dma_start(out=u_dst, in_=ub[:, :].rearrange("q (i t) -> q i t", i=DT)),
          QG, 16)
    p.wait(g, QG)
    flag_send()
    flag_wait()

    if "u" in tap:
        p.wait(g, EV)
        for i in range(DT):
            p.inc(g.dma_start(out=tap["u"][i * 128:(i + 1) * 128, :],
                              in_=ub[:, i * TT:(i + 1) * TT]), QG, 16)
        p.wait(g, QG)

    if upto == "u":
        finish()
        return nc


    p.inc(g.dma_start(out=wpj[:, 0:DT * 112].rearrange("q (k r) -> q k r", k=DT),
                      in_=inp["wprj_cat"].rearrange("(k q) r -> q k r", q=128)),
          QG, 16)
    p.inc(g.dma_start(out=wouT[:], in_=inp["wdt_outT_own"]), QG, 16)

    with nc.psum_tensor("ps_pr", [128, 2048], F32) as ps_pr:
        for th in range(2):
            for i in range(DT):
                sl = (i % 2) * TT
                p.wait(g, ET)
                p.inc(g.dma_start(out=zb[:, sl:sl + TT],
                                  in_=sh_u[th, i * 128:(i + 1) * 128, :]), QG, 16)
                p.wait(te, QG)
                first, last = (i == 0), (i == DT - 1)
                for tc in range(2):
                    cc = slice(sl + tc * 512, sl + (tc + 1) * 512)
                    oc = slice(th * 1024 + tc * 512, th * 1024 + (tc + 1) * 512)
                    p.inc(te.matmul(ps_pr[0:112, oc],
                                    wpj[:, i * 112:(i + 1) * 112], zb[:, cc],
                                    start=first, stop=last), ET, 1)
        p.wait(v, ET)
        p.inc(v.tensor_copy(d64b[:], ps_pr[0:64, :]), EV, 1)
        p.inc(v.tensor_copy(brawb[:], ps_pr[64:80, :]), EV, 1)
        p.inc(v.tensor_copy(crawb[:], ps_pr[96:112, :]), EV, 1)
        p.wait(te, EV)

        for ht in range(8):
            p.wait(te, EV)
            p.wait(te, EA)
            for tc in range(4):
                cc = slice(tc * 512, (tc + 1) * 512)
                p.inc(te.matmul(ps_pr[:, cc], wouT[:, ht * 128:(ht + 1) * 128],
                                d64b[:, cc], start=True, stop=True), ET, 1)
            for th2 in range(2):
                hc = slice(th2 * TT, (th2 + 1) * TT)
                p.wait(sc, ET)
                p.wait(sc, EV)
                p.inc(sc.activation(out=wk[:, 0:TT], in_=ps_pr[:, hc], func=AF.Abs,
                                    bias=bdt[:, ht:ht + 1]), EA, 1)
                p.wait(sc, EA)
                p.inc(sc.activation(out=wk[:, TT:2 * TT], in_=wk[:, 0:TT],
                                    func=AF.Exp, scale=-1.0), EA, 1)
                p.wait(sc, EA)
                p.inc(sc.activation(out=wk[:, 0:TT], in_=wk[:, TT:2 * TT],
                                    func=AF.Ln, bias=1.0), EA, 1)
                p.wait(v, EA)
                p.inc(v.tensor_scalar(wk2[:, 0:TT], ps_pr[:, hc],
                                      bdt[:, ht:ht + 1], 0.0, ALU.add, ALU.max),
                      EV, 1)
                p.wait(v, EV)
                p.inc(v.tensor_tensor(cb[:, ht * 2048 + th2 * TT:
                                         ht * 2048 + (th2 + 1) * TT],
                                      wk2[:, 0:TT], wk[:, 0:TT], ALU.add), EV, 1)

    if upto == "dt":
        if "dt" in tap:
            p.wait(g, EV)
            for ht in range(8):
                p.inc(g.dma_start(out=tap["dt"][ht * 128:(ht + 1) * 128, :],
                                  in_=cb[:, ht * 2048:(ht + 1) * 2048]), QG, 16)
        if "braw" in tap:
            p.wait(g, EV)
            p.inc(g.dma_start(out=tap["braw"][0:16, :], in_=brawb[:]), QG, 16)
            p.inc(g.dma_start(out=tap["braw"][16:32, :], in_=crawb[:]), QG, 16)
        p.wait(g, QG)
        finish()
        return nc

    A_own = A_a
    ubb = xr[:, :].bitcast(BF16)
    with (nc.psum_tensor("ps_y", [128, 1024], F32) as ps_y,
          nc.psum_tensor("ps_w", [128, 1024], F32) as ps_w):
        for th in range(2):
            p.wait(g, EV)
            p.wait(g, EA)
            src = sh_u[th, :, :][bass.ds(me_v * 1024, 1024), :].rearrange(
                "(i q) t -> q i t", q=128)
            p.inc(g.dma_start(out=ar[:, th * 8 * TT:(th + 1) * 8 * TT].rearrange(
                "q (i t) -> q i t", i=8), in_=src), QG, 16)
            for n in range(NA):
                for tc in range(2):
                    cc = slice(th * TT + tc * 512, th * TT + (tc + 1) * 512)
                    p.wait(te, EA)
                    p.wait(te, EV)
                    p.inc(te.matmul(ps_w[:, tc * 512:(tc + 1) * 512],
                                    ident64[0:16, n:n + 1].broadcast_to([16, 128]),
                                    brawb[:, cc], start=True, stop=True), ET, 1)
                    p.inc(te.matmul(ps_w[:, 512 + tc * 512:512 + (tc + 1) * 512]
                                    if False else ps_y[:, tc * 512:(tc + 1) * 512],
                                    ident64[0:16, n:n + 1].broadcast_to([16, 128]),
                                    crawb[:, cc], start=True, stop=True), ET, 1)
                p.wait(sc, ET)
                p.inc(sc.copy(ubb[:, n * TT:(n + 1) * TT], ps_w[:, 0:TT]), EA, 1)
                p.inc(sc.copy(ubb[:, 16384 + n * TT:16384 + (n + 1) * TT],
                              ps_y[:, 0:TT]), EA, 1)
            p.wait(g, QG)
            for ht in range(8):
                dtsl = cb[:, ht * 2048 + th * TT:ht * 2048 + (th + 1) * TT]
                ulsl = ar[:, th * 8 * TT + ht * TT:th * 8 * TT + (ht + 1) * TT]
                p.wait(v, QG)
                p.wait(v, EA)
                p.inc(v.tensor_tensor(zb[:, 0:TT], dtsl, ulsl, ALU.mult), EV, 1)
                for n in range(NA):
                    col = ht * 16 + n
                    p.wait(sc, EV)
                    p.wait(sc, ET)
                    p.inc(sc.activation(out=wk[:, 0:TT], in_=dtsl, func=AF.Exp,
                                        scale=A_own[:, col:col + 1]), EA, 1)
                    p.wait(v, EA)
                    p.inc(v.tensor_tensor(zb[:, TT:2 * TT], zb[:, 0:TT],
                                          ubb[:, n * TT:(n + 1) * TT], ALU.mult),
                          EV, 1)
                    p.wait(v, EV)
                    if th == 0:
                        p.inc(v.tensor_tensor_scan(zb2[:, TT:2 * TT],
                                                   wk[:, 0:TT], zb[:, TT:2 * TT],
                                                   0.0, ALU.mult, ALU.add), EV, 1)
                    else:
                        p.inc(v.tensor_tensor_scan(zb2[:, TT:2 * TT],
                                                   wk[:, 0:TT], zb[:, TT:2 * TT],
                                                   st_a[:, col:col + 1],
                                                   ALU.mult, ALU.add), EV, 1)
                    p.wait(v, EV)
                    if th == 0:
                        p.inc(v.tensor_copy(st_a[:, col:col + 1],
                                            zb2[:, 2 * TT - 1:2 * TT]), EV, 1)
                    p.inc(v.tensor_tensor(zb2[:, 0:TT], zb2[:, TT:2 * TT],
                                          ubb[:, 16384 + n * TT:(n + 1) * TT
                                              + 16384], ALU.mult), EV, 1)
                    p.wait(te, EV)
                    for tc in range(2):
                        cc = slice(tc * 512, (tc + 1) * 512)
                        p.inc(te.matmul(ps_y[:, cc], identB[:], zb2[:, 0:TT][:, cc],
                                        start=(n == 0), stop=(n == NA - 1)), ET, 1)
                p.wait(v, ET)
                p.inc(v.scalar_tensor_tensor(
                    ub[:, ht * 2048 + th * TT:ht * 2048 + (th + 1) * TT], ulsl,
                    Da[:, ht:ht + 1], ps_y[:], ALU.mult, ALU.add), EV, 1)

    p.wait(g, EV)
    ya_dst = sh_ya[bass.ds(me_v, 1), :, :].rearrange("a (i q) t -> (a q) i t", q=128)
    p.inc(g.dma_start(out=ya_dst, in_=ub[:, :].rearrange("q (i t) -> q i t", i=8)),
          QG, 16)
    p.wait(g, QG)
    flag_send()
    flag_wait()

    for sl_i in range(2):
        src = sh_ya[sl_i, :, :][:, bass.ds(me_v * 1024, 1024)].rearrange(
            "(i q) t -> q i t", q=128)
        p.inc(g.dma_start(out=ar[:, sl_i * 8 * TT:(sl_i + 1) * 8 * TT].rearrange(
            "q (i t) -> q i t", i=8), in_=src), QG, 16)
    for i in range(DT):
        sl = (i % 2) * TT
        p.wait(sy, EV)
        p.wait(sy, EA)
        p.inc(sy.dma_start(out=wk[:, sl:sl + TT],
                           in_=inp["xin"][i * 128:(i + 1) * 128, :]), QS, 16)
        p.wait(v, QS)
        p.wait(v, QG)
        p.inc(v.tensor_tensor(xr[:, i * TT:(i + 1) * TT], wk[:, sl:sl + TT],
                              ar[:, i * TT:(i + 1) * TT], ALU.add), EV, 1)

    if "x1" in tap:
        p.wait(sy, EV)
        for i in range(DT):
            p.inc(sy.dma_start(out=tap["x1"][i * 128:(i + 1) * 128, :],
                               in_=xr[:, i * TT:(i + 1) * TT]), QS, 16)
    if upto == "x1":
        finish()
        return nc

    def ffn_block(first):
        def xr_tile(i):
            return xr[:, i * TT:(i + 1) * TT]
        ln_to_u(xr_tile, ub, None)
        if first:
            flag_wait(expect=2)
        sfx = "a" if first else "b"
        with (nc.psum_tensor("ps_f0" + sfx, [128, 1024], F32) as f0,
              nc.psum_tensor("ps_f1" + sfx, [128, 1024], F32) as f1,
              nc.psum_tensor("ps_f2" + sfx, [128, 1024], F32) as f2,
              nc.psum_tensor("ps_f3" + sfx, [128, 1024], F32) as f3):
            ps4 = [f0, f1, f2, f3]
            for mb in range(10):
                buf = (mb % 2) * 8192
                mc = slice(mb * 512, (mb + 1) * 512)
                p.wait_lag(g, ET, 128 if mb > 0 else 0)
                for sl_i in range(2):
                    src = sh_q2fc[sl_i, :, mc].rearrange("(k q) m -> q k m", q=128)
                    p.inc(g.dma_start(
                        out=ar[:, buf + sl_i * 4096:buf + (sl_i + 1) * 4096]
                        .rearrange("q (k m) -> q k m", k=8), in_=src), QG, 16)
                p.wait(te, QG)
                p.wait(te, EV)
                p.wait(te, EA)
                for m in range(4):
                    for k in range(DT):
                        lhs = ar[:, buf + k * 512 + m * 128:
                                 buf + k * 512 + (m + 1) * 128]
                        for tch in range(2):
                            p.inc(te.matmul(
                                ps4[m][:, tch * 512:(tch + 1) * 512], lhs,
                                ub[:, k * TT + tch * 512:k * TT + (tch + 1) * 512],
                                start=(k == 0), stop=(k == DT - 1)), ET, 1)
                for m in range(4):
                    p.wait(v, ET)
                    p.wait(v, QG)
                    p.wait(v, EA)
                    p.inc(v.tensor_scalar_max(wk[:, 0:TT], ps4[m][:, 0:TT], 0.0),
                          EV, 1)
                    p.wait(sc, EV)
                    p.inc(sc.activation(out=zb[:, TT:2 * TT], in_=wk[:, 0:TT],
                                        func=AF.Square), EA, 1)
                    p.wait(g, EA)
                    p.inc(g.dma_start(out=h_dram[(mb * 4 + m) * 128:
                                                 (mb * 4 + m + 1) * 128, :],
                                      in_=zb[:, TT:2 * TT]), QG, 16)
        with (nc.psum_tensor("ps_p0" + sfx, [128, 1024], F32) as p0,
              nc.psum_tensor("ps_p1" + sfx, [128, 1024], F32) as p1,
              nc.psum_tensor("ps_p2" + sfx, [128, 1024], F32) as p2,
              nc.psum_tensor("ps_p3" + sfx, [128, 1024], F32) as p3):
            pp = [p0, p1, p2, p3]
            p.wait(g, QG)
            for m2b in range(4):
                mc = slice(m2b * 512, (m2b + 1) * 512)
                for k2 in range(40):
                    wbuf = (k2 % 2) * 1024
                    hbuf = 2048 + (k2 % 2) * 1024
                    sl_i, row = k2 // 20, (k2 % 20) * 128
                    p.wait_lag(g, ET, 8)
                    p.inc(g.dma_start(out=ar[:, wbuf:wbuf + 512],
                                      in_=sh_q2pj[sl_i, row:row + 128, mc]),
                          QG, 16)
                    p.inc(g.dma_start(out=ar[:, hbuf:hbuf + 1024],
                                      in_=h_dram[k2 * 128:(k2 + 1) * 128, :]),
                          QG, 16)
                    p.wait(te, QG)
                    if k2 == 0:
                        p.wait(te, EV)
                    for m2 in range(4):
                        lhs = ar[:, wbuf + m2 * 128:wbuf + (m2 + 1) * 128]
                        for tch in range(2):
                            p.inc(te.matmul(
                                pp[m2][:, tch * 512:(tch + 1) * 512], lhs,
                                ar[:, hbuf + tch * 512:hbuf + (tch + 1) * 512],
                                start=(k2 == 0), stop=(k2 == 39)), ET, 1)
                for m2 in range(4):
                    d_i = m2b * 4 + m2
                    p.wait(v, ET)
                    p.inc(v.scalar_tensor_tensor(
                        xr[:, d_i * TT:(d_i + 1) * TT], pp[m2][:, 0:TT],
                        sc_ffn[:, 0:1], xr[:, d_i * TT:(d_i + 1) * TT],
                        ALU.mult, ALU.add), EV, 1)

    ffn_block(first=True)
    if "rows2" in tap:
        p.wait(sy, EV)
        p.wait(sy, EA)
        p.inc(sy.dma_start(out=tap["rows2"], in_=wk[0:1, :]), QS, 16)
    if "u2" in tap:
        p.wait(g, EV)
        for i in range(DT):
            p.inc(g.dma_start(out=tap["u2"][i * 128:(i + 1) * 128, :],
                              in_=ub[:, i * TT:(i + 1) * TT]), QG, 16)
    if "hff" in tap:
        p.wait(g, QG)
        for i in range(8):
            p.inc(g.dma_start(out=wk[:, 0:TT], in_=h_dram[i * 128:(i + 1) * 128, :]), QG, 16)
            p.wait(g, QG)
            p.inc(g.dma_start(out=tap["hff"][i * 128:(i + 1) * 128, :], in_=wk[:, 0:TT]), QG, 16)
            p.wait(g, QG)
    if "x2" in tap:
        p.wait(sy, EV)
        for i in range(DT):
            p.inc(sy.dma_start(out=tap["x2"][i * 128:(i + 1) * 128, :],
                               in_=xr[:, i * TT:(i + 1) * TT]), QS, 16)
    if upto == "x2":
        finish()
        return nc

    def xr_tile2(i):
        return xr[:, i * TT:(i + 1) * TT]

    ln_to_u(xr_tile2, ub, None)
    if "u3d" in tap:
        p.wait(g, EV)
        for i in range(DT):
            p.inc(g.dma_start(out=tap["u3d"][i * 128:(i + 1) * 128, :],
                              in_=ub[:, i * TT:(i + 1) * TT]), QG, 16)
        p.wait(g, QG)

    with (nc.psum_tensor("ps_i0", [128, 1024], F32) as i0,
          nc.psum_tensor("ps_i1", [128, 1024], F32) as i1,
          nc.psum_tensor("ps_i2", [128, 1024], F32) as i2,
          nc.psum_tensor("ps_i3", [128, 1024], F32) as i3):
        ps4b = [i0, i1, i2, i3]
        for mb in range(17):
            m0 = mb * 512
            mw = min(8512, (mb + 1) * 512) - m0
            buf = (mb % 2) * 8192
            p.wait(g, ET)
            p.inc(g.dma_start(
                out=ar[:, buf:buf + 16 * mw].rearrange("q (k m) -> q k m", k=DT),
                in_=inp["inpT"][:, m0:m0 + mw].rearrange("(k q) m -> q k m", q=128)),
                QG, 16)
            p.wait(te, QG)
            p.wait(te, EV)
            p.wait(te, EA)
            ntile = (mw + 127) // 128
            for m in range(ntile):
                mrows = min(mw - m * 128, 128)
                for k in range(DT):
                    lhs = ar[:, buf + k * mw + m * 128:buf + k * mw + m * 128 + mrows]
                    for tch in range(2):
                        p.inc(te.matmul(
                            ps4b[m][0:mrows, tch * 512:(tch + 1) * 512], lhs,
                            ub[:, k * TT + tch * 512:k * TT + (tch + 1) * 512],
                            start=(k == 0), stop=(k == DT - 1)), ET, 1)
            for m in range(ntile):
                mt = mb * 4 + m
                mrows = min(mw - m * 128, 128)
                if mt < 32:
                    p.wait(sc, ET)
                    p.wait(sc, QG)
                    p.inc(sc.activation(out=zb[:, 0:TT], in_=ps4b[m][:, 0:TT],
                                        func=AF.Silu), EA, 1)
                    p.wait(g, EA)
                    p.inc(g.dma_start(out=gate_dram[mt * 128:(mt + 1) * 128, :],
                                      in_=zb[:, 0:TT]), QG, 16)
                else:
                    row = (mt - 32) * 128
                    p.wait(v, ET)
                    p.wait(v, QG)
                    p.inc(v.tensor_copy(zb2[0:mrows, 0:TT], ps4b[m][0:mrows, 0:TT]),
                          EV, 1)
                    p.wait(g, EV)
                    p.inc(g.dma_start(out=hbc_loc[row:row + mrows, :],
                                      in_=zb2[0:mrows, 0:TT]), QG, 16)
    if "sg2" in tap:
        p.wait(g, QG)
        for j in range(16):
            p.inc(g.dma_start(out=zb[:, 0:TT], in_=gate_dram[j * 128:(j + 1) * 128, :]), QG, 16)
            p.wait(g, QG)
            p.inc(g.dma_start(out=tap["sg2"][j * 128:(j + 1) * 128, :], in_=zb[:, 0:TT]), QG, 16)
            p.wait(g, QG)
    if "hbcl" in tap:
        p.wait(g, QG)
        for j in range(6):
            p.inc(g.dma_start(out=zb[:, 0:TT], in_=hbc_loc[j * 128:(j + 1) * 128, :]), QG, 16)
            p.wait(g, QG)
            p.inc(g.dma_start(out=tap["hbcl"][j * 128:(j + 1) * 128, :], in_=zb[:, 0:TT]), QG, 16)
            p.wait(g, QG)
    p.wait(g, QG)
    p.inc(g.dma_start(out=sh_hbc[bass.ds(me_v, 1), :, :], in_=hbc_loc), QG, 16)
    p.wait(g, QG)
    flag_send()
    flag_wait()

    p.inc(g.dma_start(out=d64b[0:32, :].rearrange("q (s t) -> q s t", s=2),
                      in_=sh_hbc[:, :, :][:, bass.ds(me_v * 32 + 4352, 32), :]
                      .rearrange("s q t -> q s t")), QG, 16)
    p.inc(g.dma_start(out=zconv_dram,
                      in_=sh_hbc[:, :, :][:, bass.ds(me_v * 2048, 2048), :]),
          QG, 16)
    adt = ar[0:64, 10240:14336].bitcast(F32)
    identF = ar[:, 14336:14592].bitcast(F32)
    BT0 = ar[:, 14592:14720]
    BT1 = ar[:, 14720:14848]
    zsv = ar[:, 14848:14896]
    Ssb = ar[:, 8192:10240]
    p.wait(v, QG)
    p.inc(v.tensor_copy(identF, identB[:]), EV, 1)
    p.inc(v.memset(Ssb, 0.0), EV, 1)
    for th2 in range(2):
        hc = slice(th2 * TT, (th2 + 1) * TT)
        p.wait(sc, EV)
        p.inc(sc.activation(out=wk[0:32, 0:TT], in_=d64b[0:32, hc], func=AF.Abs,
                            bias=dtb_col[:, 0:1]), EA, 1)
        p.wait(sc, EA)
        p.inc(sc.activation(out=wk[0:32, TT:2 * TT], in_=wk[0:32, 0:TT],
                            func=AF.Exp, scale=-1.0), EA, 1)
        p.wait(sc, EA)
        p.inc(sc.activation(out=wk[0:32, 0:TT], in_=wk[0:32, TT:2 * TT],
                            func=AF.Ln, bias=1.0), EA, 1)
        p.wait(v, EA)
        p.inc(v.tensor_scalar(wk2[0:32, 0:TT], d64b[0:32, hc], dtb_col[:, 0:1],
                              0.0, ALU.add, ALU.max), EV, 1)
        p.wait(v, EV)
        p.inc(v.tensor_tensor(wk2[0:32, 0:TT], wk2[0:32, 0:TT], wk[0:32, 0:TT],
                              ALU.add), EV, 1)
        p.wait(v, EV)
        p.inc(v.tensor_scalar_mul(adt[0:32, hc], wk2[0:32, 0:TT],
                                  Ab_col[:, 0:1]), EV, 1)
        p.inc(v.tensor_copy(d64b[0:32, hc], wk2[0:32, 0:TT]), EV, 1)

    p.wait(g, QG)
    p.wait(g, EV)
    p.inc(g.memset(ar2[0:32, 2048:4096], 1.0), EP, 1)
    p.wait(g, EP)
    p.inc(g.affine_select(ar2[0:32, 2048:4096].rearrange("q (a b) -> q a b", a=32),
                          ar2[0:32, 2048:4096].rearrange("q (a b) -> q a b", a=32),
                          [[1, 32], [0, 64]], ALU.is_equal, 0.0,
                          base=0, channel_multiplier=-1), EP, 1)

    p.wait(g, EV)
    p.wait(g, EA)
    p.inc(g.memset(zb[:, 1408:1920], 0.0), EP, 1)
    p.wait(g, EP)
    for sch in range(2):
        p.inc(g.affine_select(zb[:, 1408 + sch * 256:1664 + sch * 256],
                              zb[:, 1408 + sch * 256:1664 + sch * 256],
                              [[1, LCH]], ALU.is_ge, -1e30,
                              base=-(sch * 128), channel_multiplier=-1), EP, 1)

    with (nc.psum_tensor("ps_gm", [128, 1024], F32) as ps_gm,
          nc.psum_tensor("ps_y2", [128, 512], F32) as ps_y2,
          nc.psum_tensor("ps_sd", [128, 512], F32) as ps_sd,
          nc.psum_tensor("ps_t", [128, 512], F32) as ps_t,
          nc.psum_tensor("ps_tb", [128, 1024], BF16) as ps_tb,
          nc.psum_tensor("ps_x", [128, 512], F32) as ps_x,
          nc.psum_tensor("ps_m", [128, 512], F32) as ps_m):
        for th in range(2):
            p.wait(g, EV)
            p.wait(g, EA)
            p.wait(g, ET)
            src = zconv_dram[th, :, :].rearrange("(i q) t -> q i t", q=128)
            p.inc(g.dma_start(out=ub[:, 0:16 * TT].rearrange(
                "q (i t) -> q i t", i=16), in_=src), QG, 16)
            def conv_tile(zsrc, wj, dst, edge_src):
                p.wait(v, EA)
                p.inc(v.memset(wk[:, 0:3], 0.0), EV, 1)
                p.inc(v.tensor_single_scalar(wk[:, 3:TT], zsrc[:, 0:TT - 3],
                                             convw[:, wj * 4:wj * 4 + 1],
                                             ALU.mult), EV, 1)
                for kk in range(1, 4):
                    p.wait(v, EV)
                    p.inc(v.scalar_tensor_tensor(
                        wk[:, 3 - kk:TT], zsrc[:, 0:TT - 3 + kk],
                        convw[:, wj * 4 + kk:wj * 4 + kk + 1],
                        wk[:, 3 - kk:TT], ALU.mult, ALU.add), EV, 1)
                if edge_src is not None:
                    for kk in range(3):
                        p.wait(v, EV)
                        p.inc(v.scalar_tensor_tensor(
                            wk[:, 0:3 - kk], edge_src[:, kk:3],
                            convw[:, wj * 4 + kk:wj * 4 + kk + 1],
                            wk[:, 0:3 - kk], ALU.mult, ALU.add), EV, 1)
                p.wait(sc, EV)
                p.inc(sc.activation(out=dst, in_=wk[:, 0:TT], func=AF.Silu,
                                    bias=convb[:, wj:wj + 1]), EA, 1)

            p.wait(v, QG)
            for j in range(16):
                edge = zsv[:, j * 3:j * 3 + 3] if th == 1 else None
                conv_tile(ub[:, j * TT:(j + 1) * TT], j,
                          cb[:, j * TT:(j + 1) * TT], edge)
            for j in (16, 17):
                base = 4096 + (j - 16) * 128
                p.wait(g, EV)
                p.wait(g, ET)
                p.inc(g.dma_start(out=zb[:, 0:TT], in_=sh_hbc[th, base:base + 128, :]),
                      QG, 16)
                if th == 1:
                    p.inc(g.dma_start(out=zb[:, TT:TT + 3],
                                      in_=sh_hbc[0, base:base + 128, TT - 3:TT]),
                          QG, 16)
                p.wait(v, QG)
                edge = zb[:, TT:TT + 3] if th == 1 else None
                conv_tile(zb[:, 0:TT], j, ar2[:, (j - 16) * TT:(j - 15) * TT], edge)
            if th == 0:
                for j in range(16):
                    p.wait(v, EA)
                    p.inc(v.tensor_copy(zsv[:, j * 3:j * 3 + 3],
                                        ub[:, j * TT + TT - 3:(j + 1) * TT]), EV, 1)
            if "bc" in tap:
                p.wait(g, EA)
                p.inc(g.dma_start(out=tap["bc"][:, th * TT:(th + 1) * TT],
                                  in_=ar2[:, 0:TT]), QG, 16)
                p.inc(g.dma_start(out=tap["bc"][:, 2048 + th * TT:2048 + (th + 1) * TT]
                                  if False else
                                  tap["bc"][:, 2 * TT + th * TT:2 * TT + (th + 1) * TT],
                                  in_=ar2[:, TT:2 * TT]), QG, 16)
                p.wait(g, QG)
            if "hsd" in tap:
                p.wait(g, EA)
                for j in range(16):
                    p.inc(g.dma_start(out=tap["hsd"][j * 128:(j + 1) * 128,
                                                     th * TT:(th + 1) * TT],
                                      in_=cb[:, j * TT:(j + 1) * TT]), QG, 16)
                p.wait(g, QG)
            for j in range(16):
                lhs = ar2[0:32, 2048 + 2 * j * 64:2048 + 2 * j * 64 + 128]
                for tc in range(2):
                    p.wait(te, EV)
                    p.wait(te, EA)
                    p.inc(te.matmul(ps_m[:, 0:512], lhs,
                                    d64b[0:32, th * TT + tc * 512:
                                         th * TT + (tc + 1) * 512],
                                    start=True, stop=True), ET, 1)
                    p.wait(v, ET)
                    p.inc(v.tensor_tensor(
                        ub[:, j * TT + tc * 512:j * TT + (tc + 1) * 512],
                        cb[:, j * TT + tc * 512:j * TT + (tc + 1) * 512],
                        ps_m[:, 0:512], ALU.mult), EV, 1)
            for cl in range(4):
                c = th * 4 + cl
                for j in range(16):
                    for sch in range(2):
                        p.wait(te, EV)
                        p.wait(te, EA)
                        p.inc(te.transpose(
                            ps_tb[:, 0:128],
                            ub[:, j * TT + cl * LCH + sch * 128:
                               j * TT + cl * LCH + sch * 128 + 128],
                            identB[:]), ET, 1)
                        p.wait(sc, ET)
                        p.inc(sc.copy(ar[:, sch * 2048 + j * 128:
                                         sch * 2048 + (j + 1) * 128],
                                      ps_tb[:, 0:128]), EA, 1)
                acum = wk2[0:32, 128:384]
                ones_d0 = ones_colF[0:32, 0:1].broadcast_to([32, LCH])
                p.wait(v, EA)
                p.wait(v, ET)
                p.inc(v.tensor_tensor_scan(acum, ones_d0,
                                           adt[0:32, th * TT + cl * LCH:
                                               th * TT + (cl + 1) * LCH],
                                           0.0, ALU.mult, ALU.add), EV, 1)
                p.wait(g, EV)
                p.inc(g.dma_start(out=acum_dram[c, :, :], in_=acum), QG, 16)
                p.wait(g, QG)
                for sch in range(2):
                    p.wait(te, EV)
                    p.wait(te, EA)
                    p.inc(te.transpose(ps_t[:, 0:32],
                                       acum[:, sch * 128:(sch + 1) * 128],
                                       identF[0:32, 0:32]), ET, 1)
                    p.wait(v, ET)
                    p.inc(v.tensor_copy(wk2[:, 384 + sch * 32:416 + sch * 32],
                                        ps_t[:, 0:32]), EV, 1)
                p.wait(sc, EV)
                p.inc(sc.activation(out=zb[0:32, 512:768], in_=acum, func=AF.Exp),
                      EA, 1)
                p.wait(te, EV)
                p.wait(te, EA)
                p.inc(te.matmul(ps_sd[0:1, 0:32], wk2[0:32, 383:384],
                                identF[0:32, 0:32], start=True, stop=True), ET, 1)
                p.wait(sc, ET)
                p.inc(sc.copy(wk2[0:1, 448:480], ps_sd[0:1, 0:32]), EA, 1)
                p.wait(te, EA)
                p.inc(te.matmul(ps_sd[:, 64:96], ones_rowF[:], wk2[0:1, 448:480],
                                start=True, stop=True), ET, 1)
                for sch in range(2):
                    p.wait(v, ET)
                    p.inc(v.tensor_tensor(wk2[:, 512 + sch * 32:544 + sch * 32],
                                          ps_sd[:, 64:96],
                                          wk2[:, 384 + sch * 32:416 + sch * 32],
                                          ALU.subtract), EV, 1)
                    p.wait(sc, EV)
                    p.inc(sc.activation(out=zb[:, 1920 + sch * 32:1952 + sch * 32],
                                        in_=wk2[:, 512 + sch * 32:544 + sch * 32],
                                        func=AF.Exp), EA, 1)
                p.wait(sc, ET)
                p.inc(sc.activation(out=zb[:, 1984:2016], in_=ps_sd[:, 64:96],
                                    func=AF.Exp), EA, 1)
                if "dbg4" in tap and c == 0:
                    p.wait(g, EA)
                    p.inc(g.dma_start(out=tap["dbg4"][0:128, 0:64], in_=wk2[:, 384:448]), QG, 16)
                    p.inc(g.dma_start(out=tap["dbg4"][0:128, 64:128], in_=wk2[:, 512:576]), QG, 16)
                    p.inc(g.dma_start(out=tap["dbg4"][0:128, 128:224], in_=zb[:, 1920:2016]), QG, 16)
                    p.wait(g, QG)
                for grp in range(8):
                    bb_flush()
                    h4 = slice(grp * 4, grp * 4 + 4)
                    for sch in range(2):
                        src = acum_dram[c, grp * 4:grp * 4 + 4, :].rearrange(
                            "h l -> (h l)")[None, :].partition_broadcast(128)
                        p.wait(g, EV)
                        p.wait(g, EP)
                        p.wait(g, EA)
                        p.inc(g.dma_start(out=wk[:, 0:1024], in_=src), QG, 16)
                        acT = wk2[:, 384 + sch * 32:416 + sch * 32]
                        in1 = acT[:, h4].unsqueeze(2).broadcast_to([128, 4, LCH])
                        p.wait(v, QG)
                        p.wait(v, EA)
                        p.inc(v.tensor_tensor(
                            wk[:, 1024:2048].rearrange("q (h l) -> q h l", h=4),
                            wk[:, 0:1024].rearrange("q (h l) -> q h l", h=4),
                            in1, ALU.subtract), EV, 1)
                        p.wait(g, EV)
                        mskap = zb[:, 1408 + sch * 256:1664 + sch * 256] \
                            .unsqueeze(1).broadcast_to([128, 4, LCH])
                        p.inc(g.tensor_tensor(
                            wk[:, 1024:2048].rearrange("q (h l) -> q h l", h=4),
                            wk[:, 1024:2048].rearrange("q (h l) -> q h l", h=4),
                            mskap, ALU.add), EP, 1)
                        p.wait(sc, EP)
                        p.inc(sc.activation(out=wk[:, 0:1024],
                                            in_=wk[:, 1024:2048], func=AF.Exp),
                              EA, 1)
                        if "dbg" in tap and c == 2 and grp == 1:
                            p.wait(g, EA)
                            p.inc(g.dma_start(out=tap["dbg"][sch * 128:sch * 128 + 128, :],
                                              in_=wk[:, 1024:2048]), QG, 16)
                            p.inc(g.dma_start(out=tap["dbg"][256 + sch * 128:
                                                             384 + sch * 128, :],
                                              in_=wk[:, 0:1024]), QG, 16)
                            p.wait(g, QG)
                        p.wait(te, EV)
                        p.wait(te, EA)
                        for hh in range(4):
                            p.inc(te.matmul(
                                ps_gm[:, hh * 256:(hh + 1) * 256],
                                ar2[:, cl * LCH + sch * 128:
                                    cl * LCH + sch * 128 + 128],
                                ar2[:, TT + cl * LCH:TT + (cl + 1) * LCH],
                                start=True, stop=True), ET, 1)
                        p.wait(v, ET)
                        p.wait(v, EA)
                        p.inc(v.tensor_tensor(zb2[:, sch * 1024:(sch + 1) * 1024],
                                              ps_gm[:, 0:1024], wk[:, 0:1024],
                                              ALU.mult), EV, 1)

                    if "dbg2" in tap and c == 1 and grp == 0:
                        p.wait(g, EV)
                        p.inc(g.dma_start(out=tap["dbg2"][0:128, 0:1024], in_=zb2[:, 0:1024]), QG, 16)
                        p.inc(g.dma_start(out=tap["dbg2"][128:256, 0:1024], in_=zb2[:, 1024:2048]), QG, 16)
                        p.inc(g.dma_start(out=tap["dbg2"][256:384, 0:2048], in_=Ssb), QG, 16)
                        p.inc(g.dma_start(out=tap["dbg2"][384:416, 0:256], in_=zb[0:32, 512:768]), QG, 16)
                        p.wait(g, QG)
                    for hh in range(4):
                        hloc = grp * 4 + hh
                        lhs1 = ident64[0:32, hloc:hloc + 1].broadcast_to([32, 128])
                        p.wait(te, EV)
                        p.wait(te, EA)
                        p.inc(te.matmul(ps_sd[:, 0:256], lhs1, zb[0:32, 512:768],
                                        start=True, stop=True), ET, 1)
                        csd = zb[:, (hh % 2) * 256:(hh % 2) * 256 + 256]
                        p.wait(v, ET)
                        p.inc(v.tensor_tensor(csd, ar2[:, TT + cl * LCH:
                                                       TT + (cl + 1) * LCH],
                                              ps_sd[:, 0:256], ALU.mult), EV, 1)
                        pcol = (hh // 2) * 256
                        prow = 64 * (hh % 2)
                        tp = (0, 64) if hh % 2 else None
                        p.wait(te, EV)
                        for sch in range(2):
                            p.inc(te.matmul(
                                ps_y2[prow:prow + 64, pcol:pcol + 256],
                                ar[:, sch * 2048 + hloc * 64:
                                   sch * 2048 + hloc * 64 + 64],
                                zb2[:, sch * 1024 + hh * 256:
                                    sch * 1024 + (hh + 1) * 256],
                                start=(sch == 0), stop=False,
                                tile_position=tp), ET, 1)
                        p.inc(te.matmul(
                            ps_y2[prow:prow + 64, pcol:pcol + 256],
                            Ssb[:, hloc * 64:(hloc + 1) * 64], csd,
                            start=False, stop=True, tile_position=tp), ET, 1)
                        if hh % 2 == 1:
                            pr = grp * 2 + hh // 2
                            p.wait(v, ET)
                            p.wait(v, QG)
                            p.inc(v.scalar_tensor_tensor(
                                zb[:, 1120:1376],
                                cb[:, pr * TT + cl * LCH:pr * TT + (cl + 1) * LCH],
                                Dbr[:, pr:pr + 1],
                                ps_y2[:, pcol:pcol + 256], ALU.mult, ALU.add),
                                EV, 1)
                            p.wait(g, EV)
                            p.inc(g.dma_start(
                                out=yb_loc2[pr * 128:(pr + 1) * 128,
                                            th * TT + cl * LCH:
                                            th * TT + (cl + 1) * LCH],
                                in_=zb[:, 1120:1376]), QG, 16)
                for sch in range(2):
                    p.wait(te, EV)
                    p.wait(te, EA)
                    p.inc(te.transpose(ps_tb[:, 0:128],
                                       ar2[:, cl * LCH + sch * 128:
                                           cl * LCH + sch * 128 + 128],
                                       identB[:]), ET, 1)
                    p.wait(sc, ET)
                    p.inc(sc.copy(BT0 if sch == 0 else BT1, ps_tb[:, 0:128]), EA, 1)
                    dec = zb[:, 1920 + sch * 32:1952 + sch * 32]
                    in1b = dec.unsqueeze(2).broadcast_to([128, 32, 64])
                    p.wait(v, EA)
                    p.inc(v.tensor_tensor(
                        ar[:, 4096 + sch * 2048:4096 + (sch + 1) * 2048]
                        .rearrange("q (h pp) -> q h pp", h=32),
                        ar[:, sch * 2048:(sch + 1) * 2048]
                        .rearrange("q (h pp) -> q h pp", h=32),
                        in1b, ALU.mult), EV, 1)
                if "dbg3" in tap and c == 0:
                    p.wait(g, EV)
                    p.wait(g, EA)
                    p.inc(g.dma_start(out=tap["dbg3"][0:128, 0:32], in_=zb[:, 1920:1952]), QG, 16)
                    p.inc(g.dma_start(out=tap["dbg3"][0:128, 32:64], in_=zb[:, 1952:1984]), QG, 16)
                    p.inc(g.dma_start(out=tap["dbg3"][0:128, 64:96], in_=zb[:, 1984:2016]), QG, 16)
                    p.inc(g.dma_start(out=tap["dbg3"][128:256, 0:2048], in_=ar[:, 4096:6144]), QG, 16)
                    p.inc(g.dma_start(out=tap["dbg3"][256:384, 0:2048], in_=ar[:, 6144:8192]), QG, 16)
                    p.inc(g.dma_start(out=tap["dbg3"][384:512, 0:128], in_=BT0), QG, 16)
                    p.inc(g.dma_start(out=tap["dbg3"][384:512, 128:256], in_=BT1), QG, 16)
                    p.wait(g, QG)
                drep = zb[:, 1984:2016]
                for sl4 in range(4):
                    scol = slice(sl4 * 512, (sl4 + 1) * 512)
                    p.wait(te, EV)
                    p.wait(te, EA)
                    p.inc(te.matmul(ps_x[:, 0:512], BT0,
                                    ar[:, 4096:6144][:, scol],
                                    start=True, stop=False), ET, 1)
                    p.inc(te.matmul(ps_x[:, 0:512], BT1,
                                    ar[:, 6144:8192][:, scol],
                                    start=False, stop=True), ET, 1)
                    dr1 = drep[:, sl4 * 8:(sl4 + 1) * 8].unsqueeze(2) \
                        .broadcast_to([128, 8, 64])
                    p.wait(v, ET)
                    p.inc(v.tensor_tensor(
                        zb2[:, 0:512].rearrange("q (h pp) -> q h pp", h=8),
                        Ssb[:, scol].rearrange("q (h pp) -> q h pp", h=8),
                        dr1, ALU.mult), EV, 1)
                    p.wait(v, EV)
                    p.inc(v.tensor_tensor(Ssb[:, scol], zb2[:, 0:512],
                                          ps_x[:, 0:512], ALU.add), EV, 1)

    if "adt" in tap:
        p.wait(g, EV)
        p.inc(g.dma_start(out=tap["adt"][0:32, :], in_=adt[0:32, :]), QG, 16)
    if "acum" in tap:
        p.wait(g, QG)
        p.inc(g.dma_start(out=tap["acum"],
                          in_=acum_dram.rearrange("c h l -> (c h) l")), QG, 16)
    if "ybl" in tap:
        p.wait(g, QG)
        for j in range(16):
            p.inc(g.dma_start(out=zb[:, 0:2048],
                              in_=yb_loc2[j * 128:(j + 1) * 128, :]), QG, 16)
            p.wait(g, QG)
            p.inc(g.dma_start(out=tap["ybl"][j * 128:(j + 1) * 128, :],
                              in_=zb[:, 0:2048]), QG, 16)
            p.wait(g, QG)
    if "sg3" in tap:
        p.wait(g, QG)
        for j in range(16):
            p.inc(g.dma_start(out=zb[:, 0:TT], in_=gate_dram[j * 128:(j + 1) * 128, :]), QG, 16)
            p.wait(g, QG)
            p.inc(g.dma_start(out=tap["sg3"][j * 128:(j + 1) * 128, :], in_=zb[:, 0:TT]), QG, 16)
            p.wait(g, QG)
    p.wait(g, QG)
    p.inc(g.dma_start(out=sh_yb[bass.ds(me_v, 1), :, :], in_=yb_loc2), QG, 16)
    p.wait(g, QG)
    flag_send()
    flag_wait()

    p.inc(g.dma_start(out=ytok_dram, in_=sh_yb[:, :, bass.ds(me_v * TT, TT)]),
          QG, 16)
    p.wait(sy, QG)
    p.wait(sy, EV)
    for sl_i in range(2):
        dstar = ub if sl_i == 0 else ar
        src = ytok_dram[sl_i, :, :].rearrange("(i q) t -> q i t", q=128)
        p.inc(sy.dma_start(out=dstar[:, 0:16 * TT].rearrange(
            "q (i t) -> q i t", i=16), in_=src), QS, 16)
    p.wait(g, QS)

    if "yraw" in tap:
        p.wait(g, QS)
        p.wait(g, QG)
        for j in range(32):
            p.inc(g.dma_start(out=tap["yraw"][j * 128:(j + 1) * 128, :],
                              in_=ytile(j) if False else
                              (ub if j < 16 else ar)[:, (j % 16) * TT:(j % 16 + 1) * TT]),
                  QG, 16)
        p.wait(g, QG)
    if "argate" in tap:
        p.wait(g, QG)
        for j in range(2):
            p.inc(g.dma_start(out=tap["argate"][j * 128:(j + 1) * 128, :],
                              in_=ar[:, j * TT:(j + 1) * TT]), QG, 16)
            p.inc(g.dma_start(out=tap["argate"][256 + j * 128:256 + (j + 1) * 128, :],
                              in_=ub[:, j * TT:(j + 1) * TT]), QG, 16)
        p.wait(g, QG)
    if "ytk" in tap:
        p.wait(g, QG)
        for sl_i in range(2):
            for j in range(2):
                p.inc(g.dma_start(out=zb[:, 0:TT],
                                  in_=ytok_dram[sl_i, j * 128:(j + 1) * 128, :]), QG, 16)
                p.wait(g, QG)
                p.inc(g.dma_start(out=tap["ytk"][(sl_i * 2 + j) * 128:
                                                 (sl_i * 2 + j + 1) * 128, :],
                                  in_=zb[:, 0:TT]), QG, 16)
                p.wait(g, QG)

    def ytile(i):
        return (ub if i < 16 else ar)[:, (i % 16) * TT:(i % 16 + 1) * TT]

    with nc.psum_tensor("ps_r", [128, 2048], F32) as ps_r:
        for i in range(32):
            p.wait(g, EV)
            p.inc(g.dma_start(out=zb[:, 0:TT],
                              in_=gate_dram[i * 128:(i + 1) * 128, :]), QG, 16)
            p.wait(v, QG)
            p.inc(v.tensor_tensor(ytile(i), ytile(i), zb[:, 0:TT], ALU.mult),
                  EV, 1)
            p.wait(sc, EV)
            p.wait(sc, ET)
            p.inc(sc.activation(out=zb2[:, 0:TT], in_=ytile(i), func=AF.Square),
                  EA, 1)
            p.wait(te, EA)
            for tch in range(2):
                p.inc(te.matmul(ps_r[0:1, tch * 512:(tch + 1) * 512], ones_col[:],
                                zb2[:, tch * 512:(tch + 1) * 512],
                                start=(i == 0), stop=(i == 31)), ET, 1)
        p.wait(v, ET)
        p.inc(v.tensor_scalar_mul(wk[0:1, 0:TT], ps_r[0:1, 0:TT], 1.0 / INTER),
              EV, 1)
        p.wait(v, EV)
        p.inc(v.tensor_scalar_add(wk[0:1, 0:TT], wk[0:1, 0:TT], EPS), EV, 1)
        p.wait(v, EV)
        p.inc(v.reciprocal(wk[0:1, TT:2 * TT], wk[0:1, 0:TT]), EV, 1)
        p.wait(sc, EV)
        p.inc(sc.activation(out=wk[0:1, 0:TT], in_=wk[0:1, TT:2 * TT],
                            func=AF.Sqrt), EA, 1)
        p.wait(v, EA)
        p.inc(v.tensor_copy(zb[0:1, 0:TT], wk[0:1, 0:TT]), EV, 1)
        p.wait(te, EV)
        for tch in range(2):
            p.inc(te.matmul(ps_r[:, 1024 + tch * 512:1024 + (tch + 1) * 512],
                            ones_row[:], zb[0:1, tch * 512:(tch + 1) * 512],
                            start=True, stop=True), ET, 1)
        p.wait(v, ET)
        p.inc(v.tensor_copy(rsb, ps_r[:, 1024:2048]), EV, 1)
        for i in range(32):
            p.wait(v, EV)
            p.inc(v.scalar_tensor_tensor(ytile(i), ytile(i), nwr[:, i:i + 1],
                                         rsb, ALU.mult, ALU.mult), EV, 1)

    if "yn" in tap:
        p.wait(g, EV)
        for j in range(4):
            p.inc(g.dma_start(out=tap["yn"][j * 128:(j + 1) * 128, :],
                              in_=ub[:, j * TT:(j + 1) * TT]), QG, 16)
        p.wait(g, QG)
    if "sg" in tap:
        for j in range(16):
            p.inc(g.dma_start(out=zb[:, 0:TT], in_=gate_dram[j * 128:(j + 1) * 128, :]), QG, 16)
            p.wait(g, QG)
            p.inc(g.dma_start(out=tap["sg"][j * 128:(j + 1) * 128, :], in_=zb[:, 0:TT]), QG, 16)
            p.wait(g, QG)
    with (nc.psum_tensor("ps_o0", [128, 1024], F32) as o0,
          nc.psum_tensor("ps_o1", [128, 1024], F32) as o1,
          nc.psum_tensor("ps_o2", [128, 1024], F32) as o2,
          nc.psum_tensor("ps_o3", [128, 1024], F32) as o3):
        po4 = [o0, o1, o2, o3]
        for m2b in range(4):
            mc = slice(m2b * 512, (m2b + 1) * 512)
            for k2 in range(32):
                wbuf = 2048 + (k2 % 2) * 512
                p.wait(g, ET)
                p.inc(g.dma_start(out=cb[:, wbuf:wbuf + 512],
                                  in_=inp["outT"][k2 * 128:(k2 + 1) * 128, mc]),
                      QG, 16)
                p.wait(te, QG)
                if k2 == 0:
                    p.wait(te, EV)
                for m2 in range(4):
                    lhs = cb[:, wbuf + m2 * 128:wbuf + (m2 + 1) * 128]
                    for tch in range(2):
                        p.inc(te.matmul(
                            po4[m2][:, tch * 512:(tch + 1) * 512], lhs,
                            ytile(k2)[:, tch * 512:(tch + 1) * 512],
                            start=(k2 == 0), stop=(k2 == 31)), ET, 1)
            for m2 in range(4):
                d_i = m2b * 4 + m2
                p.wait(v, ET)
                p.inc(v.tensor_tensor(xr[:, d_i * TT:(d_i + 1) * TT],
                                      po4[m2][:, 0:TT],
                                      xr[:, d_i * TT:(d_i + 1) * TT], ALU.add),
                      EV, 1)

    if "ynpost" in tap:
        p.wait(g, ET)
        p.wait(g, EV)
        for j in range(8):
            p.inc(g.dma_start(out=tap["ynpost"][j * 128:(j + 1) * 128, :],
                              in_=ytile(j * 4)), QG, 16)
        p.wait(g, QG)
    if "x3" in tap:
        p.wait(sy, EV)
        for i in range(DT):
            p.inc(sy.dma_start(out=tap["x3"][i * 128:(i + 1) * 128, :],
                               in_=xr[:, i * TT:(i + 1) * TT]), QS, 16)
    if upto == "x3":
        finish()
        return nc

    ffn_block(first=False)

    finish()
    return nc


def _stage(inputs, core):
    b, half = core // 2, core % 2
    t0 = half * TT
    _ = inputs.get("L0_attn_b_mask")
    x = inputs["L0_attn_a_input"]
    d = {}
    d["xin"] = np.ascontiguousarray(np.asarray(x[b, t0:t0 + TT, :]).T)
    nonce = int(np.random.randint(1 << 20, 1 << 30))
    _stage.nonce = getattr(_stage, "nonce", nonce)
    d["meta"] = np.array([[half, 1 - half, _stage.nonce, 0, 0, 0, 0, 0]], np.int32)
    A = np.asarray
    d["wprj_cat"] = np.ascontiguousarray(np.concatenate(
        [A(inputs["L0_attn_a_W_dt_in"]).T,
         A(inputs["L0_attn_a_W_B"]).T,
         np.zeros((D, 16), np.float32),
         A(inputs["L0_attn_a_W_C"]).T], axis=1))
    d["wdt_outT_own"] = np.ascontiguousarray(
        A(inputs["L0_attn_a_W_dt_out"]).T[:, half * 1024:(half + 1) * 1024])
    d["bdt_own"] = np.ascontiguousarray(
        A(inputs["L0_attn_a_b_dt_out"]).reshape(16, 128).T[:, half * 8:half * 8 + 8])
    d["Da_own"] = np.ascontiguousarray(
        A(inputs["L0_attn_a_D"]).reshape(16, 128).T[:, half * 8:half * 8 + 8])
    d["alog_own"] = np.ascontiguousarray(
        A(inputs["L0_attn_a_A_log"]).reshape(16, 128, 16).transpose(1, 0, 2)
        .reshape(128, 256)[:, half * 128:(half + 1) * 128])
    d["inpT"] = np.ascontiguousarray(A(inputs["L0_attn_b_in_proj_w"]).T)
    cw = A(inputs["L0_attn_b_conv_w"]).reshape(34, 128, 4)
    cbv = A(inputs["L0_attn_b_conv_b"]).reshape(34, 128)
    own_cht = list(range(half * 16, half * 16 + 16)) + [32, 33]
    d["convw_own"] = np.ascontiguousarray(
        cw[own_cht].transpose(1, 0, 2).reshape(128, 72))
    d["convb_own"] = np.ascontiguousarray(cbv[own_cht].T)
    d["alogb_own"] = np.ascontiguousarray(
        A(inputs["L0_attn_b_A_log"]).reshape(NH, 1)[half * 32:half * 32 + 32])
    d["dtbias_own"] = np.ascontiguousarray(
        A(inputs["L0_attn_b_dt_bias"]).reshape(NH, 1)[half * 32:half * 32 + 32])
    d["Db_own"] = np.ascontiguousarray(
        np.repeat(A(inputs["L0_attn_b_D"]), PDIM).reshape(32, 128).T
        [:, half * 16:half * 16 + 16])
    d["normw_r"] = np.ascontiguousarray(
        A(inputs["L0_attn_b_norm_w"]).reshape(32, 128).T)
    d["outT"] = np.ascontiguousarray(A(inputs["L0_attn_b_out_proj_w"]).T)
    fcT = A(inputs["L0_ffn_w_fc"]).T
    pjT = A(inputs["L0_ffn_w_proj"]).T
    d["fcT_half"] = np.ascontiguousarray(fcT[half * (D // 2):(half + 1) * (D // 2)])
    d["projT_half"] = np.ascontiguousarray(
        pjT[half * (FFH // 2):(half + 1) * (FFH // 2)])
    return {k: np.asarray(v2) for k, v2 in d.items()}


def kernel(**inputs):
    _stage.nonce = int(np.random.randint(1 << 20, 1 << 30))
    nc = build_program()
    in_maps = [_stage(inputs, c) for c in range(8)]
    res = run_bass_kernel_spmd(nc, in_maps, core_ids=list(range(8)))
    out = np.zeros((B, S, D), np.float32)
    for c in range(8):
        b, half = c // 2, c % 2
        out[b, half * TT:(half + 1) * TT, :] = res.results[c]["xout"].T
    return out


# version 07
# speedup vs baseline: 1.2332x; 1.2332x over previous
import numpy as np
import concourse.bass as bass
import concourse.mybir as mybir
from concourse.bass_utils import run_bass_kernel_spmd

F32 = mybir.dt.float32
BF16 = mybir.dt.bfloat16
I32 = mybir.dt.int32
AF = mybir.ActivationFunctionType
ALU = mybir.AluOpType
AX = mybir.AxisListType

B, S, D = 4, 2048, 2048
TT = 1024
DT = 16
NA = 16
FFH = 5120
INTER = 4096
NH = 64
PDIM = 64
NB = 128
LCH = 256
NCH = 8
EPS = 1e-5
STAGES = ("prep", "attna", "ffn1", "attnb", "ffn2")


class P:

    def __init__(self, nc):
        self.nc = nc
        self.cnt = {}
        self.sems = {}

    def sem(self, name):
        if name not in self.sems:
            self.sems[name] = self.nc.semaphore(name).__enter__()
            self.cnt[name] = 0
        return self.sems[name]

    def inc(self, instr, name, amt):
        instr.then_inc(self.sem(name), amt)
        self.cnt[name] += amt

    def wait(self, eng, name):
        eng.wait_ge(self.sem(name), self.cnt[name])

    def wait_lag(self, eng, name, lag):
        eng.wait_ge(self.sem(name), max(0, self.cnt[name] - lag))


def build_program(upto="ffn2", taps=()):
    nc = bass.Bass("TRN2", target_bir_lowering=False)
    p = P(nc)
    g, v, sc, te, sy = nc.gpsimd, nc.vector, nc.scalar, nc.tensor, nc.sync

    inp = {}

    def din(name, shape, dt=F32):
        inp[name] = nc.dram_tensor(name, shape, dt, kind="ExternalInput").ap()
        return inp[name]

    din("xin", [D, TT])
    din("meta", [1, 8], I32)
    din("wprj_cat", [D, 112])
    din("wdt_outT_own", [64, 1024])
    din("bdt_own", [128, 8])
    din("Da_own", [128, 8])
    din("alog_own", [128, 128])
    din("inpT", [D, 8512])
    din("convw_own", [128, 18 * 4])
    din("convb_own", [128, 18])
    din("alogb_own", [32, 1])
    din("dtbias_own", [32, 1])
    din("Db_own", [128, 16])
    din("normw_r", [128, 32])
    din("outT", [INTER, D])
    din("fcT_half", [D // 2, FFH])
    din("projT_half", [FFH // 2, D])

    xout = nc.dram_tensor("xout", [D, TT], F32, kind="ExternalOutput").ap()
    tap = {}
    for tname, tshape in taps:
        tap[tname] = nc.dram_tensor("tap_" + tname, tshape, F32,
                                    kind="ExternalOutput").ap()

    shd = lambda n, s, dt: nc.dram_tensor(n, s, dt, kind="Internal",
                                          addr_space="Shared").ap()
    sh_q2fc = shd("sh_q2fc", [2, D // 2, FFH], BF16)
    sh_q2pj = shd("sh_q2pj", [2, FFH // 2, D], BF16)
    sh_u = shd("sh_u", [2, D, TT], BF16)
    sh_ya = shd("sh_ya", [2, 1024, S], BF16)
    sh_hbc = shd("sh_hbc", [2, 4416, TT], BF16)
    sh_yb = shd("sh_yb", [2, 2048, S], BF16)
    sh_flag = shd("sh_flag", [2, 16], I32)

    gate_dram = nc.dram_tensor("gate_dram", [INTER, TT], BF16, kind="Internal").ap()
    q2fc_loc = nc.dram_tensor("q2fc_loc", [D // 2, FFH], BF16, kind="Internal").ap()
    q2pj_loc = nc.dram_tensor("q2pj_loc", [FFH // 2, D], BF16, kind="Internal").ap()
    hbc_loc = nc.dram_tensor("hbc_loc", [4416, TT], BF16, kind="Internal").ap()
    yb_loc2 = nc.dram_tensor("yb_loc2", [2048, S], BF16, kind="Internal").ap()
    zconv_dram = nc.dram_tensor("zconv_dram", [2, 2048, TT], BF16, kind="Internal").ap()
    ytok_dram = nc.dram_tensor("ytok_dram", [2, 2048, TT], BF16, kind="Internal").ap()
    acum_dram = nc.dram_tensor("acum_dram", [NCH, 32, LCH], F32, kind="Internal").ap()
    h_dram = nc.dram_tensor("h_dram", [FFH, TT], BF16, kind="Internal").ap()

    sbufs = []

    def sbuf(name, shape, dt):
        t = nc.sbuf_tensor(name, shape, dt).__enter__()
        sbufs.append(t)
        return t

    mp = sbuf("mp", [1, 8], I32)
    identB = sbuf("identB", [128, 128], BF16)
    ident64 = sbuf("ident64", [64, 64], BF16)
    ones_col = sbuf("ones_col", [128, 1], BF16)
    ones_colF = sbuf("ones_colF", [128, 1], F32)
    ones_row = sbuf("ones_row", [1, 128], BF16)
    ones_rowF = sbuf("ones_rowF", [1, 128], F32)
    ph_sb = sbuf("ph_sb", [1, 16], I32)
    pf_sb = sbuf("pf_sb", [1, 16], I32)
    A_a = sbuf("A_a", [128, 128], F32)
    convw = sbuf("convw", [128, 72], F32)
    convb = sbuf("convb", [128, 18], F32)
    bdt = sbuf("bdt", [128, 8], F32)
    Da = sbuf("Da", [128, 8], F32)
    d64b = sbuf("d64b", [64, 2048], BF16)
    brawb = sbuf("brawb", [16, 2048], BF16)
    crawb = sbuf("crawb", [16, 2048], BF16)
    wpj = sbuf("wpj", [128, 1792], BF16)
    Ab_col = sbuf("Ab_col", [32, 1], F32)
    dtb_col = sbuf("dtb_col", [32, 1], F32)
    Dbr = sbuf("Dbr", [128, 16], F32)
    nwr = sbuf("nwr", [128, 32], F32)
    srow = sbuf("srow", [1, 64], F32)
    thr_c = sbuf("thr_c", [128, 2], F32)
    sc_ffn = sbuf("sc_ffn", [128, 1], F32)

    xr = sbuf("xr", [128, DT * TT], F32)
    ub = sbuf("ub", [128, DT * TT], BF16)
    cb = sbuf("cb", [128, 16384], BF16)
    ar = sbuf("ar", [128, 16384], BF16)
    ar2 = sbuf("ar2", [128, 4096], BF16)
    wouT = ar2[0:64, 3072:4096]
    wk = sbuf("wk", [128, 2048], F32)
    wk2 = sbuf("wk2", [128, 1024], F32)
    zb = sbuf("zb", [128, 2048], BF16)
    zb2 = sbuf("zb2", [128, 2048], BF16)
    rows = wk[0:1, :]
    mub = cb[:, 0:TT]
    rsb = cb[:, TT:2 * TT]
    st_a = wk2[:, 0:128]

    QS, QG, EV, EP, EA, ET, PO = "qs", "qg", "ev", "ep", "ea", "et", "po"

    p.inc(sy.dma_start(out=mp[:], in_=inp["meta"]), QS, 16)
    p.inc(sy.dma_start(out=bdt[:], in_=inp["bdt_own"]), QS, 16)
    p.inc(sy.dma_start(out=Da[:], in_=inp["Da_own"]), QS, 16)
    p.inc(sy.dma_start(out=wk[:, 0:128], in_=inp["alog_own"]), QS, 16)
    p.inc(sy.dma_start(out=convw[:], in_=inp["convw_own"]), QS, 16)
    p.inc(sy.dma_start(out=convb[:], in_=inp["convb_own"]), QS, 16)
    p.inc(sy.dma_start(out=wk[0:32, 256:257], in_=inp["alogb_own"]), QS, 16)
    p.inc(sy.dma_start(out=dtb_col[:], in_=inp["dtbias_own"]), QS, 16)
    p.inc(sy.dma_start(out=Dbr[:], in_=inp["Db_own"]), QS, 16)
    p.inc(sy.dma_start(out=nwr[:], in_=inp["normw_r"]), QS, 16)

    p.inc(g.memset(identB[:], 1.0), EP, 1)
    p.inc(g.memset(ident64[:], 1.0), EP, 1)
    p.wait(g, EP)
    p.inc(g.affine_select(identB[:], identB[:], [[1, 128]], ALU.is_equal, 0.0,
                          base=0, channel_multiplier=-1), EP, 1)
    p.inc(g.affine_select(ident64[:], ident64[:], [[1, 64]], ALU.is_equal,
                          0.0, base=0, channel_multiplier=-1), EP, 1)
    p.inc(v.memset(ph_sb[:].bitcast(F32), 0.0), EV, 1)
    p.inc(v.memset(ones_col[:], 1.0), EV, 1)
    p.inc(v.memset(ones_colF[:], 1.0), EV, 1)
    p.inc(v.memset(ones_row[:], 1.0), EV, 1)
    p.inc(v.memset(ones_rowF[:], 1.0), EV, 1)

    p.wait(sc, QS)
    p.inc(sc.activation(out=A_a[:], in_=wk[:, 0:128], func=AF.Exp), EA, 1)
    p.inc(sc.activation(out=wk2[0:32, 0:1], in_=wk[0:32, 256:257], func=AF.Exp),
          EA, 1)
    p.wait(v, EA)
    p.inc(v.tensor_scalar_mul(A_a[:], A_a[:], -1.0), EV, 1)
    p.inc(v.tensor_scalar_mul(Ab_col[:], wk2[0:32, 0:1], -1.0), EV, 1)

    p.wait(g, QS)
    r_me = g.alloc_register("r_me")
    r_peer = g.alloc_register("r_peer")
    r_phase = g.alloc_register("r_phase")
    r_want = g.alloc_register("r_want")
    r_poll = g.alloc_register("r_poll")
    r_fl = g.alloc_register("r_fl")
    g.reg_load(r_me, mp[0:1, 0:1])
    g.reg_load(r_peer, mp[0:1, 1:2])
    g.reg_load(r_phase, mp[0:1, 2:3])
    g.reg_load(r_want, mp[0:1, 2:3])
    g.reg_mov(r_poll, 16)
    me_v = g.snap(r_me, min_val=0, max_val=1)
    peer_v = g.snap(r_peer, min_val=0, max_val=1)

    spin_id = [0]
    n_sends = [0]
    flush_id = [0]

    def allwait():
        for e in (v, sc, te, g):
            for snm in (EV, EA, ET, EP, QG):
                p.wait(e, snm)

    def bb_flush():
        flush_id[0] += 1
        nm = f"flush{flush_id[0]}"
        g.br(nm)
        nc.switch_bb(nm)

    def flag_send():
        n_sends[0] += 1
        g.reg_add(r_phase, r_phase, 1)
        g.reg_save(ph_sb[0:1, 0:1], r_phase)
        p.inc(g.dma_start(out=sh_flag[bass.ds(me_v, 1), :], in_=ph_sb[:]), QG, 16)
        return n_sends[0]

    def flag_wait(expect=None):
        k = n_sends[0] if expect is None else expect
        g.reg_load(r_want, mp[0:1, 2:3])
        g.reg_add(r_want, r_want, k)
        sid = spin_id[0]
        spin_id[0] += 1
        lname, cname, ename = f"spin{sid}", f"spin{sid}_c", f"spin{sid}_end"
        g.br(lname)
        with nc.body(lname, valid_engines=None):
            g.dma_start(out=pf_sb[:], in_=sh_flag[bass.ds(peer_v, 1), :]).then_inc(
                p.sem(PO), 16)
            g.wait_ge(p.sem(PO), r_poll)
            g.reg_add(r_poll, r_poll, 16)
            g.reg_load(r_fl, pf_sb[0:1, 0:1])
            g.br_lt(r_fl, r_want, on_true=lname, on_false=cname)
        with nc.body(cname, valid_engines=None):
            g.reg_add(r_fl, r_fl, -99)
            g.br_lt(r_fl, r_want, on_true=ename, on_false=lname)
        nc.switch_bb(ename)

    def finish():
        p.wait(sy, EV)
        p.wait(sy, EA)
        p.wait(sy, EP)
        p.wait(sy, ET)
        for i in range(DT):
            p.inc(sy.dma_start(out=xout[i * 128:(i + 1) * 128, :],
                               in_=xr[:, i * TT:(i + 1) * TT]), QS, 16)
        p.wait(sy, QS)
        p.wait(g, QG)

    def tap_out(name, src_f32_fn, ntiles, tcols):
        if name not in tap:
            return
        for i in range(ntiles):
            p.wait(sy, EV)
            p.wait(sy, EA)
            p.wait(sy, ET)
            src = src_f32_fn(i)
            p.inc(sy.dma_start(out=tap[name][i * 128:(i + 1) * 128, :], in_=src),
                  QS, 16)
        p.wait(sy, QS)

    with nc.psum_tensor("ps_tz", [128, 512], F32) as ps:
        specs = [(inp["fcT_half"], sh_q2fc, D, FFH, 0),
                 (inp["projT_half"], sh_q2pj, FFH, D, 1)]
        p.wait(v, EV)
        p.inc(v.memset(wk2[:, 0:4], 0.0), EV, 1)
        chunks = []
        for wsrc, _, rn, cn, wi in specs:
            half = rn // 2
            for i in range(half // 128):
                for cc in range((cn + 2047) // 2048):
                    c0, c1 = cc * 2048, min(cn, (cc + 1) * 2048)
                    chunks.append((wsrc, i, c0, c1, wi))
        prev_incs = 0
        for k, (wsrc, i, c0, c1, wi) in enumerate(chunks):
            w_ = c1 - c0
            nsub = (w_ + 511) // 512
            buf = xr[:, (k % 2) * 2048:(k % 2) * 2048 + w_]
            p.wait_lag(g, EV, prev_incs if k > 0 else 0)
            p.inc(g.dma_start(out=buf, in_=wsrc[i * 128:(i + 1) * 128, c0:c1]),
                  QG, 16)
            p.wait(v, QG)
            for s8 in range(nsub):
                a, b = s8 * 512, min(w_, (s8 + 1) * 512)
                p.inc(v.tensor_reduce(wk2[:, 4 + s8:5 + s8], buf[:, a:b],
                                      AX.X, ALU.add,
                                      apply_absolute_value=True), EV, 1)
            p.wait(v, EV)
            p.inc(v.tensor_reduce(wk2[:, 2:3], wk2[:, 4:4 + nsub], AX.X,
                                  ALU.add), EV, 1)
            p.wait(v, EV)
            p.inc(v.tensor_tensor(wk2[:, wi:wi + 1], wk2[:, wi:wi + 1],
                                  wk2[:, 2:3], ALU.add), EV, 1)
            p.wait(v, EV)
            prev_incs = nsub + 2
        p.wait(te, EV)
        p.inc(te.matmul(ps[0:1, 0:2], ones_colF[:], wk2[:, 0:2], start=True,
                        stop=True), ET, 1)
        p.wait(v, ET)
        p.inc(v.tensor_copy(srow[0:1, 0:2], ps[0:1, 0:2]), EV, 1)
        p.wait(v, EV)
        p.inc(v.tensor_copy(ph_sb[0:1, 4:6].bitcast(F32), srow[0:1, 0:2]), EV, 1)
        p.wait(g, EV)
        flag_send()
        flag_wait()
        p.inc(g.engine_nop(), EP, 1)
        p.wait(v, EP)
        p.inc(v.tensor_tensor(srow[0:1, 4:6], srow[0:1, 0:2],
                              pf_sb[0:1, 4:6].bitcast(F32), ALU.add), EV, 1)
        p.wait(v, EV)
        p.inc(v.tensor_scalar_mul(srow[0:1, 6:7], srow[0:1, 4:5], 1.0 / (D * FFH)),
              EV, 1)
        p.inc(v.tensor_scalar_mul(srow[0:1, 7:8], srow[0:1, 5:6], 1.0 / (D * FFH)),
              EV, 1)
        p.wait(v, EV)
        p.inc(v.tensor_scalar_max(srow[0:1, 6:8], srow[0:1, 6:8], 1e-5), EV, 1)
        if "scal" in tap:
            p.wait(g, EV)
            p.inc(g.dma_start(out=tap["scal"][0:1, 0:8], in_=srow[0:1, 0:8]), QG, 16)
            p.inc(g.dma_start(out=tap["scal"][0:1, 8:24], in_=pf_sb[:].bitcast(F32)), QG, 16)
            p.inc(g.dma_start(out=tap["scal"][0:1, 24:32], in_=ph_sb[0:1, 0:8].bitcast(F32)), QG, 16)
            p.wait(g, QG)
        p.wait(v, EV)
        p.inc(v.tensor_scalar_mul(srow[0:1, 8:10], srow[0:1, 6:8], 0.7), EV, 1)
        p.wait(v, EV)
        p.inc(v.tensor_tensor(srow[0:1, 10:11], srow[0:1, 6:7], srow[0:1, 6:7],
                              ALU.mult), EV, 1)
        p.wait(v, EV)
        p.inc(v.tensor_tensor(srow[0:1, 10:11], srow[0:1, 10:11], srow[0:1, 7:8],
                              ALU.mult), EV, 1)
        p.wait(v, EV)
        p.inc(v.tensor_scalar_mul(srow[0:1, 10:11], srow[0:1, 10:11], 0.125), EV, 1)
        p.wait(te, EV)
        p.inc(te.matmul(ps[:, 0:4], ones_rowF[:], srow[0:1, 8:12], start=True,
                        stop=True), ET, 1)
        p.wait(v, ET)
        p.inc(v.tensor_copy(sc_ffn[:], ps[:, 2:3]), EV, 1)

        bounce = {0: q2fc_loc, 1: q2pj_loc}
        for wsrc, wdst, rn, cn, wi in specs:
            p.wait(v, ET)
            p.inc(v.tensor_copy(thr_c[:, 0:1], ps[:, wi:wi + 1]), EV, 1)
            p.wait(v, EV)
            p.inc(v.tensor_scalar_mul(thr_c[:, 1:2], thr_c[:, 0:1], -1.0), EV, 1)
            half = rn // 2
            for i in range(half // 128):
                nch_c = (cn + 2047) // 2048
                for cc in range(nch_c):
                    c0, c1 = cc * 2048, min(cn, (cc + 1) * 2048)
                    w_ = c1 - c0
                    p.wait(g, EV)
                    p.wait(g, EA)
                    p.inc(g.dma_start(out=wk[:, 0:w_],
                                      in_=wsrc[i * 128:(i + 1) * 128, c0:c1]),
                          QG, 16)
                    p.wait(sc, QG)
                    p.wait(sc, EV)
                    p.inc(sc.activation(out=zb[:, 0:w_], in_=wk[:, 0:w_],
                                        func=AF.Sign, bias=thr_c[:, 0:1]), EA, 1)
                    p.inc(sc.activation(out=zb2[:, 0:w_], in_=wk[:, 0:w_],
                                        func=AF.Sign, bias=thr_c[:, 1:2]), EA, 1)
                    p.wait(v, EA)
                    p.inc(v.tensor_tensor(zb[:, 0:w_], zb[:, 0:w_], zb2[:, 0:w_],
                                          ALU.add), EV, 1)
                    p.wait(g, EV)
                    p.inc(g.dma_start(out=bounce[wi][i * 128:(i + 1) * 128, c0:c1],
                                      in_=zb[:, 0:w_]), QG, 16)
        p.wait(g, QG)
        p.inc(g.dma_start(out=sh_q2fc[bass.ds(me_v, 1), :, :], in_=q2fc_loc),
              QG, 16)
        p.inc(g.dma_start(out=sh_q2pj[bass.ds(me_v, 1), :, :], in_=q2pj_loc),
              QG, 16)
        p.wait(g, QG)
        flag_send()

    if upto == "tern":
        if "q2fc" in tap:
            p.wait(sy, QG)
            flag_wait()
            for cc in range(2):
                p.wait(sy, EV)
                p.inc(sy.dma_start(out=zb[:, 0:2048],
                                   in_=sh_q2fc[0, 0:128, cc * 2048:(cc + 1) * 2048]),
                      QS, 16)
                p.wait(v, QS)
                p.inc(v.tensor_copy(wk[:, 0:2048], zb[:, 0:2048]), EV, 1)
                p.wait(sy, EV)
                p.inc(sy.dma_start(out=tap["q2fc"][:, cc * 2048:(cc + 1) * 2048],
                                   in_=wk[:, 0:2048]), QS, 16)
        if "scffn" in tap:
            p.wait(sy, EV)
            p.inc(sy.dma_start(out=tap["scffn"], in_=sc_ffn[:]), QS, 16)
        finish()
        return nc

    _ln_n = [0]

    def ln_to_u(x_tile_fn, u_arena, send_shared):
        _ln_n[0] += 1
        with nc.psum_tensor("ps_ln%d" % _ln_n[0], [128, 2048], F32) as ps_ln:
            p.wait(v, QG)
            for i in range(DT):
                xi = x_tile_fn(i)
                p.inc(v.tensor_copy(u_arena[:, i * TT:(i + 1) * TT], xi), EV, 1)
                p.wait(sc, ET)
                p.wait(sc, EV)
                p.inc(sc.activation(out=zb[:, 0:TT], in_=xi,
                                    func=AF.Square), EA, 1)
                p.wait(te, EV)
                p.wait(te, EA)
                sl = u_arena[:, i * TT:(i + 1) * TT]
                for tch in range(2):
                    c = slice(tch * 512, (tch + 1) * 512)
                    p.inc(te.matmul(ps_ln[0:1, tch * 512:(tch + 1) * 512],
                                    ones_col[:], sl[:, c], start=(i == 0),
                                    stop=(i == DT - 1)), ET, 1)
                    p.inc(te.matmul(ps_ln[0:1, 1024 + tch * 512:1536 + tch * 512],
                                    ones_col[:], zb[:, 0:TT][:, c],
                                    start=(i == 0), stop=(i == DT - 1)), ET, 1)
            p.wait(v, ET)
            p.inc(v.tensor_scalar_mul(rows[0:1, 0:TT], ps_ln[0:1, 0:TT], 1.0 / D),
                  EV, 1)
            p.inc(v.tensor_scalar_mul(rows[0:1, TT:2 * TT],
                                      ps_ln[0:1, 1024:1024 + TT], 1.0 / D), EV, 1)
            p.wait(v, EV)
            p.inc(v.tensor_tensor(wk2[0:1, 0:TT], rows[0:1, 0:TT],
                                  rows[0:1, 0:TT], ALU.mult), EV, 1)
            p.wait(v, EV)
            p.inc(v.tensor_tensor(rows[0:1, TT:2 * TT], rows[0:1, TT:2 * TT],
                                  wk2[0:1, 0:TT], ALU.subtract), EV, 1)
            p.wait(v, EV)
            p.inc(v.tensor_scalar_add(rows[0:1, TT:2 * TT], rows[0:1, TT:2 * TT],
                                      EPS), EV, 1)
            p.wait(v, EV)
            p.inc(v.reciprocal(wk2[0:1, 0:TT], rows[0:1, TT:2 * TT]), EV, 1)
            p.wait(sc, EV)
            p.inc(sc.activation(out=rows[0:1, TT:2 * TT], in_=wk2[0:1, 0:TT],
                                func=AF.Sqrt), EA, 1)
            p.wait(v, EA)
            p.inc(v.tensor_copy(zb[0:1, 0:TT], rows[0:1, 0:TT]), EV, 1)
            p.inc(v.tensor_copy(zb[0:1, TT:2 * TT], rows[0:1, TT:2 * TT]),
                  EV, 1)
            p.wait(te, EV)
            for tch in range(2):
                p.inc(te.matmul(ps_ln[:, tch * 512:(tch + 1) * 512], ones_row[:],
                                zb[0:1, tch * 512:(tch + 1) * 512], start=True,
                                stop=True), ET, 1)
                p.inc(te.matmul(ps_ln[:, 1024 + tch * 512:1536 + tch * 512],
                                ones_row[:],
                                zb[0:1, TT + tch * 512:TT + (tch + 1) * 512],
                                start=True, stop=True), ET, 1)
            p.wait(v, ET)
            p.inc(v.tensor_copy(mub[:], ps_ln[:, 0:TT]), EV, 1)
            p.inc(v.tensor_copy(rsb[:], ps_ln[:, 1024:1024 + TT]), EV, 1)
            for i in range(DT):
                sl = u_arena[:, i * TT:(i + 1) * TT]
                p.wait(v, EV)
                p.inc(v.tensor_tensor(zb2[:, 0:TT], sl, mub[:],
                                      ALU.subtract), EV, 1)
                p.wait(v, EV)
                p.inc(v.tensor_tensor(sl, zb2[:, 0:TT], rsb[:],
                                      ALU.mult), EV, 1)


    def xin_tile(i):
        p.wait(g, EV)
        p.wait(g, EA)
        slot = (i % 2) * TT
        p.inc(g.dma_start(out=wk[:, slot:slot + TT],
                          in_=inp["xin"][i * 128:(i + 1) * 128, :]), QG, 16)
        p.wait(v, QG)
        p.wait(sc, QG)
        return wk[:, slot:slot + TT]

    ln_to_u(xin_tile, ub, None)
    p.wait(g, EV)
    u_dst = sh_u[bass.ds(me_v, 1), :, :].rearrange("a (i q) t -> (a q) i t", q=128)
    p.inc(g.dma_start(out=u_dst, in_=ub[:, :].rearrange("q (i t) -> q i t", i=DT)),
          QG, 16)
    p.wait(g, QG)
    flag_send()
    flag_wait()

    if "u" in tap:
        p.wait(g, EV)
        for i in range(DT):
            p.inc(g.dma_start(out=tap["u"][i * 128:(i + 1) * 128, :],
                              in_=ub[:, i * TT:(i + 1) * TT]), QG, 16)
        p.wait(g, QG)

    if upto == "u":
        finish()
        return nc


    p.inc(g.dma_start(out=wpj[:, 0:DT * 112].rearrange("q (k r) -> q k r", k=DT),
                      in_=inp["wprj_cat"].rearrange("(k q) r -> q k r", q=128)),
          QG, 16)
    p.inc(g.dma_start(out=wouT[:], in_=inp["wdt_outT_own"]), QG, 16)

    with nc.psum_tensor("ps_pr", [128, 2048], F32) as ps_pr:
        for th in range(2):
            for i in range(DT):
                sl = (i % 2) * TT
                p.wait(g, ET)
                p.inc(g.dma_start(out=zb[:, sl:sl + TT],
                                  in_=sh_u[th, i * 128:(i + 1) * 128, :]), QG, 16)
                p.wait(te, QG)
                first, last = (i == 0), (i == DT - 1)
                for tc in range(2):
                    cc = slice(sl + tc * 512, sl + (tc + 1) * 512)
                    oc = slice(th * 1024 + tc * 512, th * 1024 + (tc + 1) * 512)
                    p.inc(te.matmul(ps_pr[0:112, oc],
                                    wpj[:, i * 112:(i + 1) * 112], zb[:, cc],
                                    start=first, stop=last), ET, 1)
        p.wait(v, ET)
        p.inc(v.tensor_copy(d64b[:], ps_pr[0:64, :]), EV, 1)
        p.inc(v.tensor_copy(brawb[:], ps_pr[64:80, :]), EV, 1)
        p.inc(v.tensor_copy(crawb[:], ps_pr[96:112, :]), EV, 1)
        p.wait(te, EV)

        for ht in range(8):
            p.wait(te, EV)
            p.wait(te, EA)
            for tc in range(4):
                cc = slice(tc * 512, (tc + 1) * 512)
                p.inc(te.matmul(ps_pr[:, cc], wouT[:, ht * 128:(ht + 1) * 128],
                                d64b[:, cc], start=True, stop=True), ET, 1)
            for th2 in range(2):
                hc = slice(th2 * TT, (th2 + 1) * TT)
                p.wait(sc, ET)
                p.wait(sc, EV)
                p.inc(sc.activation(out=wk[:, 0:TT], in_=ps_pr[:, hc], func=AF.Abs,
                                    bias=bdt[:, ht:ht + 1]), EA, 1)
                p.wait(sc, EA)
                p.inc(sc.activation(out=wk[:, TT:2 * TT], in_=wk[:, 0:TT],
                                    func=AF.Exp, scale=-1.0), EA, 1)
                p.wait(sc, EA)
                p.inc(sc.activation(out=wk[:, 0:TT], in_=wk[:, TT:2 * TT],
                                    func=AF.Ln, bias=1.0), EA, 1)
                p.wait(v, EA)
                p.inc(v.tensor_scalar(wk2[:, 0:TT], ps_pr[:, hc],
                                      bdt[:, ht:ht + 1], 0.0, ALU.add, ALU.max),
                      EV, 1)
                p.wait(v, EV)
                p.inc(v.tensor_tensor(cb[:, ht * 2048 + th2 * TT:
                                         ht * 2048 + (th2 + 1) * TT],
                                      wk2[:, 0:TT], wk[:, 0:TT], ALU.add), EV, 1)

    if upto == "dt":
        if "dt" in tap:
            p.wait(g, EV)
            for ht in range(8):
                p.inc(g.dma_start(out=tap["dt"][ht * 128:(ht + 1) * 128, :],
                                  in_=cb[:, ht * 2048:(ht + 1) * 2048]), QG, 16)
        if "braw" in tap:
            p.wait(g, EV)
            p.inc(g.dma_start(out=tap["braw"][0:16, :], in_=brawb[:]), QG, 16)
            p.inc(g.dma_start(out=tap["braw"][16:32, :], in_=crawb[:]), QG, 16)
        p.wait(g, QG)
        finish()
        return nc

    A_own = A_a
    ubb = xr[:, :].bitcast(BF16)
    with (nc.psum_tensor("ps_y", [128, 1024], F32) as ps_y,
          nc.psum_tensor("ps_w", [128, 1024], F32) as ps_w):
        for th in range(2):
            p.wait(g, EV)
            p.wait(g, EA)
            src = sh_u[th, :, :][bass.ds(me_v * 1024, 1024), :].rearrange(
                "(i q) t -> q i t", q=128)
            p.inc(g.dma_start(out=ar[:, th * 8 * TT:(th + 1) * 8 * TT].rearrange(
                "q (i t) -> q i t", i=8), in_=src), QG, 16)
            for n in range(NA):
                for tc in range(2):
                    cc = slice(th * TT + tc * 512, th * TT + (tc + 1) * 512)
                    p.wait(te, EA)
                    p.wait(te, EV)
                    p.inc(te.matmul(ps_w[:, tc * 512:(tc + 1) * 512],
                                    ident64[0:16, n:n + 1].broadcast_to([16, 128]),
                                    brawb[:, cc], start=True, stop=True), ET, 1)
                    p.inc(te.matmul(ps_w[:, 512 + tc * 512:512 + (tc + 1) * 512]
                                    if False else ps_y[:, tc * 512:(tc + 1) * 512],
                                    ident64[0:16, n:n + 1].broadcast_to([16, 128]),
                                    crawb[:, cc], start=True, stop=True), ET, 1)
                p.wait(sc, ET)
                p.inc(sc.copy(ubb[:, n * TT:(n + 1) * TT], ps_w[:, 0:TT]), EA, 1)
                p.inc(sc.copy(ubb[:, 16384 + n * TT:16384 + (n + 1) * TT],
                              ps_y[:, 0:TT]), EA, 1)
            p.wait(g, QG)
            for ht in range(8):
                dtsl = cb[:, ht * 2048 + th * TT:ht * 2048 + (th + 1) * TT]
                ulsl = ar[:, th * 8 * TT + ht * TT:th * 8 * TT + (ht + 1) * TT]
                p.wait(v, QG)
                p.wait(v, EA)
                p.inc(v.tensor_tensor(zb[:, 0:TT], dtsl, ulsl, ALU.mult), EV, 1)
                for n in range(NA):
                    col = ht * 16 + n
                    p.wait(sc, EV)
                    p.wait(sc, ET)
                    p.inc(sc.activation(out=wk[:, 0:TT], in_=dtsl, func=AF.Exp,
                                        scale=A_own[:, col:col + 1]), EA, 1)
                    p.wait(v, EA)
                    p.inc(v.tensor_tensor(zb[:, TT:2 * TT], zb[:, 0:TT],
                                          ubb[:, n * TT:(n + 1) * TT], ALU.mult),
                          EV, 1)
                    p.wait(v, EV)
                    if th == 0:
                        p.inc(v.tensor_tensor_scan(zb2[:, TT:2 * TT],
                                                   wk[:, 0:TT], zb[:, TT:2 * TT],
                                                   0.0, ALU.mult, ALU.add), EV, 1)
                    else:
                        p.inc(v.tensor_tensor_scan(zb2[:, TT:2 * TT],
                                                   wk[:, 0:TT], zb[:, TT:2 * TT],
                                                   st_a[:, col:col + 1],
                                                   ALU.mult, ALU.add), EV, 1)
                    p.wait(v, EV)
                    if th == 0:
                        p.inc(v.tensor_copy(st_a[:, col:col + 1],
                                            zb2[:, 2 * TT - 1:2 * TT]), EV, 1)
                    p.inc(v.tensor_tensor(zb2[:, 0:TT], zb2[:, TT:2 * TT],
                                          ubb[:, 16384 + n * TT:(n + 1) * TT
                                              + 16384], ALU.mult), EV, 1)
                    p.wait(te, EV)
                    for tc in range(2):
                        cc = slice(tc * 512, (tc + 1) * 512)
                        p.inc(te.matmul(ps_y[:, cc], identB[:], zb2[:, 0:TT][:, cc],
                                        start=(n == 0), stop=(n == NA - 1)), ET, 1)
                p.wait(v, ET)
                p.inc(v.scalar_tensor_tensor(
                    ub[:, ht * 2048 + th * TT:ht * 2048 + (th + 1) * TT], ulsl,
                    Da[:, ht:ht + 1], ps_y[:], ALU.mult, ALU.add), EV, 1)

    p.wait(g, EV)
    ya_dst = sh_ya[bass.ds(me_v, 1), :, :].rearrange("a (i q) t -> (a q) i t", q=128)
    p.inc(g.dma_start(out=ya_dst, in_=ub[:, :].rearrange("q (i t) -> q i t", i=8)),
          QG, 16)
    p.wait(g, QG)
    flag_send()
    flag_wait()

    for sl_i in range(2):
        src = sh_ya[sl_i, :, :][:, bass.ds(me_v * 1024, 1024)].rearrange(
            "(i q) t -> q i t", q=128)
        p.inc(g.dma_start(out=ar[:, sl_i * 8 * TT:(sl_i + 1) * 8 * TT].rearrange(
            "q (i t) -> q i t", i=8), in_=src), QG, 16)
    for i in range(DT):
        sl = (i % 2) * TT
        p.wait(sy, EV)
        p.wait(sy, EA)
        p.inc(sy.dma_start(out=wk[:, sl:sl + TT],
                           in_=inp["xin"][i * 128:(i + 1) * 128, :]), QS, 16)
        p.wait(v, QS)
        p.wait(v, QG)
        p.inc(v.tensor_tensor(xr[:, i * TT:(i + 1) * TT], wk[:, sl:sl + TT],
                              ar[:, i * TT:(i + 1) * TT], ALU.add), EV, 1)

    if "x1" in tap:
        p.wait(sy, EV)
        for i in range(DT):
            p.inc(sy.dma_start(out=tap["x1"][i * 128:(i + 1) * 128, :],
                               in_=xr[:, i * TT:(i + 1) * TT]), QS, 16)
    if upto == "x1":
        finish()
        return nc

    def ffn_block(first):
        def xr_tile(i):
            return xr[:, i * TT:(i + 1) * TT]
        ln_to_u(xr_tile, ub, None)
        if first:
            flag_wait(expect=2)
        sfx = "a" if first else "b"
        with (nc.psum_tensor("ps_f0" + sfx, [128, 1024], F32) as f0,
              nc.psum_tensor("ps_f1" + sfx, [128, 1024], F32) as f1,
              nc.psum_tensor("ps_f2" + sfx, [128, 1024], F32) as f2,
              nc.psum_tensor("ps_f3" + sfx, [128, 1024], F32) as f3):
            ps4 = [f0, f1, f2, f3]
            for mb in range(10):
                buf = (mb % 2) * 8192
                mc = slice(mb * 512, (mb + 1) * 512)
                p.wait_lag(g, ET, 128 if mb > 0 else 0)
                p.wait_lag(sy, ET, 128 if mb > 0 else 0)
                for sl_i in range(2):
                    src = sh_q2fc[sl_i, :, mc].rearrange("(k q) m -> q k m", q=128)
                    eng, qn = (g, QG) if sl_i == 0 else (sy, QS)
                    p.inc(eng.dma_start(
                        out=ar[:, buf + sl_i * 4096:buf + (sl_i + 1) * 4096]
                        .rearrange("q (k m) -> q k m", k=8), in_=src), qn, 16)
                p.wait(te, QG)
                p.wait(te, QS)
                p.wait(te, EV)
                p.wait(te, EA)
                for m in range(4):
                    for k in range(DT):
                        lhs = ar[:, buf + k * 512 + m * 128:
                                 buf + k * 512 + (m + 1) * 128]
                        for tch in range(2):
                            p.inc(te.matmul(
                                ps4[m][:, tch * 512:(tch + 1) * 512], lhs,
                                ub[:, k * TT + tch * 512:k * TT + (tch + 1) * 512],
                                start=(k == 0), stop=(k == DT - 1)), ET, 1)
                for m in range(4):
                    p.wait(v, ET)
                    p.wait(v, QG)
                    p.wait(v, EA)
                    p.inc(v.tensor_scalar_max(wk[:, 0:TT], ps4[m][:, 0:TT], 0.0),
                          EV, 1)
                    p.wait(sc, EV)
                    p.inc(sc.activation(out=zb[:, TT:2 * TT], in_=wk[:, 0:TT],
                                        func=AF.Square), EA, 1)
                    p.wait(g, EA)
                    p.inc(g.dma_start(out=h_dram[(mb * 4 + m) * 128:
                                                 (mb * 4 + m + 1) * 128, :],
                                      in_=zb[:, TT:2 * TT]), QG, 16)
        with (nc.psum_tensor("ps_p0" + sfx, [128, 1024], F32) as p0,
              nc.psum_tensor("ps_p1" + sfx, [128, 1024], F32) as p1,
              nc.psum_tensor("ps_p2" + sfx, [128, 1024], F32) as p2,
              nc.psum_tensor("ps_p3" + sfx, [128, 1024], F32) as p3):
            pp = [p0, p1, p2, p3]
            p.wait(g, QG)
            p.wait(sy, QG)
            for m2b in range(4):
                mc = slice(m2b * 512, (m2b + 1) * 512)
                for k2 in range(40):
                    wbuf = (k2 % 2) * 1024
                    hbuf = 2048 + (k2 % 2) * 1024
                    sl_i, row = k2 // 20, (k2 % 20) * 128
                    p.wait_lag(g, ET, 8)
                    p.wait_lag(sy, ET, 8)
                    p.inc(g.dma_start(out=ar[:, wbuf:wbuf + 512],
                                      in_=sh_q2pj[sl_i, row:row + 128, mc]),
                          QG, 16)
                    p.inc(sy.dma_start(out=ar[:, hbuf:hbuf + 1024],
                                       in_=h_dram[k2 * 128:(k2 + 1) * 128, :]),
                          QS, 16)
                    p.wait(te, QG)
                    p.wait(te, QS)
                    if k2 == 0:
                        p.wait(te, EV)
                    for m2 in range(4):
                        lhs = ar[:, wbuf + m2 * 128:wbuf + (m2 + 1) * 128]
                        for tch in range(2):
                            p.inc(te.matmul(
                                pp[m2][:, tch * 512:(tch + 1) * 512], lhs,
                                ar[:, hbuf + tch * 512:hbuf + (tch + 1) * 512],
                                start=(k2 == 0), stop=(k2 == 39)), ET, 1)
                for m2 in range(4):
                    d_i = m2b * 4 + m2
                    p.wait(v, ET)
                    p.inc(v.scalar_tensor_tensor(
                        xr[:, d_i * TT:(d_i + 1) * TT], pp[m2][:, 0:TT],
                        sc_ffn[:, 0:1], xr[:, d_i * TT:(d_i + 1) * TT],
                        ALU.mult, ALU.add), EV, 1)

    ffn_block(first=True)
    if "rows2" in tap:
        p.wait(sy, EV)
        p.wait(sy, EA)
        p.inc(sy.dma_start(out=tap["rows2"], in_=wk[0:1, :]), QS, 16)
    if "u2" in tap:
        p.wait(g, EV)
        for i in range(DT):
            p.inc(g.dma_start(out=tap["u2"][i * 128:(i + 1) * 128, :],
                              in_=ub[:, i * TT:(i + 1) * TT]), QG, 16)
    if "hff" in tap:
        p.wait(g, QG)
        for i in range(8):
            p.inc(g.dma_start(out=wk[:, 0:TT], in_=h_dram[i * 128:(i + 1) * 128, :]), QG, 16)
            p.wait(g, QG)
            p.inc(g.dma_start(out=tap["hff"][i * 128:(i + 1) * 128, :], in_=wk[:, 0:TT]), QG, 16)
            p.wait(g, QG)
    if "x2" in tap:
        p.wait(sy, EV)
        for i in range(DT):
            p.inc(sy.dma_start(out=tap["x2"][i * 128:(i + 1) * 128, :],
                               in_=xr[:, i * TT:(i + 1) * TT]), QS, 16)
    if upto == "x2":
        finish()
        return nc

    def xr_tile2(i):
        return xr[:, i * TT:(i + 1) * TT]

    ln_to_u(xr_tile2, ub, None)
    if "u3d" in tap:
        p.wait(g, EV)
        for i in range(DT):
            p.inc(g.dma_start(out=tap["u3d"][i * 128:(i + 1) * 128, :],
                              in_=ub[:, i * TT:(i + 1) * TT]), QG, 16)
        p.wait(g, QG)

    with (nc.psum_tensor("ps_i0", [128, 1024], F32) as i0,
          nc.psum_tensor("ps_i1", [128, 1024], F32) as i1,
          nc.psum_tensor("ps_i2", [128, 1024], F32) as i2,
          nc.psum_tensor("ps_i3", [128, 1024], F32) as i3):
        ps4b = [i0, i1, i2, i3]
        for mb in range(17):
            m0 = mb * 512
            mw = min(8512, (mb + 1) * 512) - m0
            buf = (mb % 2) * 8192
            p.wait(g, ET)
            p.inc(g.dma_start(
                out=ar[:, buf:buf + 16 * mw].rearrange("q (k m) -> q k m", k=DT),
                in_=inp["inpT"][:, m0:m0 + mw].rearrange("(k q) m -> q k m", q=128)),
                QG, 16)
            p.wait(te, QG)
            p.wait(te, EV)
            p.wait(te, EA)
            ntile = (mw + 127) // 128
            for m in range(ntile):
                mrows = min(mw - m * 128, 128)
                for k in range(DT):
                    lhs = ar[:, buf + k * mw + m * 128:buf + k * mw + m * 128 + mrows]
                    for tch in range(2):
                        p.inc(te.matmul(
                            ps4b[m][0:mrows, tch * 512:(tch + 1) * 512], lhs,
                            ub[:, k * TT + tch * 512:k * TT + (tch + 1) * 512],
                            start=(k == 0), stop=(k == DT - 1)), ET, 1)
            for m in range(ntile):
                mt = mb * 4 + m
                mrows = min(mw - m * 128, 128)
                if mt < 32:
                    p.wait(sc, ET)
                    p.wait(sc, QG)
                    p.inc(sc.activation(out=zb[:, 0:TT], in_=ps4b[m][:, 0:TT],
                                        func=AF.Silu), EA, 1)
                    p.wait(g, EA)
                    p.inc(g.dma_start(out=gate_dram[mt * 128:(mt + 1) * 128, :],
                                      in_=zb[:, 0:TT]), QG, 16)
                else:
                    row = (mt - 32) * 128
                    p.wait(v, ET)
                    p.wait(v, QG)
                    p.inc(v.tensor_copy(zb2[0:mrows, 0:TT], ps4b[m][0:mrows, 0:TT]),
                          EV, 1)
                    p.wait(g, EV)
                    p.inc(g.dma_start(out=hbc_loc[row:row + mrows, :],
                                      in_=zb2[0:mrows, 0:TT]), QG, 16)
    if "sg2" in tap:
        p.wait(g, QG)
        for j in range(16):
            p.inc(g.dma_start(out=zb[:, 0:TT], in_=gate_dram[j * 128:(j + 1) * 128, :]), QG, 16)
            p.wait(g, QG)
            p.inc(g.dma_start(out=tap["sg2"][j * 128:(j + 1) * 128, :], in_=zb[:, 0:TT]), QG, 16)
            p.wait(g, QG)
    if "hbcl" in tap:
        p.wait(g, QG)
        for j in range(6):
            p.inc(g.dma_start(out=zb[:, 0:TT], in_=hbc_loc[j * 128:(j + 1) * 128, :]), QG, 16)
            p.wait(g, QG)
            p.inc(g.dma_start(out=tap["hbcl"][j * 128:(j + 1) * 128, :], in_=zb[:, 0:TT]), QG, 16)
            p.wait(g, QG)
    p.wait(g, QG)
    p.inc(g.dma_start(out=sh_hbc[bass.ds(me_v, 1), :, :], in_=hbc_loc), QG, 16)
    p.wait(g, QG)
    flag_send()
    flag_wait()

    p.inc(g.dma_start(out=d64b[0:32, :].rearrange("q (s t) -> q s t", s=2),
                      in_=sh_hbc[:, :, :][:, bass.ds(me_v * 32 + 4352, 32), :]
                      .rearrange("s q t -> q s t")), QG, 16)
    p.inc(g.dma_start(out=zconv_dram,
                      in_=sh_hbc[:, :, :][:, bass.ds(me_v * 2048, 2048), :]),
          QG, 16)
    adt = ar[0:64, 10240:14336].bitcast(F32)
    identF = ar[:, 14336:14592].bitcast(F32)
    BT0 = ar[:, 14592:14720]
    BT1 = ar[:, 14720:14848]
    zsv = ar[:, 14848:14896]
    Ssb = ar[:, 8192:10240]
    p.wait(v, QG)
    p.inc(v.tensor_copy(identF, identB[:]), EV, 1)
    p.inc(v.memset(Ssb, 0.0), EV, 1)
    for th2 in range(2):
        hc = slice(th2 * TT, (th2 + 1) * TT)
        p.wait(sc, EV)
        p.inc(sc.activation(out=wk[0:32, 0:TT], in_=d64b[0:32, hc], func=AF.Abs,
                            bias=dtb_col[:, 0:1]), EA, 1)
        p.wait(sc, EA)
        p.inc(sc.activation(out=wk[0:32, TT:2 * TT], in_=wk[0:32, 0:TT],
                            func=AF.Exp, scale=-1.0), EA, 1)
        p.wait(sc, EA)
        p.inc(sc.activation(out=wk[0:32, 0:TT], in_=wk[0:32, TT:2 * TT],
                            func=AF.Ln, bias=1.0), EA, 1)
        p.wait(v, EA)
        p.inc(v.tensor_scalar(wk2[0:32, 0:TT], d64b[0:32, hc], dtb_col[:, 0:1],
                              0.0, ALU.add, ALU.max), EV, 1)
        p.wait(v, EV)
        p.inc(v.tensor_tensor(wk2[0:32, 0:TT], wk2[0:32, 0:TT], wk[0:32, 0:TT],
                              ALU.add), EV, 1)
        p.wait(v, EV)
        p.inc(v.tensor_scalar_mul(adt[0:32, hc], wk2[0:32, 0:TT],
                                  Ab_col[:, 0:1]), EV, 1)
        p.inc(v.tensor_copy(d64b[0:32, hc], wk2[0:32, 0:TT]), EV, 1)

    p.wait(g, QG)
    p.wait(g, EV)
    p.inc(g.memset(ar2[0:32, 2048:4096], 1.0), EP, 1)
    p.wait(g, EP)
    p.inc(g.affine_select(ar2[0:32, 2048:4096].rearrange("q (a b) -> q a b", a=32),
                          ar2[0:32, 2048:4096].rearrange("q (a b) -> q a b", a=32),
                          [[1, 32], [0, 64]], ALU.is_equal, 0.0,
                          base=0, channel_multiplier=-1), EP, 1)

    p.wait(g, EV)
    p.wait(g, EA)
    p.inc(g.memset(zb[:, 1408:1920], 0.0), EP, 1)
    p.wait(g, EP)
    for sch in range(2):
        p.inc(g.affine_select(zb[:, 1408 + sch * 256:1664 + sch * 256],
                              zb[:, 1408 + sch * 256:1664 + sch * 256],
                              [[1, LCH]], ALU.is_ge, -1e30,
                              base=-(sch * 128), channel_multiplier=-1), EP, 1)

    with (nc.psum_tensor("ps_gm", [128, 1024], F32) as ps_gm,
          nc.psum_tensor("ps_y2", [128, 512], F32) as ps_y2,
          nc.psum_tensor("ps_sd", [128, 512], F32) as ps_sd,
          nc.psum_tensor("ps_t", [128, 512], F32) as ps_t,
          nc.psum_tensor("ps_tb", [128, 1024], BF16) as ps_tb,
          nc.psum_tensor("ps_x", [128, 512], F32) as ps_x,
          nc.psum_tensor("ps_m", [128, 512], F32) as ps_m):
        for th in range(2):
            p.wait(g, EV)
            p.wait(g, EA)
            p.wait(g, ET)
            src = zconv_dram[th, :, :].rearrange("(i q) t -> q i t", q=128)
            p.inc(g.dma_start(out=ub[:, 0:16 * TT].rearrange(
                "q (i t) -> q i t", i=16), in_=src), QG, 16)
            def conv_tile(zsrc, wj, dst, edge_src):
                p.wait(v, EA)
                p.inc(v.memset(wk[:, 0:3], 0.0), EV, 1)
                p.inc(v.tensor_single_scalar(wk[:, 3:TT], zsrc[:, 0:TT - 3],
                                             convw[:, wj * 4:wj * 4 + 1],
                                             ALU.mult), EV, 1)
                for kk in range(1, 4):
                    p.wait(v, EV)
                    p.inc(v.scalar_tensor_tensor(
                        wk[:, 3 - kk:TT], zsrc[:, 0:TT - 3 + kk],
                        convw[:, wj * 4 + kk:wj * 4 + kk + 1],
                        wk[:, 3 - kk:TT], ALU.mult, ALU.add), EV, 1)
                if edge_src is not None:
                    for kk in range(3):
                        p.wait(v, EV)
                        p.inc(v.scalar_tensor_tensor(
                            wk[:, 0:3 - kk], edge_src[:, kk:3],
                            convw[:, wj * 4 + kk:wj * 4 + kk + 1],
                            wk[:, 0:3 - kk], ALU.mult, ALU.add), EV, 1)
                p.wait(sc, EV)
                p.inc(sc.activation(out=dst, in_=wk[:, 0:TT], func=AF.Silu,
                                    bias=convb[:, wj:wj + 1]), EA, 1)

            p.wait(v, QG)
            for j in range(16):
                edge = zsv[:, j * 3:j * 3 + 3] if th == 1 else None
                conv_tile(ub[:, j * TT:(j + 1) * TT], j,
                          cb[:, j * TT:(j + 1) * TT], edge)
            for j in (16, 17):
                base = 4096 + (j - 16) * 128
                p.wait(g, EV)
                p.wait(g, ET)
                p.inc(g.dma_start(out=zb[:, 0:TT], in_=sh_hbc[th, base:base + 128, :]),
                      QG, 16)
                if th == 1:
                    p.inc(g.dma_start(out=zb[:, TT:TT + 3],
                                      in_=sh_hbc[0, base:base + 128, TT - 3:TT]),
                          QG, 16)
                p.wait(v, QG)
                edge = zb[:, TT:TT + 3] if th == 1 else None
                conv_tile(zb[:, 0:TT], j, ar2[:, (j - 16) * TT:(j - 15) * TT], edge)
            if th == 0:
                for j in range(16):
                    p.wait(v, EA)
                    p.inc(v.tensor_copy(zsv[:, j * 3:j * 3 + 3],
                                        ub[:, j * TT + TT - 3:(j + 1) * TT]), EV, 1)
            if "bc" in tap:
                p.wait(g, EA)
                p.inc(g.dma_start(out=tap["bc"][:, th * TT:(th + 1) * TT],
                                  in_=ar2[:, 0:TT]), QG, 16)
                p.inc(g.dma_start(out=tap["bc"][:, 2048 + th * TT:2048 + (th + 1) * TT]
                                  if False else
                                  tap["bc"][:, 2 * TT + th * TT:2 * TT + (th + 1) * TT],
                                  in_=ar2[:, TT:2 * TT]), QG, 16)
                p.wait(g, QG)
            if "hsd" in tap:
                p.wait(g, EA)
                for j in range(16):
                    p.inc(g.dma_start(out=tap["hsd"][j * 128:(j + 1) * 128,
                                                     th * TT:(th + 1) * TT],
                                      in_=cb[:, j * TT:(j + 1) * TT]), QG, 16)
                p.wait(g, QG)
            for j in range(16):
                lhs = ar2[0:32, 2048 + 2 * j * 64:2048 + 2 * j * 64 + 128]
                for tc in range(2):
                    p.wait(te, EV)
                    p.wait(te, EA)
                    p.inc(te.matmul(ps_m[:, 0:512], lhs,
                                    d64b[0:32, th * TT + tc * 512:
                                         th * TT + (tc + 1) * 512],
                                    start=True, stop=True), ET, 1)
                    p.wait(v, ET)
                    p.inc(v.tensor_tensor(
                        ub[:, j * TT + tc * 512:j * TT + (tc + 1) * 512],
                        cb[:, j * TT + tc * 512:j * TT + (tc + 1) * 512],
                        ps_m[:, 0:512], ALU.mult), EV, 1)
            for cl in range(4):
                c = th * 4 + cl
                for j in range(16):
                    for sch in range(2):
                        p.wait(te, EV)
                        p.wait(te, EA)
                        p.inc(te.transpose(
                            ps_tb[:, 0:128],
                            ub[:, j * TT + cl * LCH + sch * 128:
                               j * TT + cl * LCH + sch * 128 + 128],
                            identB[:]), ET, 1)
                        p.wait(sc, ET)
                        p.inc(sc.copy(ar[:, sch * 2048 + j * 128:
                                         sch * 2048 + (j + 1) * 128],
                                      ps_tb[:, 0:128]), EA, 1)
                acum = wk2[0:32, 128:384]
                ones_d0 = ones_colF[0:32, 0:1].broadcast_to([32, LCH])
                p.wait(v, EA)
                p.wait(v, ET)
                p.inc(v.tensor_tensor_scan(acum, ones_d0,
                                           adt[0:32, th * TT + cl * LCH:
                                               th * TT + (cl + 1) * LCH],
                                           0.0, ALU.mult, ALU.add), EV, 1)
                p.wait(g, EV)
                p.inc(g.dma_start(out=acum_dram[c, :, :], in_=acum), QG, 16)
                p.wait(g, QG)
                for sch in range(2):
                    p.wait(te, EV)
                    p.wait(te, EA)
                    p.inc(te.transpose(ps_t[:, 0:32],
                                       acum[:, sch * 128:(sch + 1) * 128],
                                       identF[0:32, 0:32]), ET, 1)
                    p.wait(v, ET)
                    p.inc(v.tensor_copy(wk2[:, 384 + sch * 32:416 + sch * 32],
                                        ps_t[:, 0:32]), EV, 1)
                p.wait(sc, EV)
                p.inc(sc.activation(out=zb[0:32, 512:768], in_=acum, func=AF.Exp),
                      EA, 1)
                p.wait(te, EV)
                p.wait(te, EA)
                p.inc(te.matmul(ps_sd[0:1, 0:32], wk2[0:32, 383:384],
                                identF[0:32, 0:32], start=True, stop=True), ET, 1)
                p.wait(sc, ET)
                p.inc(sc.copy(wk2[0:1, 448:480], ps_sd[0:1, 0:32]), EA, 1)
                p.wait(te, EA)
                p.inc(te.matmul(ps_sd[:, 64:96], ones_rowF[:], wk2[0:1, 448:480],
                                start=True, stop=True), ET, 1)
                for sch in range(2):
                    p.wait(v, ET)
                    p.inc(v.tensor_tensor(wk2[:, 512 + sch * 32:544 + sch * 32],
                                          ps_sd[:, 64:96],
                                          wk2[:, 384 + sch * 32:416 + sch * 32],
                                          ALU.subtract), EV, 1)
                    p.wait(sc, EV)
                    p.inc(sc.activation(out=zb[:, 1920 + sch * 32:1952 + sch * 32],
                                        in_=wk2[:, 512 + sch * 32:544 + sch * 32],
                                        func=AF.Exp), EA, 1)
                p.wait(sc, ET)
                p.inc(sc.activation(out=zb[:, 1984:2016], in_=ps_sd[:, 64:96],
                                    func=AF.Exp), EA, 1)
                if "dbg4" in tap and c == 0:
                    p.wait(g, EA)
                    p.inc(g.dma_start(out=tap["dbg4"][0:128, 0:64], in_=wk2[:, 384:448]), QG, 16)
                    p.inc(g.dma_start(out=tap["dbg4"][0:128, 64:128], in_=wk2[:, 512:576]), QG, 16)
                    p.inc(g.dma_start(out=tap["dbg4"][0:128, 128:224], in_=zb[:, 1920:2016]), QG, 16)
                    p.wait(g, QG)
                for grp in range(8):
                    bb_flush()
                    h4 = slice(grp * 4, grp * 4 + 4)
                    for sch in range(2):
                        src = acum_dram[c, grp * 4:grp * 4 + 4, :].rearrange(
                            "h l -> (h l)")[None, :].partition_broadcast(128)
                        p.wait(g, EV)
                        p.wait(g, EP)
                        p.wait(g, EA)
                        p.inc(g.dma_start(out=wk[:, 0:1024], in_=src), QG, 16)
                        acT = wk2[:, 384 + sch * 32:416 + sch * 32]
                        in1 = acT[:, h4].unsqueeze(2).broadcast_to([128, 4, LCH])
                        p.wait(v, QG)
                        p.wait(v, EA)
                        p.inc(v.tensor_tensor(
                            wk[:, 1024:2048].rearrange("q (h l) -> q h l", h=4),
                            wk[:, 0:1024].rearrange("q (h l) -> q h l", h=4),
                            in1, ALU.subtract), EV, 1)
                        p.wait(g, EV)
                        mskap = zb[:, 1408 + sch * 256:1664 + sch * 256] \
                            .unsqueeze(1).broadcast_to([128, 4, LCH])
                        p.inc(g.tensor_tensor(
                            wk[:, 1024:2048].rearrange("q (h l) -> q h l", h=4),
                            wk[:, 1024:2048].rearrange("q (h l) -> q h l", h=4),
                            mskap, ALU.add), EP, 1)
                        p.wait(sc, EP)
                        p.inc(sc.activation(out=wk[:, 0:1024],
                                            in_=wk[:, 1024:2048], func=AF.Exp),
                              EA, 1)
                        if "dbg" in tap and c == 2 and grp == 1:
                            p.wait(g, EA)
                            p.inc(g.dma_start(out=tap["dbg"][sch * 128:sch * 128 + 128, :],
                                              in_=wk[:, 1024:2048]), QG, 16)
                            p.inc(g.dma_start(out=tap["dbg"][256 + sch * 128:
                                                             384 + sch * 128, :],
                                              in_=wk[:, 0:1024]), QG, 16)
                            p.wait(g, QG)
                        p.wait(te, EV)
                        p.wait(te, EA)
                        for hh in range(4):
                            p.inc(te.matmul(
                                ps_gm[:, hh * 256:(hh + 1) * 256],
                                ar2[:, cl * LCH + sch * 128:
                                    cl * LCH + sch * 128 + 128],
                                ar2[:, TT + cl * LCH:TT + (cl + 1) * LCH],
                                start=True, stop=True), ET, 1)
                        p.wait(v, ET)
                        p.wait(v, EA)
                        p.inc(v.tensor_tensor(zb2[:, sch * 1024:(sch + 1) * 1024],
                                              ps_gm[:, 0:1024], wk[:, 0:1024],
                                              ALU.mult), EV, 1)

                    if "dbg2" in tap and c == 1 and grp == 0:
                        p.wait(g, EV)
                        p.inc(g.dma_start(out=tap["dbg2"][0:128, 0:1024], in_=zb2[:, 0:1024]), QG, 16)
                        p.inc(g.dma_start(out=tap["dbg2"][128:256, 0:1024], in_=zb2[:, 1024:2048]), QG, 16)
                        p.inc(g.dma_start(out=tap["dbg2"][256:384, 0:2048], in_=Ssb), QG, 16)
                        p.inc(g.dma_start(out=tap["dbg2"][384:416, 0:256], in_=zb[0:32, 512:768]), QG, 16)
                        p.wait(g, QG)
                    for hh in range(4):
                        hloc = grp * 4 + hh
                        lhs1 = ident64[0:32, hloc:hloc + 1].broadcast_to([32, 128])
                        p.wait(te, EV)
                        p.wait(te, EA)
                        p.inc(te.matmul(ps_sd[:, 0:256], lhs1, zb[0:32, 512:768],
                                        start=True, stop=True), ET, 1)
                        csd = zb[:, (hh % 2) * 256:(hh % 2) * 256 + 256]
                        p.wait(v, ET)
                        p.inc(v.tensor_tensor(csd, ar2[:, TT + cl * LCH:
                                                       TT + (cl + 1) * LCH],
                                              ps_sd[:, 0:256], ALU.mult), EV, 1)
                        pcol = (hh // 2) * 256
                        prow = 64 * (hh % 2)
                        tp = (0, 64) if hh % 2 else None
                        p.wait(te, EV)
                        for sch in range(2):
                            p.inc(te.matmul(
                                ps_y2[prow:prow + 64, pcol:pcol + 256],
                                ar[:, sch * 2048 + hloc * 64:
                                   sch * 2048 + hloc * 64 + 64],
                                zb2[:, sch * 1024 + hh * 256:
                                    sch * 1024 + (hh + 1) * 256],
                                start=(sch == 0), stop=False,
                                tile_position=tp), ET, 1)
                        p.inc(te.matmul(
                            ps_y2[prow:prow + 64, pcol:pcol + 256],
                            Ssb[:, hloc * 64:(hloc + 1) * 64], csd,
                            start=False, stop=True, tile_position=tp), ET, 1)
                        if hh % 2 == 1:
                            pr = grp * 2 + hh // 2
                            p.wait(v, ET)
                            p.wait(v, QG)
                            p.inc(v.scalar_tensor_tensor(
                                zb[:, 1120:1376],
                                cb[:, pr * TT + cl * LCH:pr * TT + (cl + 1) * LCH],
                                Dbr[:, pr:pr + 1],
                                ps_y2[:, pcol:pcol + 256], ALU.mult, ALU.add),
                                EV, 1)
                            p.wait(g, EV)
                            p.inc(g.dma_start(
                                out=yb_loc2[pr * 128:(pr + 1) * 128,
                                            th * TT + cl * LCH:
                                            th * TT + (cl + 1) * LCH],
                                in_=zb[:, 1120:1376]), QG, 16)
                for sch in range(2):
                    p.wait(te, EV)
                    p.wait(te, EA)
                    p.inc(te.transpose(ps_tb[:, 0:128],
                                       ar2[:, cl * LCH + sch * 128:
                                           cl * LCH + sch * 128 + 128],
                                       identB[:]), ET, 1)
                    p.wait(sc, ET)
                    p.inc(sc.copy(BT0 if sch == 0 else BT1, ps_tb[:, 0:128]), EA, 1)
                    dec = zb[:, 1920 + sch * 32:1952 + sch * 32]
                    in1b = dec.unsqueeze(2).broadcast_to([128, 32, 64])
                    p.wait(v, EA)
                    p.inc(v.tensor_tensor(
                        ar[:, 4096 + sch * 2048:4096 + (sch + 1) * 2048]
                        .rearrange("q (h pp) -> q h pp", h=32),
                        ar[:, sch * 2048:(sch + 1) * 2048]
                        .rearrange("q (h pp) -> q h pp", h=32),
                        in1b, ALU.mult), EV, 1)
                if "dbg3" in tap and c == 0:
                    p.wait(g, EV)
                    p.wait(g, EA)
                    p.inc(g.dma_start(out=tap["dbg3"][0:128, 0:32], in_=zb[:, 1920:1952]), QG, 16)
                    p.inc(g.dma_start(out=tap["dbg3"][0:128, 32:64], in_=zb[:, 1952:1984]), QG, 16)
                    p.inc(g.dma_start(out=tap["dbg3"][0:128, 64:96], in_=zb[:, 1984:2016]), QG, 16)
                    p.inc(g.dma_start(out=tap["dbg3"][128:256, 0:2048], in_=ar[:, 4096:6144]), QG, 16)
                    p.inc(g.dma_start(out=tap["dbg3"][256:384, 0:2048], in_=ar[:, 6144:8192]), QG, 16)
                    p.inc(g.dma_start(out=tap["dbg3"][384:512, 0:128], in_=BT0), QG, 16)
                    p.inc(g.dma_start(out=tap["dbg3"][384:512, 128:256], in_=BT1), QG, 16)
                    p.wait(g, QG)
                drep = zb[:, 1984:2016]
                for sl4 in range(4):
                    scol = slice(sl4 * 512, (sl4 + 1) * 512)
                    p.wait(te, EV)
                    p.wait(te, EA)
                    p.inc(te.matmul(ps_x[:, 0:512], BT0,
                                    ar[:, 4096:6144][:, scol],
                                    start=True, stop=False), ET, 1)
                    p.inc(te.matmul(ps_x[:, 0:512], BT1,
                                    ar[:, 6144:8192][:, scol],
                                    start=False, stop=True), ET, 1)
                    dr1 = drep[:, sl4 * 8:(sl4 + 1) * 8].unsqueeze(2) \
                        .broadcast_to([128, 8, 64])
                    p.wait(v, ET)
                    p.inc(v.tensor_tensor(
                        zb2[:, 0:512].rearrange("q (h pp) -> q h pp", h=8),
                        Ssb[:, scol].rearrange("q (h pp) -> q h pp", h=8),
                        dr1, ALU.mult), EV, 1)
                    p.wait(v, EV)
                    p.inc(v.tensor_tensor(Ssb[:, scol], zb2[:, 0:512],
                                          ps_x[:, 0:512], ALU.add), EV, 1)

    if "adt" in tap:
        p.wait(g, EV)
        p.inc(g.dma_start(out=tap["adt"][0:32, :], in_=adt[0:32, :]), QG, 16)
    if "acum" in tap:
        p.wait(g, QG)
        p.inc(g.dma_start(out=tap["acum"],
                          in_=acum_dram.rearrange("c h l -> (c h) l")), QG, 16)
    if "ybl" in tap:
        p.wait(g, QG)
        for j in range(16):
            p.inc(g.dma_start(out=zb[:, 0:2048],
                              in_=yb_loc2[j * 128:(j + 1) * 128, :]), QG, 16)
            p.wait(g, QG)
            p.inc(g.dma_start(out=tap["ybl"][j * 128:(j + 1) * 128, :],
                              in_=zb[:, 0:2048]), QG, 16)
            p.wait(g, QG)
    if "sg3" in tap:
        p.wait(g, QG)
        for j in range(16):
            p.inc(g.dma_start(out=zb[:, 0:TT], in_=gate_dram[j * 128:(j + 1) * 128, :]), QG, 16)
            p.wait(g, QG)
            p.inc(g.dma_start(out=tap["sg3"][j * 128:(j + 1) * 128, :], in_=zb[:, 0:TT]), QG, 16)
            p.wait(g, QG)
    p.wait(g, QG)
    p.inc(g.dma_start(out=sh_yb[bass.ds(me_v, 1), :, :], in_=yb_loc2), QG, 16)
    p.wait(g, QG)
    flag_send()
    flag_wait()

    p.inc(g.dma_start(out=ytok_dram, in_=sh_yb[:, :, bass.ds(me_v * TT, TT)]),
          QG, 16)
    p.wait(sy, QG)
    p.wait(sy, EV)
    for sl_i in range(2):
        dstar = ub if sl_i == 0 else ar
        src = ytok_dram[sl_i, :, :].rearrange("(i q) t -> q i t", q=128)
        p.inc(sy.dma_start(out=dstar[:, 0:16 * TT].rearrange(
            "q (i t) -> q i t", i=16), in_=src), QS, 16)
    p.wait(g, QS)

    if "yraw" in tap:
        p.wait(g, QS)
        p.wait(g, QG)
        for j in range(32):
            p.inc(g.dma_start(out=tap["yraw"][j * 128:(j + 1) * 128, :],
                              in_=ytile(j) if False else
                              (ub if j < 16 else ar)[:, (j % 16) * TT:(j % 16 + 1) * TT]),
                  QG, 16)
        p.wait(g, QG)
    if "argate" in tap:
        p.wait(g, QG)
        for j in range(2):
            p.inc(g.dma_start(out=tap["argate"][j * 128:(j + 1) * 128, :],
                              in_=ar[:, j * TT:(j + 1) * TT]), QG, 16)
            p.inc(g.dma_start(out=tap["argate"][256 + j * 128:256 + (j + 1) * 128, :],
                              in_=ub[:, j * TT:(j + 1) * TT]), QG, 16)
        p.wait(g, QG)
    if "ytk" in tap:
        p.wait(g, QG)
        for sl_i in range(2):
            for j in range(2):
                p.inc(g.dma_start(out=zb[:, 0:TT],
                                  in_=ytok_dram[sl_i, j * 128:(j + 1) * 128, :]), QG, 16)
                p.wait(g, QG)
                p.inc(g.dma_start(out=tap["ytk"][(sl_i * 2 + j) * 128:
                                                 (sl_i * 2 + j + 1) * 128, :],
                                  in_=zb[:, 0:TT]), QG, 16)
                p.wait(g, QG)

    def ytile(i):
        return (ub if i < 16 else ar)[:, (i % 16) * TT:(i % 16 + 1) * TT]

    with nc.psum_tensor("ps_r", [128, 2048], F32) as ps_r:
        for i in range(32):
            p.wait(g, EV)
            p.inc(g.dma_start(out=zb[:, 0:TT],
                              in_=gate_dram[i * 128:(i + 1) * 128, :]), QG, 16)
            p.wait(v, QG)
            p.inc(v.tensor_tensor(ytile(i), ytile(i), zb[:, 0:TT], ALU.mult),
                  EV, 1)
            p.wait(sc, EV)
            p.wait(sc, ET)
            p.inc(sc.activation(out=zb2[:, 0:TT], in_=ytile(i), func=AF.Square),
                  EA, 1)
            p.wait(te, EA)
            for tch in range(2):
                p.inc(te.matmul(ps_r[0:1, tch * 512:(tch + 1) * 512], ones_col[:],
                                zb2[:, tch * 512:(tch + 1) * 512],
                                start=(i == 0), stop=(i == 31)), ET, 1)
        p.wait(v, ET)
        p.inc(v.tensor_scalar_mul(wk[0:1, 0:TT], ps_r[0:1, 0:TT], 1.0 / INTER),
              EV, 1)
        p.wait(v, EV)
        p.inc(v.tensor_scalar_add(wk[0:1, 0:TT], wk[0:1, 0:TT], EPS), EV, 1)
        p.wait(v, EV)
        p.inc(v.reciprocal(wk[0:1, TT:2 * TT], wk[0:1, 0:TT]), EV, 1)
        p.wait(sc, EV)
        p.inc(sc.activation(out=wk[0:1, 0:TT], in_=wk[0:1, TT:2 * TT],
                            func=AF.Sqrt), EA, 1)
        p.wait(v, EA)
        p.inc(v.tensor_copy(zb[0:1, 0:TT], wk[0:1, 0:TT]), EV, 1)
        p.wait(te, EV)
        for tch in range(2):
            p.inc(te.matmul(ps_r[:, 1024 + tch * 512:1024 + (tch + 1) * 512],
                            ones_row[:], zb[0:1, tch * 512:(tch + 1) * 512],
                            start=True, stop=True), ET, 1)
        p.wait(v, ET)
        p.inc(v.tensor_copy(rsb, ps_r[:, 1024:2048]), EV, 1)
        for i in range(32):
            p.wait(v, EV)
            p.inc(v.scalar_tensor_tensor(ytile(i), ytile(i), nwr[:, i:i + 1],
                                         rsb, ALU.mult, ALU.mult), EV, 1)

    if "yn" in tap:
        p.wait(g, EV)
        for j in range(4):
            p.inc(g.dma_start(out=tap["yn"][j * 128:(j + 1) * 128, :],
                              in_=ub[:, j * TT:(j + 1) * TT]), QG, 16)
        p.wait(g, QG)
    if "sg" in tap:
        for j in range(16):
            p.inc(g.dma_start(out=zb[:, 0:TT], in_=gate_dram[j * 128:(j + 1) * 128, :]), QG, 16)
            p.wait(g, QG)
            p.inc(g.dma_start(out=tap["sg"][j * 128:(j + 1) * 128, :], in_=zb[:, 0:TT]), QG, 16)
            p.wait(g, QG)
    with (nc.psum_tensor("ps_o0", [128, 1024], F32) as o0,
          nc.psum_tensor("ps_o1", [128, 1024], F32) as o1,
          nc.psum_tensor("ps_o2", [128, 1024], F32) as o2,
          nc.psum_tensor("ps_o3", [128, 1024], F32) as o3):
        po4 = [o0, o1, o2, o3]
        for m2b in range(4):
            mc = slice(m2b * 512, (m2b + 1) * 512)
            for k2 in range(32):
                wbuf = 2048 + (k2 % 2) * 512
                p.wait(g, ET)
                p.inc(g.dma_start(out=cb[:, wbuf:wbuf + 512],
                                  in_=inp["outT"][k2 * 128:(k2 + 1) * 128, mc]),
                      QG, 16)
                p.wait(te, QG)
                if k2 == 0:
                    p.wait(te, EV)
                for m2 in range(4):
                    lhs = cb[:, wbuf + m2 * 128:wbuf + (m2 + 1) * 128]
                    for tch in range(2):
                        p.inc(te.matmul(
                            po4[m2][:, tch * 512:(tch + 1) * 512], lhs,
                            ytile(k2)[:, tch * 512:(tch + 1) * 512],
                            start=(k2 == 0), stop=(k2 == 31)), ET, 1)
            for m2 in range(4):
                d_i = m2b * 4 + m2
                p.wait(v, ET)
                p.inc(v.tensor_tensor(xr[:, d_i * TT:(d_i + 1) * TT],
                                      po4[m2][:, 0:TT],
                                      xr[:, d_i * TT:(d_i + 1) * TT], ALU.add),
                      EV, 1)

    if "ynpost" in tap:
        p.wait(g, ET)
        p.wait(g, EV)
        for j in range(8):
            p.inc(g.dma_start(out=tap["ynpost"][j * 128:(j + 1) * 128, :],
                              in_=ytile(j * 4)), QG, 16)
        p.wait(g, QG)
    if "x3" in tap:
        p.wait(sy, EV)
        for i in range(DT):
            p.inc(sy.dma_start(out=tap["x3"][i * 128:(i + 1) * 128, :],
                               in_=xr[:, i * TT:(i + 1) * TT]), QS, 16)
    if upto == "x3":
        finish()
        return nc

    ffn_block(first=False)

    finish()
    return nc


def _stage(inputs, core):
    b, half = core // 2, core % 2
    t0 = half * TT
    _ = inputs.get("L0_attn_b_mask")
    x = inputs["L0_attn_a_input"]
    d = {}
    d["xin"] = np.ascontiguousarray(np.asarray(x[b, t0:t0 + TT, :]).T)
    nonce = int(np.random.randint(1 << 20, 1 << 30))
    _stage.nonce = getattr(_stage, "nonce", nonce)
    d["meta"] = np.array([[half, 1 - half, _stage.nonce, 0, 0, 0, 0, 0]], np.int32)
    A = np.asarray
    d["wprj_cat"] = np.ascontiguousarray(np.concatenate(
        [A(inputs["L0_attn_a_W_dt_in"]).T,
         A(inputs["L0_attn_a_W_B"]).T,
         np.zeros((D, 16), np.float32),
         A(inputs["L0_attn_a_W_C"]).T], axis=1))
    d["wdt_outT_own"] = np.ascontiguousarray(
        A(inputs["L0_attn_a_W_dt_out"]).T[:, half * 1024:(half + 1) * 1024])
    d["bdt_own"] = np.ascontiguousarray(
        A(inputs["L0_attn_a_b_dt_out"]).reshape(16, 128).T[:, half * 8:half * 8 + 8])
    d["Da_own"] = np.ascontiguousarray(
        A(inputs["L0_attn_a_D"]).reshape(16, 128).T[:, half * 8:half * 8 + 8])
    d["alog_own"] = np.ascontiguousarray(
        A(inputs["L0_attn_a_A_log"]).reshape(16, 128, 16).transpose(1, 0, 2)
        .reshape(128, 256)[:, half * 128:(half + 1) * 128])
    d["inpT"] = np.ascontiguousarray(A(inputs["L0_attn_b_in_proj_w"]).T)
    cw = A(inputs["L0_attn_b_conv_w"]).reshape(34, 128, 4)
    cbv = A(inputs["L0_attn_b_conv_b"]).reshape(34, 128)
    own_cht = list(range(half * 16, half * 16 + 16)) + [32, 33]
    d["convw_own"] = np.ascontiguousarray(
        cw[own_cht].transpose(1, 0, 2).reshape(128, 72))
    d["convb_own"] = np.ascontiguousarray(cbv[own_cht].T)
    d["alogb_own"] = np.ascontiguousarray(
        A(inputs["L0_attn_b_A_log"]).reshape(NH, 1)[half * 32:half * 32 + 32])
    d["dtbias_own"] = np.ascontiguousarray(
        A(inputs["L0_attn_b_dt_bias"]).reshape(NH, 1)[half * 32:half * 32 + 32])
    d["Db_own"] = np.ascontiguousarray(
        np.repeat(A(inputs["L0_attn_b_D"]), PDIM).reshape(32, 128).T
        [:, half * 16:half * 16 + 16])
    d["normw_r"] = np.ascontiguousarray(
        A(inputs["L0_attn_b_norm_w"]).reshape(32, 128).T)
    d["outT"] = np.ascontiguousarray(A(inputs["L0_attn_b_out_proj_w"]).T)
    fcT = A(inputs["L0_ffn_w_fc"]).T
    pjT = A(inputs["L0_ffn_w_proj"]).T
    d["fcT_half"] = np.ascontiguousarray(fcT[half * (D // 2):(half + 1) * (D // 2)])
    d["projT_half"] = np.ascontiguousarray(
        pjT[half * (FFH // 2):(half + 1) * (FFH // 2)])
    return {k: np.asarray(v2) for k, v2 in d.items()}


def kernel(**inputs):
    _stage.nonce = int(np.random.randint(1 << 20, 1 << 30))
    nc = build_program()
    in_maps = [_stage(inputs, c) for c in range(8)]
    res = run_bass_kernel_spmd(nc, in_maps, core_ids=list(range(8)))
    out = np.zeros((B, S, D), np.float32)
    for c in range(8):
        b, half = c // 2, c % 2
        out[b, half * TT:(half + 1) * TT, :] = res.results[c]["xout"].T
    return out


# version 08 (final)
# speedup vs baseline: 1.4461x; 1.1726x over previous
import numpy as np
import concourse.bass as bass
import concourse.mybir as mybir
from concourse.bass_utils import run_bass_kernel_spmd

F32 = mybir.dt.float32
BF16 = mybir.dt.bfloat16
I32 = mybir.dt.int32
AF = mybir.ActivationFunctionType
ALU = mybir.AluOpType
AX = mybir.AxisListType

B, S, D = 4, 2048, 2048
TT = 1024
DT = 16
NA = 16
FFH = 5120
INTER = 4096
NH = 64
PDIM = 64
NB = 128
LCH = 256
NCH = 8
EPS = 1e-5
STAGES = ("prep", "attna", "ffn1", "attnb", "ffn2")


class P:

    def __init__(self, nc):
        self.nc = nc
        self.cnt = {}
        self.sems = {}

    def sem(self, name):
        if name not in self.sems:
            self.sems[name] = self.nc.semaphore(name).__enter__()
            self.cnt[name] = 0
        return self.sems[name]

    def inc(self, instr, name, amt):
        instr.then_inc(self.sem(name), amt)
        self.cnt[name] += amt

    def wait(self, eng, name):
        eng.wait_ge(self.sem(name), self.cnt[name])

    def wait_lag(self, eng, name, lag):
        eng.wait_ge(self.sem(name), max(0, self.cnt[name] - lag))


def build_program(upto="ffn2", taps=()):
    nc = bass.Bass("TRN2", target_bir_lowering=False)
    p = P(nc)
    g, v, sc, te, sy = nc.gpsimd, nc.vector, nc.scalar, nc.tensor, nc.sync

    inp = {}

    def din(name, shape, dt=F32):
        inp[name] = nc.dram_tensor(name, shape, dt, kind="ExternalInput").ap()
        return inp[name]

    din("xin", [D, TT])
    din("meta", [1, 8], I32)
    din("wprj_cat", [D, 112])
    din("wdt_outT_own", [64, 1024])
    din("bdt_own", [128, 8])
    din("Da_own", [128, 8])
    din("alog_own", [128, 128])
    din("inpT", [D, 8512])
    din("convw_own", [128, 18 * 4])
    din("convb_own", [128, 18])
    din("alogb_own", [32, 1])
    din("dtbias_own", [32, 1])
    din("Db_own", [128, 16])
    din("normw_r", [128, 32])
    din("outT", [INTER, D])
    din("fcT_half", [D // 2, FFH])
    din("projT_half", [FFH // 2, D])

    xout = nc.dram_tensor("xout", [D, TT], F32, kind="ExternalOutput").ap()
    tap = {}
    for tname, tshape in taps:
        tap[tname] = nc.dram_tensor("tap_" + tname, tshape, F32,
                                    kind="ExternalOutput").ap()

    shd = lambda n, s, dt: nc.dram_tensor(n, s, dt, kind="Internal",
                                          addr_space="Shared").ap()
    sh_q2fc = shd("sh_q2fc", [2, D // 2, FFH], BF16)
    sh_q2pj = shd("sh_q2pj", [2, FFH // 2, D], BF16)
    sh_u = shd("sh_u", [2, D, TT], BF16)
    sh_ya = shd("sh_ya", [2, 1024, S], BF16)
    sh_hbc = shd("sh_hbc", [2, 4416, TT], BF16)
    sh_yb = shd("sh_yb", [2, 2048, S], BF16)
    sh_flag = shd("sh_flag", [2, 16], I32)

    gate_dram = nc.dram_tensor("gate_dram", [INTER, TT], BF16, kind="Internal").ap()
    q2fc_loc = nc.dram_tensor("q2fc_loc", [D // 2, FFH], BF16, kind="Internal").ap()
    q2pj_loc = nc.dram_tensor("q2pj_loc", [FFH // 2, D], BF16, kind="Internal").ap()
    hbc_loc = nc.dram_tensor("hbc_loc", [4416, TT], BF16, kind="Internal").ap()
    yb_loc2 = nc.dram_tensor("yb_loc2", [2048, S], BF16, kind="Internal").ap()
    zconv_dram = nc.dram_tensor("zconv_dram", [2, 2048, TT], BF16, kind="Internal").ap()
    ytok_dram = nc.dram_tensor("ytok_dram", [2, 2048, TT], BF16, kind="Internal").ap()
    acum_dram = nc.dram_tensor("acum_dram", [NCH, 32, LCH], F32, kind="Internal").ap()
    h_dram = nc.dram_tensor("h_dram", [FFH, TT], BF16, kind="Internal").ap()

    sbufs = []

    def sbuf(name, shape, dt):
        t = nc.sbuf_tensor(name, shape, dt).__enter__()
        sbufs.append(t)
        return t

    mp = sbuf("mp", [1, 8], I32)
    identB = sbuf("identB", [128, 128], BF16)
    ident64 = sbuf("ident64", [64, 64], BF16)
    ones_col = sbuf("ones_col", [128, 1], BF16)
    ones_colF = sbuf("ones_colF", [128, 1], F32)
    ones_row = sbuf("ones_row", [1, 128], BF16)
    ones_rowF = sbuf("ones_rowF", [1, 128], F32)
    ph_sb = sbuf("ph_sb", [1, 16], I32)
    pf_sb = sbuf("pf_sb", [1, 16], I32)
    A_a = sbuf("A_a", [128, 128], F32)
    convw = sbuf("convw", [128, 72], F32)
    convb = sbuf("convb", [128, 18], F32)
    bdt = sbuf("bdt", [128, 8], F32)
    Da = sbuf("Da", [128, 8], F32)
    d64b = sbuf("d64b", [64, 2048], BF16)
    brawb = sbuf("brawb", [16, 2048], BF16)
    crawb = sbuf("crawb", [16, 2048], BF16)
    wpj = sbuf("wpj", [128, 1792], BF16)
    Ab_col = sbuf("Ab_col", [32, 1], F32)
    dtb_col = sbuf("dtb_col", [32, 1], F32)
    Dbr = sbuf("Dbr", [128, 16], F32)
    nwr = sbuf("nwr", [128, 32], F32)
    srow = sbuf("srow", [1, 64], F32)
    thr_c = sbuf("thr_c", [128, 2], F32)
    sc_ffn = sbuf("sc_ffn", [128, 1], F32)

    xr = sbuf("xr", [128, DT * TT], F32)
    ub = sbuf("ub", [128, DT * TT], BF16)
    cb = sbuf("cb", [128, 16384], BF16)
    ar = sbuf("ar", [128, 16384], BF16)
    ar2 = sbuf("ar2", [128, 4096], BF16)
    wouT = ar2[0:64, 3072:4096]
    wk = sbuf("wk", [128, 2048], F32)
    wk2 = sbuf("wk2", [128, 1024], F32)
    zb = sbuf("zb", [128, 2048], BF16)
    zb2 = sbuf("zb2", [128, 2048], BF16)
    rows = wk[0:1, :]
    mub = cb[:, 0:TT]
    rsb = cb[:, TT:2 * TT]
    st_a = wk2[:, 0:128]

    QS, QG, EV, EP, EA, ET, PO = "qs", "qg", "ev", "ep", "ea", "et", "po"

    p.inc(sy.dma_start(out=mp[:], in_=inp["meta"]), QS, 16)
    p.inc(sy.dma_start(out=bdt[:], in_=inp["bdt_own"]), QS, 16)
    p.inc(sy.dma_start(out=Da[:], in_=inp["Da_own"]), QS, 16)
    p.inc(sy.dma_start(out=wk[:, 0:128], in_=inp["alog_own"]), QS, 16)
    p.inc(sy.dma_start(out=convw[:], in_=inp["convw_own"]), QS, 16)
    p.inc(sy.dma_start(out=convb[:], in_=inp["convb_own"]), QS, 16)
    p.inc(sy.dma_start(out=wk[0:32, 256:257], in_=inp["alogb_own"]), QS, 16)
    p.inc(sy.dma_start(out=dtb_col[:], in_=inp["dtbias_own"]), QS, 16)
    p.inc(sy.dma_start(out=Dbr[:], in_=inp["Db_own"]), QS, 16)
    p.inc(sy.dma_start(out=nwr[:], in_=inp["normw_r"]), QS, 16)

    p.inc(g.memset(identB[:], 1.0), EP, 1)
    p.inc(g.memset(ident64[:], 1.0), EP, 1)
    p.wait(g, EP)
    p.inc(g.affine_select(identB[:], identB[:], [[1, 128]], ALU.is_equal, 0.0,
                          base=0, channel_multiplier=-1), EP, 1)
    p.inc(g.affine_select(ident64[:], ident64[:], [[1, 64]], ALU.is_equal,
                          0.0, base=0, channel_multiplier=-1), EP, 1)
    p.inc(v.memset(ph_sb[:].bitcast(F32), 0.0), EV, 1)
    p.inc(v.memset(ones_col[:], 1.0), EV, 1)
    p.inc(v.memset(ones_colF[:], 1.0), EV, 1)
    p.inc(v.memset(ones_row[:], 1.0), EV, 1)
    p.inc(v.memset(ones_rowF[:], 1.0), EV, 1)

    p.wait(sc, QS)
    p.inc(sc.activation(out=A_a[:], in_=wk[:, 0:128], func=AF.Exp), EA, 1)
    p.inc(sc.activation(out=wk2[0:32, 0:1], in_=wk[0:32, 256:257], func=AF.Exp),
          EA, 1)
    p.wait(v, EA)
    p.inc(v.tensor_scalar_mul(A_a[:], A_a[:], -1.0), EV, 1)
    p.inc(v.tensor_scalar_mul(Ab_col[:], wk2[0:32, 0:1], -1.0), EV, 1)

    p.wait(g, QS)
    r_me = g.alloc_register("r_me")
    r_peer = g.alloc_register("r_peer")
    r_phase = g.alloc_register("r_phase")
    r_want = g.alloc_register("r_want")
    r_poll = g.alloc_register("r_poll")
    r_fl = g.alloc_register("r_fl")
    g.reg_load(r_me, mp[0:1, 0:1])
    g.reg_load(r_peer, mp[0:1, 1:2])
    g.reg_load(r_phase, mp[0:1, 2:3])
    g.reg_load(r_want, mp[0:1, 2:3])
    g.reg_mov(r_poll, 16)
    me_v = g.snap(r_me, min_val=0, max_val=1)
    peer_v = g.snap(r_peer, min_val=0, max_val=1)

    spin_id = [0]
    n_sends = [0]
    flush_id = [0]

    def allwait():
        for e in (v, sc, te, g):
            for snm in (EV, EA, ET, EP, QG):
                p.wait(e, snm)

    def bb_flush():
        flush_id[0] += 1
        nm = f"flush{flush_id[0]}"
        g.br(nm)
        nc.switch_bb(nm)

    def flag_send():
        n_sends[0] += 1
        g.reg_add(r_phase, r_phase, 1)
        g.reg_save(ph_sb[0:1, 0:1], r_phase)
        p.inc(g.dma_start(out=sh_flag[bass.ds(me_v, 1), :], in_=ph_sb[:]), QG, 16)
        return n_sends[0]

    def flag_wait(expect=None):
        k = n_sends[0] if expect is None else expect
        g.reg_load(r_want, mp[0:1, 2:3])
        g.reg_add(r_want, r_want, k)
        sid = spin_id[0]
        spin_id[0] += 1
        lname, cname, ename = f"spin{sid}", f"spin{sid}_c", f"spin{sid}_end"
        g.br(lname)
        with nc.body(lname, valid_engines=None):
            g.dma_start(out=pf_sb[:], in_=sh_flag[bass.ds(peer_v, 1), :]).then_inc(
                p.sem(PO), 16)
            g.wait_ge(p.sem(PO), r_poll)
            g.reg_add(r_poll, r_poll, 16)
            g.reg_load(r_fl, pf_sb[0:1, 0:1])
            g.br_lt(r_fl, r_want, on_true=lname, on_false=cname)
        with nc.body(cname, valid_engines=None):
            g.reg_add(r_fl, r_fl, -99)
            g.br_lt(r_fl, r_want, on_true=ename, on_false=lname)
        nc.switch_bb(ename)

    def finish():
        p.wait(sy, EV)
        p.wait(sy, EA)
        p.wait(sy, EP)
        p.wait(sy, ET)
        for i in range(DT):
            p.inc(sy.dma_start(out=xout[i * 128:(i + 1) * 128, :],
                               in_=xr[:, i * TT:(i + 1) * TT]), QS, 16)
        p.wait(sy, QS)
        p.wait(g, QG)

    def tap_out(name, src_f32_fn, ntiles, tcols):
        if name not in tap:
            return
        for i in range(ntiles):
            p.wait(sy, EV)
            p.wait(sy, EA)
            p.wait(sy, ET)
            src = src_f32_fn(i)
            p.inc(sy.dma_start(out=tap[name][i * 128:(i + 1) * 128, :], in_=src),
                  QS, 16)
        p.wait(sy, QS)

    with nc.psum_tensor("ps_tz", [128, 512], F32) as ps:
        specs = [(inp["fcT_half"], sh_q2fc, D, FFH, 0),
                 (inp["projT_half"], sh_q2pj, FFH, D, 1)]
        p.wait(v, EV)
        p.inc(v.memset(wk2[:, 0:4], 0.0), EV, 1)
        chunks = []
        for wsrc, _, rn, cn, wi in specs:
            half = rn // 2
            for i in range(half // 128):
                for cc in range((cn + 2047) // 2048):
                    c0, c1 = cc * 2048, min(cn, (cc + 1) * 2048)
                    chunks.append((wsrc, i, c0, c1, wi))
        prev_incs = 0
        for k, (wsrc, i, c0, c1, wi) in enumerate(chunks):
            w_ = c1 - c0
            nsub = (w_ + 511) // 512
            buf = xr[:, (k % 2) * 2048:(k % 2) * 2048 + w_]
            p.wait_lag(g, EV, prev_incs if k > 0 else 0)
            p.inc(g.dma_start(out=buf, in_=wsrc[i * 128:(i + 1) * 128, c0:c1]),
                  QG, 16)
            p.wait(v, QG)
            for s8 in range(nsub):
                a, b = s8 * 512, min(w_, (s8 + 1) * 512)
                p.inc(v.tensor_reduce(wk2[:, 4 + s8:5 + s8], buf[:, a:b],
                                      AX.X, ALU.add,
                                      apply_absolute_value=True), EV, 1)
            p.wait(v, EV)
            p.inc(v.tensor_reduce(wk2[:, 2:3], wk2[:, 4:4 + nsub], AX.X,
                                  ALU.add), EV, 1)
            p.wait(v, EV)
            p.inc(v.tensor_tensor(wk2[:, wi:wi + 1], wk2[:, wi:wi + 1],
                                  wk2[:, 2:3], ALU.add), EV, 1)
            p.wait(v, EV)
            prev_incs = nsub + 2
        p.wait(te, EV)
        p.inc(te.matmul(ps[0:1, 0:2], ones_colF[:], wk2[:, 0:2], start=True,
                        stop=True), ET, 1)
        p.wait(v, ET)
        p.inc(v.tensor_copy(srow[0:1, 0:2], ps[0:1, 0:2]), EV, 1)
        p.wait(v, EV)
        p.inc(v.tensor_copy(ph_sb[0:1, 4:6].bitcast(F32), srow[0:1, 0:2]), EV, 1)
        p.wait(g, EV)
        flag_send()
        flag_wait()
        p.inc(g.engine_nop(), EP, 1)
        p.wait(v, EP)
        p.inc(v.tensor_tensor(srow[0:1, 4:6], srow[0:1, 0:2],
                              pf_sb[0:1, 4:6].bitcast(F32), ALU.add), EV, 1)
        p.wait(v, EV)
        p.inc(v.tensor_scalar_mul(srow[0:1, 6:7], srow[0:1, 4:5], 1.0 / (D * FFH)),
              EV, 1)
        p.inc(v.tensor_scalar_mul(srow[0:1, 7:8], srow[0:1, 5:6], 1.0 / (D * FFH)),
              EV, 1)
        p.wait(v, EV)
        p.inc(v.tensor_scalar_max(srow[0:1, 6:8], srow[0:1, 6:8], 1e-5), EV, 1)
        if "scal" in tap:
            p.wait(g, EV)
            p.inc(g.dma_start(out=tap["scal"][0:1, 0:8], in_=srow[0:1, 0:8]), QG, 16)
            p.inc(g.dma_start(out=tap["scal"][0:1, 8:24], in_=pf_sb[:].bitcast(F32)), QG, 16)
            p.inc(g.dma_start(out=tap["scal"][0:1, 24:32], in_=ph_sb[0:1, 0:8].bitcast(F32)), QG, 16)
            p.wait(g, QG)
        p.wait(v, EV)
        p.inc(v.tensor_scalar_mul(srow[0:1, 8:10], srow[0:1, 6:8], 0.7), EV, 1)
        p.wait(v, EV)
        p.inc(v.tensor_tensor(srow[0:1, 10:11], srow[0:1, 6:7], srow[0:1, 6:7],
                              ALU.mult), EV, 1)
        p.wait(v, EV)
        p.inc(v.tensor_tensor(srow[0:1, 10:11], srow[0:1, 10:11], srow[0:1, 7:8],
                              ALU.mult), EV, 1)
        p.wait(v, EV)
        p.inc(v.tensor_scalar_mul(srow[0:1, 10:11], srow[0:1, 10:11], 0.125), EV, 1)
        p.wait(te, EV)
        p.inc(te.matmul(ps[:, 0:4], ones_rowF[:], srow[0:1, 8:12], start=True,
                        stop=True), ET, 1)
        p.wait(v, ET)
        p.inc(v.tensor_copy(sc_ffn[:], ps[:, 2:3]), EV, 1)

        bounce = {0: q2fc_loc, 1: q2pj_loc}
        kk = 0
        for wsrc, wdst, rn, cn, wi in specs:
            p.wait(v, ET)
            p.wait(v, EA)
            p.inc(v.tensor_copy(thr_c[:, 0:1], ps[:, wi:wi + 1]), EV, 1)
            p.wait(v, EV)
            p.inc(v.tensor_scalar_mul(thr_c[:, 1:2], thr_c[:, 0:1], -1.0), EV, 1)
            half = rn // 2
            for i in range(half // 128):
                nch_c = (cn + 2047) // 2048
                for cc in range(nch_c):
                    c0, c1 = cc * 2048, min(cn, (cc + 1) * 2048)
                    w_ = c1 - c0
                    buf = xr[:, (kk % 2) * 2048:(kk % 2) * 2048 + w_]
                    p.wait_lag(g, EA, 2 if kk > 0 else 0)
                    p.inc(g.dma_start(out=buf,
                                      in_=wsrc[i * 128:(i + 1) * 128, c0:c1]),
                          QG, 16)
                    p.wait(sc, QG)
                    p.wait(sc, EV)
                    p.inc(sc.activation(out=zb[:, 0:w_], in_=buf,
                                        func=AF.Sign, bias=thr_c[:, 0:1]), EA, 1)
                    p.inc(sc.activation(out=zb2[:, 0:w_], in_=buf,
                                        func=AF.Sign, bias=thr_c[:, 1:2]), EA, 1)
                    p.wait(v, EA)
                    p.inc(v.tensor_tensor(zb[:, 0:w_], zb[:, 0:w_], zb2[:, 0:w_],
                                          ALU.add), EV, 1)
                    p.wait(g, EV)
                    p.inc(g.dma_start(out=bounce[wi][i * 128:(i + 1) * 128, c0:c1],
                                      in_=zb[:, 0:w_]), QG, 16)
                    kk += 1
        p.wait(g, QG)
        p.inc(g.dma_start(out=sh_q2fc[bass.ds(me_v, 1), :, :], in_=q2fc_loc),
              QG, 16)
        p.inc(g.dma_start(out=sh_q2pj[bass.ds(me_v, 1), :, :], in_=q2pj_loc),
              QG, 16)
        p.wait(g, QG)
        flag_send()

    if upto == "tern":
        if "q2fc" in tap:
            p.wait(sy, QG)
            flag_wait()
            for cc in range(2):
                p.wait(sy, EV)
                p.inc(sy.dma_start(out=zb[:, 0:2048],
                                   in_=sh_q2fc[0, 0:128, cc * 2048:(cc + 1) * 2048]),
                      QS, 16)
                p.wait(v, QS)
                p.inc(v.tensor_copy(wk[:, 0:2048], zb[:, 0:2048]), EV, 1)
                p.wait(sy, EV)
                p.inc(sy.dma_start(out=tap["q2fc"][:, cc * 2048:(cc + 1) * 2048],
                                   in_=wk[:, 0:2048]), QS, 16)
        if "scffn" in tap:
            p.wait(sy, EV)
            p.inc(sy.dma_start(out=tap["scffn"], in_=sc_ffn[:]), QS, 16)
        finish()
        return nc

    _ln_n = [0]

    def ln_to_u(x_tile_fn, u_arena, send_shared):
        _ln_n[0] += 1
        with nc.psum_tensor("ps_ln%d" % _ln_n[0], [128, 2048], F32) as ps_ln:
            p.wait(v, QG)
            for i in range(DT):
                xi = x_tile_fn(i)
                p.inc(v.tensor_copy(u_arena[:, i * TT:(i + 1) * TT], xi), EV, 1)
                p.wait(sc, ET)
                p.wait(sc, EV)
                p.inc(sc.activation(out=zb[:, 0:TT], in_=xi,
                                    func=AF.Square), EA, 1)
                p.wait(te, EV)
                p.wait(te, EA)
                sl = u_arena[:, i * TT:(i + 1) * TT]
                for tch in range(2):
                    c = slice(tch * 512, (tch + 1) * 512)
                    p.inc(te.matmul(ps_ln[0:1, tch * 512:(tch + 1) * 512],
                                    ones_col[:], sl[:, c], start=(i == 0),
                                    stop=(i == DT - 1)), ET, 1)
                    p.inc(te.matmul(ps_ln[0:1, 1024 + tch * 512:1536 + tch * 512],
                                    ones_col[:], zb[:, 0:TT][:, c],
                                    start=(i == 0), stop=(i == DT - 1)), ET, 1)
            p.wait(v, ET)
            p.inc(v.tensor_scalar_mul(rows[0:1, 0:TT], ps_ln[0:1, 0:TT], 1.0 / D),
                  EV, 1)
            p.inc(v.tensor_scalar_mul(rows[0:1, TT:2 * TT],
                                      ps_ln[0:1, 1024:1024 + TT], 1.0 / D), EV, 1)
            p.wait(v, EV)
            p.inc(v.tensor_tensor(wk2[0:1, 0:TT], rows[0:1, 0:TT],
                                  rows[0:1, 0:TT], ALU.mult), EV, 1)
            p.wait(v, EV)
            p.inc(v.tensor_tensor(rows[0:1, TT:2 * TT], rows[0:1, TT:2 * TT],
                                  wk2[0:1, 0:TT], ALU.subtract), EV, 1)
            p.wait(v, EV)
            p.inc(v.tensor_scalar_add(rows[0:1, TT:2 * TT], rows[0:1, TT:2 * TT],
                                      EPS), EV, 1)
            p.wait(v, EV)
            p.inc(v.reciprocal(wk2[0:1, 0:TT], rows[0:1, TT:2 * TT]), EV, 1)
            p.wait(sc, EV)
            p.inc(sc.activation(out=rows[0:1, TT:2 * TT], in_=wk2[0:1, 0:TT],
                                func=AF.Sqrt), EA, 1)
            p.wait(v, EA)
            p.inc(v.tensor_copy(zb[0:1, 0:TT], rows[0:1, 0:TT]), EV, 1)
            p.inc(v.tensor_copy(zb[0:1, TT:2 * TT], rows[0:1, TT:2 * TT]),
                  EV, 1)
            p.wait(te, EV)
            for tch in range(2):
                p.inc(te.matmul(ps_ln[:, tch * 512:(tch + 1) * 512], ones_row[:],
                                zb[0:1, tch * 512:(tch + 1) * 512], start=True,
                                stop=True), ET, 1)
                p.inc(te.matmul(ps_ln[:, 1024 + tch * 512:1536 + tch * 512],
                                ones_row[:],
                                zb[0:1, TT + tch * 512:TT + (tch + 1) * 512],
                                start=True, stop=True), ET, 1)
            p.wait(v, ET)
            p.inc(v.tensor_copy(mub[:], ps_ln[:, 0:TT]), EV, 1)
            p.inc(v.tensor_copy(rsb[:], ps_ln[:, 1024:1024 + TT]), EV, 1)
            for i in range(DT):
                sl = u_arena[:, i * TT:(i + 1) * TT]
                p.wait(v, EV)
                p.inc(v.tensor_tensor(zb2[:, 0:TT], sl, mub[:],
                                      ALU.subtract), EV, 1)
                p.wait(v, EV)
                p.inc(v.tensor_tensor(sl, zb2[:, 0:TT], rsb[:],
                                      ALU.mult), EV, 1)


    def xin_tile(i):
        p.wait(g, EV)
        p.wait(g, EA)
        slot = (i % 2) * TT
        p.inc(g.dma_start(out=wk[:, slot:slot + TT],
                          in_=inp["xin"][i * 128:(i + 1) * 128, :]), QG, 16)
        p.wait(v, QG)
        p.wait(sc, QG)
        return wk[:, slot:slot + TT]

    ln_to_u(xin_tile, ub, None)
    p.wait(g, EV)
    u_dst = sh_u[bass.ds(me_v, 1), :, :].rearrange("a (i q) t -> (a q) i t", q=128)
    p.inc(g.dma_start(out=u_dst, in_=ub[:, :].rearrange("q (i t) -> q i t", i=DT)),
          QG, 16)
    p.wait(g, QG)
    flag_send()
    flag_wait()

    if "u" in tap:
        p.wait(g, EV)
        for i in range(DT):
            p.inc(g.dma_start(out=tap["u"][i * 128:(i + 1) * 128, :],
                              in_=ub[:, i * TT:(i + 1) * TT]), QG, 16)
        p.wait(g, QG)

    if upto == "u":
        finish()
        return nc


    p.inc(g.dma_start(out=wpj[:, 0:DT * 112].rearrange("q (k r) -> q k r", k=DT),
                      in_=inp["wprj_cat"].rearrange("(k q) r -> q k r", q=128)),
          QG, 16)
    p.inc(g.dma_start(out=wouT[:], in_=inp["wdt_outT_own"]), QG, 16)

    with nc.psum_tensor("ps_pr", [128, 2048], F32) as ps_pr:
        for th in range(2):
            for i in range(DT):
                sl = (i % 2) * TT
                p.wait(g, ET)
                p.inc(g.dma_start(out=zb[:, sl:sl + TT],
                                  in_=sh_u[th, i * 128:(i + 1) * 128, :]), QG, 16)
                p.wait(te, QG)
                first, last = (i == 0), (i == DT - 1)
                for tc in range(2):
                    cc = slice(sl + tc * 512, sl + (tc + 1) * 512)
                    oc = slice(th * 1024 + tc * 512, th * 1024 + (tc + 1) * 512)
                    p.inc(te.matmul(ps_pr[0:112, oc],
                                    wpj[:, i * 112:(i + 1) * 112], zb[:, cc],
                                    start=first, stop=last), ET, 1)
        p.wait(v, ET)
        p.inc(v.tensor_copy(d64b[:], ps_pr[0:64, :]), EV, 1)
        p.inc(v.tensor_copy(brawb[:], ps_pr[64:80, :]), EV, 1)
        p.inc(v.tensor_copy(crawb[:], ps_pr[96:112, :]), EV, 1)
        p.wait(te, EV)

        for ht in range(8):
            p.wait(te, EV)
            p.wait(te, EA)
            for tc in range(4):
                cc = slice(tc * 512, (tc + 1) * 512)
                p.inc(te.matmul(ps_pr[:, cc], wouT[:, ht * 128:(ht + 1) * 128],
                                d64b[:, cc], start=True, stop=True), ET, 1)
            for th2 in range(2):
                hc = slice(th2 * TT, (th2 + 1) * TT)
                p.wait(sc, ET)
                p.wait(sc, EV)
                p.inc(sc.activation(out=wk[:, 0:TT], in_=ps_pr[:, hc], func=AF.Abs,
                                    bias=bdt[:, ht:ht + 1]), EA, 1)
                p.wait(sc, EA)
                p.inc(sc.activation(out=wk[:, TT:2 * TT], in_=wk[:, 0:TT],
                                    func=AF.Exp, scale=-1.0), EA, 1)
                p.wait(sc, EA)
                p.inc(sc.activation(out=wk[:, 0:TT], in_=wk[:, TT:2 * TT],
                                    func=AF.Ln, bias=1.0), EA, 1)
                p.wait(v, EA)
                p.inc(v.tensor_scalar(wk2[:, 0:TT], ps_pr[:, hc],
                                      bdt[:, ht:ht + 1], 0.0, ALU.add, ALU.max),
                      EV, 1)
                p.wait(v, EV)
                p.inc(v.tensor_tensor(cb[:, ht * 2048 + th2 * TT:
                                         ht * 2048 + (th2 + 1) * TT],
                                      wk2[:, 0:TT], wk[:, 0:TT], ALU.add), EV, 1)

    if upto == "dt":
        if "dt" in tap:
            p.wait(g, EV)
            for ht in range(8):
                p.inc(g.dma_start(out=tap["dt"][ht * 128:(ht + 1) * 128, :],
                                  in_=cb[:, ht * 2048:(ht + 1) * 2048]), QG, 16)
        if "braw" in tap:
            p.wait(g, EV)
            p.inc(g.dma_start(out=tap["braw"][0:16, :], in_=brawb[:]), QG, 16)
            p.inc(g.dma_start(out=tap["braw"][16:32, :], in_=crawb[:]), QG, 16)
        p.wait(g, QG)
        finish()
        return nc

    A_own = A_a
    ubb = xr[:, :].bitcast(BF16)
    with (nc.psum_tensor("ps_y", [128, 1024], F32) as ps_y,
          nc.psum_tensor("ps_w", [128, 1024], F32) as ps_w):
        for th in range(2):
            p.wait(g, EV)
            p.wait(g, EA)
            src = sh_u[th, :, :][bass.ds(me_v * 1024, 1024), :].rearrange(
                "(i q) t -> q i t", q=128)
            p.inc(g.dma_start(out=ar[:, th * 8 * TT:(th + 1) * 8 * TT].rearrange(
                "q (i t) -> q i t", i=8), in_=src), QG, 16)
            for n in range(NA):
                for tc in range(2):
                    cc = slice(th * TT + tc * 512, th * TT + (tc + 1) * 512)
                    p.wait(te, EA)
                    p.wait(te, EV)
                    p.inc(te.matmul(ps_w[:, tc * 512:(tc + 1) * 512],
                                    ident64[0:16, n:n + 1].broadcast_to([16, 128]),
                                    brawb[:, cc], start=True, stop=True), ET, 1)
                    p.inc(te.matmul(ps_w[:, 512 + tc * 512:512 + (tc + 1) * 512]
                                    if False else ps_y[:, tc * 512:(tc + 1) * 512],
                                    ident64[0:16, n:n + 1].broadcast_to([16, 128]),
                                    crawb[:, cc], start=True, stop=True), ET, 1)
                p.wait(sc, ET)
                p.inc(sc.copy(ubb[:, n * TT:(n + 1) * TT], ps_w[:, 0:TT]), EA, 1)
                p.inc(sc.copy(ubb[:, 16384 + n * TT:16384 + (n + 1) * TT],
                              ps_y[:, 0:TT]), EA, 1)
            p.wait(g, QG)
            for ht in range(8):
                dtsl = cb[:, ht * 2048 + th * TT:ht * 2048 + (th + 1) * TT]
                ulsl = ar[:, th * 8 * TT + ht * TT:th * 8 * TT + (ht + 1) * TT]
                p.wait(v, QG)
                p.wait(v, EA)
                p.inc(v.tensor_tensor(zb[:, 0:TT], dtsl, ulsl, ALU.mult), EV, 1)
                for n in range(NA):
                    col = ht * 16 + n
                    p.wait(sc, EV)
                    p.wait(sc, ET)
                    p.inc(sc.activation(out=wk[:, 0:TT], in_=dtsl, func=AF.Exp,
                                        scale=A_own[:, col:col + 1]), EA, 1)
                    p.wait(v, EA)
                    p.inc(v.tensor_tensor(zb[:, TT:2 * TT], zb[:, 0:TT],
                                          ubb[:, n * TT:(n + 1) * TT], ALU.mult),
                          EV, 1)
                    p.wait(v, EV)
                    if th == 0:
                        p.inc(v.tensor_tensor_scan(zb2[:, TT:2 * TT],
                                                   wk[:, 0:TT], zb[:, TT:2 * TT],
                                                   0.0, ALU.mult, ALU.add), EV, 1)
                    else:
                        p.inc(v.tensor_tensor_scan(zb2[:, TT:2 * TT],
                                                   wk[:, 0:TT], zb[:, TT:2 * TT],
                                                   st_a[:, col:col + 1],
                                                   ALU.mult, ALU.add), EV, 1)
                    p.wait(v, EV)
                    if th == 0:
                        p.inc(v.tensor_copy(st_a[:, col:col + 1],
                                            zb2[:, 2 * TT - 1:2 * TT]), EV, 1)
                    p.inc(v.tensor_tensor(zb2[:, 0:TT], zb2[:, TT:2 * TT],
                                          ubb[:, 16384 + n * TT:(n + 1) * TT
                                              + 16384], ALU.mult), EV, 1)
                    p.wait(te, EV)
                    for tc in range(2):
                        cc = slice(tc * 512, (tc + 1) * 512)
                        p.inc(te.matmul(ps_y[:, cc], identB[:], zb2[:, 0:TT][:, cc],
                                        start=(n == 0), stop=(n == NA - 1)), ET, 1)
                p.wait(v, ET)
                p.inc(v.scalar_tensor_tensor(
                    ub[:, ht * 2048 + th * TT:ht * 2048 + (th + 1) * TT], ulsl,
                    Da[:, ht:ht + 1], ps_y[:], ALU.mult, ALU.add), EV, 1)

    p.wait(g, EV)
    ya_dst = sh_ya[bass.ds(me_v, 1), :, :].rearrange("a (i q) t -> (a q) i t", q=128)
    p.inc(g.dma_start(out=ya_dst, in_=ub[:, :].rearrange("q (i t) -> q i t", i=8)),
          QG, 16)
    p.wait(g, QG)
    flag_send()
    flag_wait()

    for sl_i in range(2):
        src = sh_ya[sl_i, :, :][:, bass.ds(me_v * 1024, 1024)].rearrange(
            "(i q) t -> q i t", q=128)
        p.inc(g.dma_start(out=ar[:, sl_i * 8 * TT:(sl_i + 1) * 8 * TT].rearrange(
            "q (i t) -> q i t", i=8), in_=src), QG, 16)
    for i in range(DT):
        sl = (i % 2) * TT
        p.wait(sy, EV)
        p.wait(sy, EA)
        p.inc(sy.dma_start(out=wk[:, sl:sl + TT],
                           in_=inp["xin"][i * 128:(i + 1) * 128, :]), QS, 16)
        p.wait(v, QS)
        p.wait(v, QG)
        p.inc(v.tensor_tensor(xr[:, i * TT:(i + 1) * TT], wk[:, sl:sl + TT],
                              ar[:, i * TT:(i + 1) * TT], ALU.add), EV, 1)

    if "x1" in tap:
        p.wait(sy, EV)
        for i in range(DT):
            p.inc(sy.dma_start(out=tap["x1"][i * 128:(i + 1) * 128, :],
                               in_=xr[:, i * TT:(i + 1) * TT]), QS, 16)
    if upto == "x1":
        finish()
        return nc

    def ffn_block(first):
        def xr_tile(i):
            return xr[:, i * TT:(i + 1) * TT]
        ln_to_u(xr_tile, ub, None)
        if first:
            flag_wait(expect=2)
        sfx = "a" if first else "b"
        with (nc.psum_tensor("ps_f0" + sfx, [128, 1024], F32) as f0,
              nc.psum_tensor("ps_f1" + sfx, [128, 1024], F32) as f1,
              nc.psum_tensor("ps_f2" + sfx, [128, 1024], F32) as f2,
              nc.psum_tensor("ps_f3" + sfx, [128, 1024], F32) as f3):
            ps4 = [f0, f1, f2, f3]
            for mb in range(10):
                buf = (mb % 2) * 8192
                mc = slice(mb * 512, (mb + 1) * 512)
                p.wait_lag(g, ET, 128 if mb > 0 else 0)
                for sl_i in range(2):
                    src = sh_q2fc[sl_i, :, mc].rearrange("(k q) m -> q k m", q=128)
                    p.inc(g.dma_start(
                        out=ar[:, buf + sl_i * 4096:buf + (sl_i + 1) * 4096]
                        .rearrange("q (k m) -> q k m", k=8), in_=src), QG, 16)
                p.wait(te, QG)
                p.wait(te, EV)
                p.wait(te, EA)
                for m in range(4):
                    for k in range(DT):
                        lhs = ar[:, buf + k * 512 + m * 128:
                                 buf + k * 512 + (m + 1) * 128]
                        for tch in range(2):
                            p.inc(te.matmul(
                                ps4[m][:, tch * 512:(tch + 1) * 512], lhs,
                                ub[:, k * TT + tch * 512:k * TT + (tch + 1) * 512],
                                start=(k == 0), stop=(k == DT - 1)), ET, 1)
                for m in range(4):
                    p.wait(v, ET)
                    p.wait(v, QG)
                    p.wait(v, EA)
                    p.inc(v.tensor_scalar_max(wk[:, 0:TT], ps4[m][:, 0:TT], 0.0),
                          EV, 1)
                    p.wait(sc, EV)
                    p.inc(sc.activation(out=zb[:, TT:2 * TT], in_=wk[:, 0:TT],
                                        func=AF.Square), EA, 1)
                    p.wait(g, EA)
                    p.inc(g.dma_start(out=h_dram[(mb * 4 + m) * 128:
                                                 (mb * 4 + m + 1) * 128, :],
                                      in_=zb[:, TT:2 * TT]), QG, 16)
        with (nc.psum_tensor("ps_p0" + sfx, [128, 1024], F32) as p0,
              nc.psum_tensor("ps_p1" + sfx, [128, 1024], F32) as p1,
              nc.psum_tensor("ps_p2" + sfx, [128, 1024], F32) as p2,
              nc.psum_tensor("ps_p3" + sfx, [128, 1024], F32) as p3):
            pp = [p0, p1, p2, p3]
            p.wait(g, QG)
            for m2b in range(4):
                mc = slice(m2b * 512, (m2b + 1) * 512)
                for k2 in range(40):
                    wbuf = (k2 % 2) * 1024
                    hbuf = 2048 + (k2 % 2) * 1024
                    sl_i, row = k2 // 20, (k2 % 20) * 128
                    p.wait_lag(g, ET, 8)
                    p.inc(g.dma_start(out=ar[:, wbuf:wbuf + 512],
                                      in_=sh_q2pj[sl_i, row:row + 128, mc]),
                          QG, 16)
                    p.inc(g.dma_start(out=ar[:, hbuf:hbuf + 1024],
                                      in_=h_dram[k2 * 128:(k2 + 1) * 128, :]),
                          QG, 16)
                    p.wait(te, QG)
                    if k2 == 0:
                        p.wait(te, EV)
                    for m2 in range(4):
                        lhs = ar[:, wbuf + m2 * 128:wbuf + (m2 + 1) * 128]
                        for tch in range(2):
                            p.inc(te.matmul(
                                pp[m2][:, tch * 512:(tch + 1) * 512], lhs,
                                ar[:, hbuf + tch * 512:hbuf + (tch + 1) * 512],
                                start=(k2 == 0), stop=(k2 == 39)), ET, 1)
                for m2 in range(4):
                    d_i = m2b * 4 + m2
                    p.wait(v, ET)
                    p.inc(v.scalar_tensor_tensor(
                        xr[:, d_i * TT:(d_i + 1) * TT], pp[m2][:, 0:TT],
                        sc_ffn[:, 0:1], xr[:, d_i * TT:(d_i + 1) * TT],
                        ALU.mult, ALU.add), EV, 1)

    ffn_block(first=True)
    if "rows2" in tap:
        p.wait(sy, EV)
        p.wait(sy, EA)
        p.inc(sy.dma_start(out=tap["rows2"], in_=wk[0:1, :]), QS, 16)
    if "u2" in tap:
        p.wait(g, EV)
        for i in range(DT):
            p.inc(g.dma_start(out=tap["u2"][i * 128:(i + 1) * 128, :],
                              in_=ub[:, i * TT:(i + 1) * TT]), QG, 16)
    if "hff" in tap:
        p.wait(g, QG)
        for i in range(8):
            p.inc(g.dma_start(out=wk[:, 0:TT], in_=h_dram[i * 128:(i + 1) * 128, :]), QG, 16)
            p.wait(g, QG)
            p.inc(g.dma_start(out=tap["hff"][i * 128:(i + 1) * 128, :], in_=wk[:, 0:TT]), QG, 16)
            p.wait(g, QG)
    if "x2" in tap:
        p.wait(sy, EV)
        for i in range(DT):
            p.inc(sy.dma_start(out=tap["x2"][i * 128:(i + 1) * 128, :],
                               in_=xr[:, i * TT:(i + 1) * TT]), QS, 16)
    if upto == "x2":
        finish()
        return nc

    def xr_tile2(i):
        return xr[:, i * TT:(i + 1) * TT]

    ln_to_u(xr_tile2, ub, None)
    if "u3d" in tap:
        p.wait(g, EV)
        for i in range(DT):
            p.inc(g.dma_start(out=tap["u3d"][i * 128:(i + 1) * 128, :],
                              in_=ub[:, i * TT:(i + 1) * TT]), QG, 16)
        p.wait(g, QG)

    with (nc.psum_tensor("ps_i0", [128, 1024], F32) as i0,
          nc.psum_tensor("ps_i1", [128, 1024], F32) as i1,
          nc.psum_tensor("ps_i2", [128, 1024], F32) as i2,
          nc.psum_tensor("ps_i3", [128, 1024], F32) as i3):
        ps4b = [i0, i1, i2, i3]
        for mb in range(17):
            m0 = mb * 512
            mw = min(8512, (mb + 1) * 512) - m0
            buf = (mb % 2) * 8192
            p.wait(g, ET)
            p.inc(g.dma_start(
                out=ar[:, buf:buf + 16 * mw].rearrange("q (k m) -> q k m", k=DT),
                in_=inp["inpT"][:, m0:m0 + mw].rearrange("(k q) m -> q k m", q=128)),
                QG, 16)
            p.wait(te, QG)
            p.wait(te, EV)
            p.wait(te, EA)
            ntile = (mw + 127) // 128
            for m in range(ntile):
                mrows = min(mw - m * 128, 128)
                for k in range(DT):
                    lhs = ar[:, buf + k * mw + m * 128:buf + k * mw + m * 128 + mrows]
                    for tch in range(2):
                        p.inc(te.matmul(
                            ps4b[m][0:mrows, tch * 512:(tch + 1) * 512], lhs,
                            ub[:, k * TT + tch * 512:k * TT + (tch + 1) * 512],
                            start=(k == 0), stop=(k == DT - 1)), ET, 1)
            for m in range(ntile):
                mt = mb * 4 + m
                mrows = min(mw - m * 128, 128)
                if mt < 32:
                    p.wait(sc, ET)
                    p.wait(sc, QG)
                    p.inc(sc.activation(out=zb[:, 0:TT], in_=ps4b[m][:, 0:TT],
                                        func=AF.Silu), EA, 1)
                    p.wait(g, EA)
                    p.inc(g.dma_start(out=gate_dram[mt * 128:(mt + 1) * 128, :],
                                      in_=zb[:, 0:TT]), QG, 16)
                else:
                    row = (mt - 32) * 128
                    p.wait(v, ET)
                    p.wait(v, QG)
                    p.inc(v.tensor_copy(zb2[0:mrows, 0:TT], ps4b[m][0:mrows, 0:TT]),
                          EV, 1)
                    p.wait(g, EV)
                    p.inc(g.dma_start(out=hbc_loc[row:row + mrows, :],
                                      in_=zb2[0:mrows, 0:TT]), QG, 16)
    if "sg2" in tap:
        p.wait(g, QG)
        for j in range(16):
            p.inc(g.dma_start(out=zb[:, 0:TT], in_=gate_dram[j * 128:(j + 1) * 128, :]), QG, 16)
            p.wait(g, QG)
            p.inc(g.dma_start(out=tap["sg2"][j * 128:(j + 1) * 128, :], in_=zb[:, 0:TT]), QG, 16)
            p.wait(g, QG)
    if "hbcl" in tap:
        p.wait(g, QG)
        for j in range(6):
            p.inc(g.dma_start(out=zb[:, 0:TT], in_=hbc_loc[j * 128:(j + 1) * 128, :]), QG, 16)
            p.wait(g, QG)
            p.inc(g.dma_start(out=tap["hbcl"][j * 128:(j + 1) * 128, :], in_=zb[:, 0:TT]), QG, 16)
            p.wait(g, QG)
    p.wait(g, QG)
    p.inc(g.dma_start(out=sh_hbc[bass.ds(me_v, 1), :, :], in_=hbc_loc), QG, 16)
    p.wait(g, QG)
    flag_send()
    flag_wait()

    p.inc(g.dma_start(out=d64b[0:32, :].rearrange("q (s t) -> q s t", s=2),
                      in_=sh_hbc[:, :, :][:, bass.ds(me_v * 32 + 4352, 32), :]
                      .rearrange("s q t -> q s t")), QG, 16)
    p.inc(g.dma_start(out=zconv_dram,
                      in_=sh_hbc[:, :, :][:, bass.ds(me_v * 2048, 2048), :]),
          QG, 16)
    adt = ar[0:64, 10240:14336].bitcast(F32)
    identF = ar[:, 14336:14592].bitcast(F32)
    BT0 = ar[:, 14592:14720]
    BT1 = ar[:, 14720:14848]
    zsv = ar[:, 14848:14896]
    Ssb = ar[:, 8192:10240]
    p.wait(v, QG)
    p.inc(v.tensor_copy(identF, identB[:]), EV, 1)
    p.inc(v.memset(Ssb, 0.0), EV, 1)
    for th2 in range(2):
        hc = slice(th2 * TT, (th2 + 1) * TT)
        p.wait(sc, EV)
        p.inc(sc.activation(out=wk[0:32, 0:TT], in_=d64b[0:32, hc], func=AF.Abs,
                            bias=dtb_col[:, 0:1]), EA, 1)
        p.wait(sc, EA)
        p.inc(sc.activation(out=wk[0:32, TT:2 * TT], in_=wk[0:32, 0:TT],
                            func=AF.Exp, scale=-1.0), EA, 1)
        p.wait(sc, EA)
        p.inc(sc.activation(out=wk[0:32, 0:TT], in_=wk[0:32, TT:2 * TT],
                            func=AF.Ln, bias=1.0), EA, 1)
        p.wait(v, EA)
        p.inc(v.tensor_scalar(wk2[0:32, 0:TT], d64b[0:32, hc], dtb_col[:, 0:1],
                              0.0, ALU.add, ALU.max), EV, 1)
        p.wait(v, EV)
        p.inc(v.tensor_tensor(wk2[0:32, 0:TT], wk2[0:32, 0:TT], wk[0:32, 0:TT],
                              ALU.add), EV, 1)
        p.wait(v, EV)
        p.inc(v.tensor_scalar_mul(adt[0:32, hc], wk2[0:32, 0:TT],
                                  Ab_col[:, 0:1]), EV, 1)
        p.inc(v.tensor_copy(d64b[0:32, hc], wk2[0:32, 0:TT]), EV, 1)

    p.wait(g, QG)
    p.wait(g, EV)
    p.inc(g.memset(ar2[0:32, 2048:4096], 1.0), EP, 1)
    p.wait(g, EP)
    p.inc(g.affine_select(ar2[0:32, 2048:4096].rearrange("q (a b) -> q a b", a=32),
                          ar2[0:32, 2048:4096].rearrange("q (a b) -> q a b", a=32),
                          [[1, 32], [0, 64]], ALU.is_equal, 0.0,
                          base=0, channel_multiplier=-1), EP, 1)

    p.wait(g, EV)
    p.wait(g, EA)
    p.inc(g.memset(zb[:, 1408:1920], 0.0), EP, 1)
    p.wait(g, EP)
    for sch in range(2):
        p.inc(g.affine_select(zb[:, 1408 + sch * 256:1664 + sch * 256],
                              zb[:, 1408 + sch * 256:1664 + sch * 256],
                              [[1, LCH]], ALU.is_ge, -1e30,
                              base=-(sch * 128), channel_multiplier=-1), EP, 1)

    with (nc.psum_tensor("ps_gm", [128, 1024], F32) as ps_gm,
          nc.psum_tensor("ps_y2", [128, 512], F32) as ps_y2,
          nc.psum_tensor("ps_sd", [128, 512], F32) as ps_sd,
          nc.psum_tensor("ps_t", [128, 512], F32) as ps_t,
          nc.psum_tensor("ps_tb", [128, 1024], BF16) as ps_tb,
          nc.psum_tensor("ps_x", [128, 512], F32) as ps_x,
          nc.psum_tensor("ps_m", [128, 512], F32) as ps_m):
        for th in range(2):
            p.wait(g, EV)
            p.wait(g, EA)
            p.wait(g, ET)
            src = zconv_dram[th, :, :].rearrange("(i q) t -> q i t", q=128)
            p.inc(g.dma_start(out=ub[:, 0:16 * TT].rearrange(
                "q (i t) -> q i t", i=16), in_=src), QG, 16)
            def conv_tile(zsrc, wj, dst, edge_src):
                p.wait(v, EA)
                p.inc(v.memset(wk[:, 0:3], 0.0), EV, 1)
                p.inc(v.tensor_single_scalar(wk[:, 3:TT], zsrc[:, 0:TT - 3],
                                             convw[:, wj * 4:wj * 4 + 1],
                                             ALU.mult), EV, 1)
                for kk in range(1, 4):
                    p.wait(v, EV)
                    p.inc(v.scalar_tensor_tensor(
                        wk[:, 3 - kk:TT], zsrc[:, 0:TT - 3 + kk],
                        convw[:, wj * 4 + kk:wj * 4 + kk + 1],
                        wk[:, 3 - kk:TT], ALU.mult, ALU.add), EV, 1)
                if edge_src is not None:
                    for kk in range(3):
                        p.wait(v, EV)
                        p.inc(v.scalar_tensor_tensor(
                            wk[:, 0:3 - kk], edge_src[:, kk:3],
                            convw[:, wj * 4 + kk:wj * 4 + kk + 1],
                            wk[:, 0:3 - kk], ALU.mult, ALU.add), EV, 1)
                p.wait(sc, EV)
                p.inc(sc.activation(out=dst, in_=wk[:, 0:TT], func=AF.Silu,
                                    bias=convb[:, wj:wj + 1]), EA, 1)

            p.wait(v, QG)
            for j in range(16):
                edge = zsv[:, j * 3:j * 3 + 3] if th == 1 else None
                conv_tile(ub[:, j * TT:(j + 1) * TT], j,
                          cb[:, j * TT:(j + 1) * TT], edge)
            for j in (16, 17):
                base = 4096 + (j - 16) * 128
                p.wait(g, EV)
                p.wait(g, ET)
                p.inc(g.dma_start(out=zb[:, 0:TT], in_=sh_hbc[th, base:base + 128, :]),
                      QG, 16)
                if th == 1:
                    p.inc(g.dma_start(out=zb[:, TT:TT + 3],
                                      in_=sh_hbc[0, base:base + 128, TT - 3:TT]),
                          QG, 16)
                p.wait(v, QG)
                edge = zb[:, TT:TT + 3] if th == 1 else None
                conv_tile(zb[:, 0:TT], j, ar2[:, (j - 16) * TT:(j - 15) * TT], edge)
            if th == 0:
                for j in range(16):
                    p.wait(v, EA)
                    p.inc(v.tensor_copy(zsv[:, j * 3:j * 3 + 3],
                                        ub[:, j * TT + TT - 3:(j + 1) * TT]), EV, 1)
            if "bc" in tap:
                p.wait(g, EA)
                p.inc(g.dma_start(out=tap["bc"][:, th * TT:(th + 1) * TT],
                                  in_=ar2[:, 0:TT]), QG, 16)
                p.inc(g.dma_start(out=tap["bc"][:, 2048 + th * TT:2048 + (th + 1) * TT]
                                  if False else
                                  tap["bc"][:, 2 * TT + th * TT:2 * TT + (th + 1) * TT],
                                  in_=ar2[:, TT:2 * TT]), QG, 16)
                p.wait(g, QG)
            if "hsd" in tap:
                p.wait(g, EA)
                for j in range(16):
                    p.inc(g.dma_start(out=tap["hsd"][j * 128:(j + 1) * 128,
                                                     th * TT:(th + 1) * TT],
                                      in_=cb[:, j * TT:(j + 1) * TT]), QG, 16)
                p.wait(g, QG)
            for j in range(16):
                lhs = ar2[0:32, 2048 + 2 * j * 64:2048 + 2 * j * 64 + 128]
                for tc in range(2):
                    p.wait(te, EV)
                    p.wait(te, EA)
                    p.inc(te.matmul(ps_m[:, 0:512], lhs,
                                    d64b[0:32, th * TT + tc * 512:
                                         th * TT + (tc + 1) * 512],
                                    start=True, stop=True), ET, 1)
                    p.wait(v, ET)
                    p.inc(v.tensor_tensor(
                        ub[:, j * TT + tc * 512:j * TT + (tc + 1) * 512],
                        cb[:, j * TT + tc * 512:j * TT + (tc + 1) * 512],
                        ps_m[:, 0:512], ALU.mult), EV, 1)
            for cl in range(4):
                c = th * 4 + cl
                for j in range(16):
                    for sch in range(2):
                        p.wait(te, EV)
                        p.wait(te, EA)
                        p.inc(te.transpose(
                            ps_tb[:, 0:128],
                            ub[:, j * TT + cl * LCH + sch * 128:
                               j * TT + cl * LCH + sch * 128 + 128],
                            identB[:]), ET, 1)
                        p.wait(sc, ET)
                        p.inc(sc.copy(ar[:, sch * 2048 + j * 128:
                                         sch * 2048 + (j + 1) * 128],
                                      ps_tb[:, 0:128]), EA, 1)
                acum = wk2[0:32, 128:384]
                ones_d0 = ones_colF[0:32, 0:1].broadcast_to([32, LCH])
                p.wait(v, EA)
                p.wait(v, ET)
                p.inc(v.tensor_tensor_scan(acum, ones_d0,
                                           adt[0:32, th * TT + cl * LCH:
                                               th * TT + (cl + 1) * LCH],
                                           0.0, ALU.mult, ALU.add), EV, 1)
                p.wait(g, EV)
                p.inc(g.dma_start(out=acum_dram[c, :, :], in_=acum), QG, 16)
                p.wait(g, QG)
                for sch in range(2):
                    p.wait(te, EV)
                    p.wait(te, EA)
                    p.inc(te.transpose(ps_t[:, 0:32],
                                       acum[:, sch * 128:(sch + 1) * 128],
                                       identF[0:32, 0:32]), ET, 1)
                    p.wait(v, ET)
                    p.inc(v.tensor_copy(wk2[:, 384 + sch * 32:416 + sch * 32],
                                        ps_t[:, 0:32]), EV, 1)
                p.wait(sc, EV)
                p.inc(sc.activation(out=zb[0:32, 512:768], in_=acum, func=AF.Exp),
                      EA, 1)
                p.wait(te, EV)
                p.wait(te, EA)
                p.inc(te.matmul(ps_sd[0:1, 0:32], wk2[0:32, 383:384],
                                identF[0:32, 0:32], start=True, stop=True), ET, 1)
                p.wait(sc, ET)
                p.inc(sc.copy(wk2[0:1, 448:480], ps_sd[0:1, 0:32]), EA, 1)
                p.wait(te, EA)
                p.inc(te.matmul(ps_sd[:, 64:96], ones_rowF[:], wk2[0:1, 448:480],
                                start=True, stop=True), ET, 1)
                for sch in range(2):
                    p.wait(v, ET)
                    p.inc(v.tensor_tensor(wk2[:, 512 + sch * 32:544 + sch * 32],
                                          ps_sd[:, 64:96],
                                          wk2[:, 384 + sch * 32:416 + sch * 32],
                                          ALU.subtract), EV, 1)
                    p.wait(sc, EV)
                    p.inc(sc.activation(out=zb[:, 1920 + sch * 32:1952 + sch * 32],
                                        in_=wk2[:, 512 + sch * 32:544 + sch * 32],
                                        func=AF.Exp), EA, 1)
                p.wait(sc, ET)
                p.inc(sc.activation(out=zb[:, 1984:2016], in_=ps_sd[:, 64:96],
                                    func=AF.Exp), EA, 1)
                if "dbg4" in tap and c == 0:
                    p.wait(g, EA)
                    p.inc(g.dma_start(out=tap["dbg4"][0:128, 0:64], in_=wk2[:, 384:448]), QG, 16)
                    p.inc(g.dma_start(out=tap["dbg4"][0:128, 64:128], in_=wk2[:, 512:576]), QG, 16)
                    p.inc(g.dma_start(out=tap["dbg4"][0:128, 128:224], in_=zb[:, 1920:2016]), QG, 16)
                    p.wait(g, QG)
                for grp in range(8):
                    bb_flush()
                    h4 = slice(grp * 4, grp * 4 + 4)
                    for sch in range(2):
                        src = acum_dram[c, grp * 4:grp * 4 + 4, :].rearrange(
                            "h l -> (h l)")[None, :].partition_broadcast(128)
                        p.wait(g, EV)
                        p.wait(g, EP)
                        p.wait(g, EA)
                        p.inc(g.dma_start(out=wk[:, 0:1024], in_=src), QG, 16)
                        acT = wk2[:, 384 + sch * 32:416 + sch * 32]
                        in1 = acT[:, h4].unsqueeze(2).broadcast_to([128, 4, LCH])
                        p.wait(v, QG)
                        p.wait(v, EA)
                        p.inc(v.tensor_tensor(
                            wk[:, 1024:2048].rearrange("q (h l) -> q h l", h=4),
                            wk[:, 0:1024].rearrange("q (h l) -> q h l", h=4),
                            in1, ALU.subtract), EV, 1)
                        p.wait(g, EV)
                        mskap = zb[:, 1408 + sch * 256:1664 + sch * 256] \
                            .unsqueeze(1).broadcast_to([128, 4, LCH])
                        p.inc(g.tensor_tensor(
                            wk[:, 1024:2048].rearrange("q (h l) -> q h l", h=4),
                            wk[:, 1024:2048].rearrange("q (h l) -> q h l", h=4),
                            mskap, ALU.add), EP, 1)
                        p.wait(sc, EP)
                        p.inc(sc.activation(out=wk[:, 0:1024],
                                            in_=wk[:, 1024:2048], func=AF.Exp),
                              EA, 1)
                        if "dbg" in tap and c == 2 and grp == 1:
                            p.wait(g, EA)
                            p.inc(g.dma_start(out=tap["dbg"][sch * 128:sch * 128 + 128, :],
                                              in_=wk[:, 1024:2048]), QG, 16)
                            p.inc(g.dma_start(out=tap["dbg"][256 + sch * 128:
                                                             384 + sch * 128, :],
                                              in_=wk[:, 0:1024]), QG, 16)
                            p.wait(g, QG)
                        p.wait(te, EV)
                        p.wait(te, EA)
                        for hh in range(4):
                            p.inc(te.matmul(
                                ps_gm[:, hh * 256:(hh + 1) * 256],
                                ar2[:, cl * LCH + sch * 128:
                                    cl * LCH + sch * 128 + 128],
                                ar2[:, TT + cl * LCH:TT + (cl + 1) * LCH],
                                start=True, stop=True), ET, 1)
                        p.wait(v, ET)
                        p.wait(v, EA)
                        p.inc(v.tensor_tensor(zb2[:, sch * 1024:(sch + 1) * 1024],
                                              ps_gm[:, 0:1024], wk[:, 0:1024],
                                              ALU.mult), EV, 1)

                    if "dbg2" in tap and c == 1 and grp == 0:
                        p.wait(g, EV)
                        p.inc(g.dma_start(out=tap["dbg2"][0:128, 0:1024], in_=zb2[:, 0:1024]), QG, 16)
                        p.inc(g.dma_start(out=tap["dbg2"][128:256, 0:1024], in_=zb2[:, 1024:2048]), QG, 16)
                        p.inc(g.dma_start(out=tap["dbg2"][256:384, 0:2048], in_=Ssb), QG, 16)
                        p.inc(g.dma_start(out=tap["dbg2"][384:416, 0:256], in_=zb[0:32, 512:768]), QG, 16)
                        p.wait(g, QG)
                    for hh in range(4):
                        hloc = grp * 4 + hh
                        lhs1 = ident64[0:32, hloc:hloc + 1].broadcast_to([32, 128])
                        p.wait(te, EV)
                        p.wait(te, EA)
                        p.inc(te.matmul(ps_sd[:, 0:256], lhs1, zb[0:32, 512:768],
                                        start=True, stop=True), ET, 1)
                        csd = zb[:, (hh % 2) * 256:(hh % 2) * 256 + 256]
                        p.wait(v, ET)
                        p.inc(v.tensor_tensor(csd, ar2[:, TT + cl * LCH:
                                                       TT + (cl + 1) * LCH],
                                              ps_sd[:, 0:256], ALU.mult), EV, 1)
                        pcol = (hh // 2) * 256
                        prow = 64 * (hh % 2)
                        tp = (0, 64) if hh % 2 else None
                        p.wait(te, EV)
                        for sch in range(2):
                            p.inc(te.matmul(
                                ps_y2[prow:prow + 64, pcol:pcol + 256],
                                ar[:, sch * 2048 + hloc * 64:
                                   sch * 2048 + hloc * 64 + 64],
                                zb2[:, sch * 1024 + hh * 256:
                                    sch * 1024 + (hh + 1) * 256],
                                start=(sch == 0), stop=False,
                                tile_position=tp), ET, 1)
                        p.inc(te.matmul(
                            ps_y2[prow:prow + 64, pcol:pcol + 256],
                            Ssb[:, hloc * 64:(hloc + 1) * 64], csd,
                            start=False, stop=True, tile_position=tp), ET, 1)
                        if hh % 2 == 1:
                            pr = grp * 2 + hh // 2
                            p.wait(v, ET)
                            p.wait(v, QG)
                            p.inc(v.scalar_tensor_tensor(
                                zb[:, 1120:1376],
                                cb[:, pr * TT + cl * LCH:pr * TT + (cl + 1) * LCH],
                                Dbr[:, pr:pr + 1],
                                ps_y2[:, pcol:pcol + 256], ALU.mult, ALU.add),
                                EV, 1)
                            p.wait(g, EV)
                            p.inc(g.dma_start(
                                out=yb_loc2[pr * 128:(pr + 1) * 128,
                                            th * TT + cl * LCH:
                                            th * TT + (cl + 1) * LCH],
                                in_=zb[:, 1120:1376]), QG, 16)
                for sch in range(2):
                    p.wait(te, EV)
                    p.wait(te, EA)
                    p.inc(te.transpose(ps_tb[:, 0:128],
                                       ar2[:, cl * LCH + sch * 128:
                                           cl * LCH + sch * 128 + 128],
                                       identB[:]), ET, 1)
                    p.wait(sc, ET)
                    p.inc(sc.copy(BT0 if sch == 0 else BT1, ps_tb[:, 0:128]), EA, 1)
                    dec = zb[:, 1920 + sch * 32:1952 + sch * 32]
                    in1b = dec.unsqueeze(2).broadcast_to([128, 32, 64])
                    p.wait(v, EA)
                    p.inc(v.tensor_tensor(
                        ar[:, 4096 + sch * 2048:4096 + (sch + 1) * 2048]
                        .rearrange("q (h pp) -> q h pp", h=32),
                        ar[:, sch * 2048:(sch + 1) * 2048]
                        .rearrange("q (h pp) -> q h pp", h=32),
                        in1b, ALU.mult), EV, 1)
                if "dbg3" in tap and c == 0:
                    p.wait(g, EV)
                    p.wait(g, EA)
                    p.inc(g.dma_start(out=tap["dbg3"][0:128, 0:32], in_=zb[:, 1920:1952]), QG, 16)
                    p.inc(g.dma_start(out=tap["dbg3"][0:128, 32:64], in_=zb[:, 1952:1984]), QG, 16)
                    p.inc(g.dma_start(out=tap["dbg3"][0:128, 64:96], in_=zb[:, 1984:2016]), QG, 16)
                    p.inc(g.dma_start(out=tap["dbg3"][128:256, 0:2048], in_=ar[:, 4096:6144]), QG, 16)
                    p.inc(g.dma_start(out=tap["dbg3"][256:384, 0:2048], in_=ar[:, 6144:8192]), QG, 16)
                    p.inc(g.dma_start(out=tap["dbg3"][384:512, 0:128], in_=BT0), QG, 16)
                    p.inc(g.dma_start(out=tap["dbg3"][384:512, 128:256], in_=BT1), QG, 16)
                    p.wait(g, QG)
                drep = zb[:, 1984:2016]
                for sl4 in range(4):
                    scol = slice(sl4 * 512, (sl4 + 1) * 512)
                    p.wait(te, EV)
                    p.wait(te, EA)
                    p.inc(te.matmul(ps_x[:, 0:512], BT0,
                                    ar[:, 4096:6144][:, scol],
                                    start=True, stop=False), ET, 1)
                    p.inc(te.matmul(ps_x[:, 0:512], BT1,
                                    ar[:, 6144:8192][:, scol],
                                    start=False, stop=True), ET, 1)
                    dr1 = drep[:, sl4 * 8:(sl4 + 1) * 8].unsqueeze(2) \
                        .broadcast_to([128, 8, 64])
                    p.wait(v, ET)
                    p.inc(v.tensor_tensor(
                        zb2[:, 0:512].rearrange("q (h pp) -> q h pp", h=8),
                        Ssb[:, scol].rearrange("q (h pp) -> q h pp", h=8),
                        dr1, ALU.mult), EV, 1)
                    p.wait(v, EV)
                    p.inc(v.tensor_tensor(Ssb[:, scol], zb2[:, 0:512],
                                          ps_x[:, 0:512], ALU.add), EV, 1)

    if "adt" in tap:
        p.wait(g, EV)
        p.inc(g.dma_start(out=tap["adt"][0:32, :], in_=adt[0:32, :]), QG, 16)
    if "acum" in tap:
        p.wait(g, QG)
        p.inc(g.dma_start(out=tap["acum"],
                          in_=acum_dram.rearrange("c h l -> (c h) l")), QG, 16)
    if "ybl" in tap:
        p.wait(g, QG)
        for j in range(16):
            p.inc(g.dma_start(out=zb[:, 0:2048],
                              in_=yb_loc2[j * 128:(j + 1) * 128, :]), QG, 16)
            p.wait(g, QG)
            p.inc(g.dma_start(out=tap["ybl"][j * 128:(j + 1) * 128, :],
                              in_=zb[:, 0:2048]), QG, 16)
            p.wait(g, QG)
    if "sg3" in tap:
        p.wait(g, QG)
        for j in range(16):
            p.inc(g.dma_start(out=zb[:, 0:TT], in_=gate_dram[j * 128:(j + 1) * 128, :]), QG, 16)
            p.wait(g, QG)
            p.inc(g.dma_start(out=tap["sg3"][j * 128:(j + 1) * 128, :], in_=zb[:, 0:TT]), QG, 16)
            p.wait(g, QG)
    p.wait(g, QG)
    p.inc(g.dma_start(out=sh_yb[bass.ds(me_v, 1), :, :], in_=yb_loc2), QG, 16)
    p.wait(g, QG)
    flag_send()
    flag_wait()

    p.inc(g.dma_start(out=ytok_dram, in_=sh_yb[:, :, bass.ds(me_v * TT, TT)]),
          QG, 16)
    p.wait(sy, QG)
    p.wait(sy, EV)
    for sl_i in range(2):
        dstar = ub if sl_i == 0 else ar
        src = ytok_dram[sl_i, :, :].rearrange("(i q) t -> q i t", q=128)
        p.inc(sy.dma_start(out=dstar[:, 0:16 * TT].rearrange(
            "q (i t) -> q i t", i=16), in_=src), QS, 16)
    p.wait(g, QS)

    if "yraw" in tap:
        p.wait(g, QS)
        p.wait(g, QG)
        for j in range(32):
            p.inc(g.dma_start(out=tap["yraw"][j * 128:(j + 1) * 128, :],
                              in_=ytile(j) if False else
                              (ub if j < 16 else ar)[:, (j % 16) * TT:(j % 16 + 1) * TT]),
                  QG, 16)
        p.wait(g, QG)
    if "argate" in tap:
        p.wait(g, QG)
        for j in range(2):
            p.inc(g.dma_start(out=tap["argate"][j * 128:(j + 1) * 128, :],
                              in_=ar[:, j * TT:(j + 1) * TT]), QG, 16)
            p.inc(g.dma_start(out=tap["argate"][256 + j * 128:256 + (j + 1) * 128, :],
                              in_=ub[:, j * TT:(j + 1) * TT]), QG, 16)
        p.wait(g, QG)
    if "ytk" in tap:
        p.wait(g, QG)
        for sl_i in range(2):
            for j in range(2):
                p.inc(g.dma_start(out=zb[:, 0:TT],
                                  in_=ytok_dram[sl_i, j * 128:(j + 1) * 128, :]), QG, 16)
                p.wait(g, QG)
                p.inc(g.dma_start(out=tap["ytk"][(sl_i * 2 + j) * 128:
                                                 (sl_i * 2 + j + 1) * 128, :],
                                  in_=zb[:, 0:TT]), QG, 16)
                p.wait(g, QG)

    def ytile(i):
        return (ub if i < 16 else ar)[:, (i % 16) * TT:(i % 16 + 1) * TT]

    with nc.psum_tensor("ps_r", [128, 2048], F32) as ps_r:
        for i in range(32):
            p.wait(g, EV)
            p.inc(g.dma_start(out=zb[:, 0:TT],
                              in_=gate_dram[i * 128:(i + 1) * 128, :]), QG, 16)
            p.wait(v, QG)
            p.inc(v.tensor_tensor(ytile(i), ytile(i), zb[:, 0:TT], ALU.mult),
                  EV, 1)
            p.wait(sc, EV)
            p.wait(sc, ET)
            p.inc(sc.activation(out=zb2[:, 0:TT], in_=ytile(i), func=AF.Square),
                  EA, 1)
            p.wait(te, EA)
            for tch in range(2):
                p.inc(te.matmul(ps_r[0:1, tch * 512:(tch + 1) * 512], ones_col[:],
                                zb2[:, tch * 512:(tch + 1) * 512],
                                start=(i == 0), stop=(i == 31)), ET, 1)
        p.wait(v, ET)
        p.inc(v.tensor_scalar_mul(wk[0:1, 0:TT], ps_r[0:1, 0:TT], 1.0 / INTER),
              EV, 1)
        p.wait(v, EV)
        p.inc(v.tensor_scalar_add(wk[0:1, 0:TT], wk[0:1, 0:TT], EPS), EV, 1)
        p.wait(v, EV)
        p.inc(v.reciprocal(wk[0:1, TT:2 * TT], wk[0:1, 0:TT]), EV, 1)
        p.wait(sc, EV)
        p.inc(sc.activation(out=wk[0:1, 0:TT], in_=wk[0:1, TT:2 * TT],
                            func=AF.Sqrt), EA, 1)
        p.wait(v, EA)
        p.inc(v.tensor_copy(zb[0:1, 0:TT], wk[0:1, 0:TT]), EV, 1)
        p.wait(te, EV)
        for tch in range(2):
            p.inc(te.matmul(ps_r[:, 1024 + tch * 512:1024 + (tch + 1) * 512],
                            ones_row[:], zb[0:1, tch * 512:(tch + 1) * 512],
                            start=True, stop=True), ET, 1)
        p.wait(v, ET)
        p.inc(v.tensor_copy(rsb, ps_r[:, 1024:2048]), EV, 1)
        for i in range(32):
            p.wait(v, EV)
            p.inc(v.scalar_tensor_tensor(ytile(i), ytile(i), nwr[:, i:i + 1],
                                         rsb, ALU.mult, ALU.mult), EV, 1)

    if "yn" in tap:
        p.wait(g, EV)
        for j in range(4):
            p.inc(g.dma_start(out=tap["yn"][j * 128:(j + 1) * 128, :],
                              in_=ub[:, j * TT:(j + 1) * TT]), QG, 16)
        p.wait(g, QG)
    if "sg" in tap:
        for j in range(16):
            p.inc(g.dma_start(out=zb[:, 0:TT], in_=gate_dram[j * 128:(j + 1) * 128, :]), QG, 16)
            p.wait(g, QG)
            p.inc(g.dma_start(out=tap["sg"][j * 128:(j + 1) * 128, :], in_=zb[:, 0:TT]), QG, 16)
            p.wait(g, QG)
    with (nc.psum_tensor("ps_o0", [128, 1024], F32) as o0,
          nc.psum_tensor("ps_o1", [128, 1024], F32) as o1,
          nc.psum_tensor("ps_o2", [128, 1024], F32) as o2,
          nc.psum_tensor("ps_o3", [128, 1024], F32) as o3):
        po4 = [o0, o1, o2, o3]
        for m2b in range(4):
            mc = slice(m2b * 512, (m2b + 1) * 512)
            for k2 in range(32):
                wbuf = 2048 + (k2 % 2) * 512
                p.wait(g, ET)
                p.inc(g.dma_start(out=cb[:, wbuf:wbuf + 512],
                                  in_=inp["outT"][k2 * 128:(k2 + 1) * 128, mc]),
                      QG, 16)
                p.wait(te, QG)
                if k2 == 0:
                    p.wait(te, EV)
                for m2 in range(4):
                    lhs = cb[:, wbuf + m2 * 128:wbuf + (m2 + 1) * 128]
                    for tch in range(2):
                        p.inc(te.matmul(
                            po4[m2][:, tch * 512:(tch + 1) * 512], lhs,
                            ytile(k2)[:, tch * 512:(tch + 1) * 512],
                            start=(k2 == 0), stop=(k2 == 31)), ET, 1)
            for m2 in range(4):
                d_i = m2b * 4 + m2
                p.wait(v, ET)
                p.inc(v.tensor_tensor(xr[:, d_i * TT:(d_i + 1) * TT],
                                      po4[m2][:, 0:TT],
                                      xr[:, d_i * TT:(d_i + 1) * TT], ALU.add),
                      EV, 1)

    if "ynpost" in tap:
        p.wait(g, ET)
        p.wait(g, EV)
        for j in range(8):
            p.inc(g.dma_start(out=tap["ynpost"][j * 128:(j + 1) * 128, :],
                              in_=ytile(j * 4)), QG, 16)
        p.wait(g, QG)
    if "x3" in tap:
        p.wait(sy, EV)
        for i in range(DT):
            p.inc(sy.dma_start(out=tap["x3"][i * 128:(i + 1) * 128, :],
                               in_=xr[:, i * TT:(i + 1) * TT]), QS, 16)
    if upto == "x3":
        finish()
        return nc

    ffn_block(first=False)

    finish()
    return nc


def _stage(inputs, core):
    b, half = core // 2, core % 2
    t0 = half * TT
    _ = inputs.get("L0_attn_b_mask")
    x = inputs["L0_attn_a_input"]
    d = {}
    d["xin"] = np.ascontiguousarray(np.asarray(x[b, t0:t0 + TT, :]).T)
    nonce = int(np.random.randint(1 << 20, 1 << 30))
    _stage.nonce = getattr(_stage, "nonce", nonce)
    d["meta"] = np.array([[half, 1 - half, _stage.nonce, 0, 0, 0, 0, 0]], np.int32)
    A = np.asarray
    d["wprj_cat"] = np.ascontiguousarray(np.concatenate(
        [A(inputs["L0_attn_a_W_dt_in"]).T,
         A(inputs["L0_attn_a_W_B"]).T,
         np.zeros((D, 16), np.float32),
         A(inputs["L0_attn_a_W_C"]).T], axis=1))
    d["wdt_outT_own"] = np.ascontiguousarray(
        A(inputs["L0_attn_a_W_dt_out"]).T[:, half * 1024:(half + 1) * 1024])
    d["bdt_own"] = np.ascontiguousarray(
        A(inputs["L0_attn_a_b_dt_out"]).reshape(16, 128).T[:, half * 8:half * 8 + 8])
    d["Da_own"] = np.ascontiguousarray(
        A(inputs["L0_attn_a_D"]).reshape(16, 128).T[:, half * 8:half * 8 + 8])
    d["alog_own"] = np.ascontiguousarray(
        A(inputs["L0_attn_a_A_log"]).reshape(16, 128, 16).transpose(1, 0, 2)
        .reshape(128, 256)[:, half * 128:(half + 1) * 128])
    d["inpT"] = np.ascontiguousarray(A(inputs["L0_attn_b_in_proj_w"]).T)
    cw = A(inputs["L0_attn_b_conv_w"]).reshape(34, 128, 4)
    cbv = A(inputs["L0_attn_b_conv_b"]).reshape(34, 128)
    own_cht = list(range(half * 16, half * 16 + 16)) + [32, 33]
    d["convw_own"] = np.ascontiguousarray(
        cw[own_cht].transpose(1, 0, 2).reshape(128, 72))
    d["convb_own"] = np.ascontiguousarray(cbv[own_cht].T)
    d["alogb_own"] = np.ascontiguousarray(
        A(inputs["L0_attn_b_A_log"]).reshape(NH, 1)[half * 32:half * 32 + 32])
    d["dtbias_own"] = np.ascontiguousarray(
        A(inputs["L0_attn_b_dt_bias"]).reshape(NH, 1)[half * 32:half * 32 + 32])
    d["Db_own"] = np.ascontiguousarray(
        np.repeat(A(inputs["L0_attn_b_D"]), PDIM).reshape(32, 128).T
        [:, half * 16:half * 16 + 16])
    d["normw_r"] = np.ascontiguousarray(
        A(inputs["L0_attn_b_norm_w"]).reshape(32, 128).T)
    d["outT"] = np.ascontiguousarray(A(inputs["L0_attn_b_out_proj_w"]).T)
    fcT = A(inputs["L0_ffn_w_fc"]).T
    pjT = A(inputs["L0_ffn_w_proj"]).T
    d["fcT_half"] = np.ascontiguousarray(fcT[half * (D // 2):(half + 1) * (D // 2)])
    d["projT_half"] = np.ascontiguousarray(
        pjT[half * (FFH // 2):(half + 1) * (FFH // 2)])
    return {k: np.asarray(v2) for k, v2 in d.items()}


def kernel(**inputs):
    _stage.nonce = int(np.random.randint(1 << 20, 1 << 30))
    nc = build_program()
    in_maps = [_stage(inputs, c) for c in range(8)]
    res = run_bass_kernel_spmd(nc, in_maps, core_ids=list(range(8)))
    out = np.zeros((B, S, D), np.float32)
    for c in range(8):
        b, half = c // 2, c % 2
        out[b, half * TT:(half + 1) * TT, :] = res.results[c]["xout"].T
    return out


# version 09
# speedup vs baseline: 1.4594x; 1.0092x over previous
import numpy as np
import concourse.bass as bass
import concourse.mybir as mybir
from concourse.bass_utils import run_bass_kernel_spmd

F32 = mybir.dt.float32
BF16 = mybir.dt.bfloat16
I32 = mybir.dt.int32
AF = mybir.ActivationFunctionType
ALU = mybir.AluOpType
AX = mybir.AxisListType

B, S, D = 4, 2048, 2048
TT = 1024
DT = 16
NA = 16
FFH = 5120
INTER = 4096
NH = 64
PDIM = 64
NB = 128
LCH = 256
NCH = 8
EPS = 1e-5
STAGES = ("prep", "attna", "ffn1", "attnb", "ffn2")


class P:

    def __init__(self, nc):
        self.nc = nc
        self.cnt = {}
        self.sems = {}

    def sem(self, name):
        if name not in self.sems:
            self.sems[name] = self.nc.semaphore(name).__enter__()
            self.cnt[name] = 0
        return self.sems[name]

    def inc(self, instr, name, amt):
        instr.then_inc(self.sem(name), amt)
        self.cnt[name] += amt

    def wait(self, eng, name):
        eng.wait_ge(self.sem(name), self.cnt[name])

    def wait_lag(self, eng, name, lag):
        eng.wait_ge(self.sem(name), max(0, self.cnt[name] - lag))


def build_program(upto="ffn2", taps=()):
    nc = bass.Bass("TRN2", target_bir_lowering=False)
    p = P(nc)
    g, v, sc, te, sy = nc.gpsimd, nc.vector, nc.scalar, nc.tensor, nc.sync

    inp = {}

    def din(name, shape, dt=F32):
        inp[name] = nc.dram_tensor(name, shape, dt, kind="ExternalInput").ap()
        return inp[name]

    din("xin", [D, TT])
    din("meta", [1, 8], I32)
    din("wprj_cat", [D, 112])
    din("wdt_outT_own", [64, 1024])
    din("bdt_own", [128, 8])
    din("Da_own", [128, 8])
    din("alog_own", [128, 128])
    din("inpT", [D, 8512])
    din("convw_own", [128, 18 * 4])
    din("convb_own", [128, 18])
    din("alogb_own", [32, 1])
    din("dtbias_own", [32, 1])
    din("Db_own", [128, 16])
    din("normw_r", [128, 32])
    din("outT", [INTER, D])
    din("fcT_half", [D // 2, FFH])
    din("projT_half", [FFH // 2, D])

    xout = nc.dram_tensor("xout", [D, TT], F32, kind="ExternalOutput").ap()
    tap = {}
    for tname, tshape in taps:
        tap[tname] = nc.dram_tensor("tap_" + tname, tshape, F32,
                                    kind="ExternalOutput").ap()

    shd = lambda n, s, dt: nc.dram_tensor(n, s, dt, kind="Internal",
                                          addr_space="Shared").ap()
    sh_q2fc = shd("sh_q2fc", [2, D // 2, FFH], BF16)
    sh_q2pj = shd("sh_q2pj", [2, FFH // 2, D], BF16)
    sh_u = shd("sh_u", [2, D, TT], BF16)
    sh_ya = shd("sh_ya", [2, 1024, S], BF16)
    sh_hbc = shd("sh_hbc", [2, 4416, TT], BF16)
    sh_yb = shd("sh_yb", [2, 2048, S], BF16)
    sh_flag = shd("sh_flag", [2, 16], I32)

    gate_dram = nc.dram_tensor("gate_dram", [INTER, TT], BF16, kind="Internal").ap()
    q2fc_loc = nc.dram_tensor("q2fc_loc", [D // 2, FFH], BF16, kind="Internal").ap()
    q2pj_loc = nc.dram_tensor("q2pj_loc", [FFH // 2, D], BF16, kind="Internal").ap()
    hbc_loc = nc.dram_tensor("hbc_loc", [4416, TT], BF16, kind="Internal").ap()
    yb_loc2 = nc.dram_tensor("yb_loc2", [2048, S], BF16, kind="Internal").ap()
    zconv_dram = nc.dram_tensor("zconv_dram", [2, 2048, TT], BF16, kind="Internal").ap()
    ytok_dram = nc.dram_tensor("ytok_dram", [2, 2048, TT], BF16, kind="Internal").ap()
    acum_dram = nc.dram_tensor("acum_dram", [NCH, 32, LCH], F32, kind="Internal").ap()
    h_dram = nc.dram_tensor("h_dram", [FFH, TT], BF16, kind="Internal").ap()

    sbufs = []

    def sbuf(name, shape, dt):
        t = nc.sbuf_tensor(name, shape, dt).__enter__()
        sbufs.append(t)
        return t

    mp = sbuf("mp", [1, 8], I32)
    identB = sbuf("identB", [128, 128], BF16)
    ident64 = sbuf("ident64", [64, 64], BF16)
    ones_col = sbuf("ones_col", [128, 1], BF16)
    ones_colF = sbuf("ones_colF", [128, 1], F32)
    ones_row = sbuf("ones_row", [1, 128], BF16)
    ones_rowF = sbuf("ones_rowF", [1, 128], F32)
    ph_sb = sbuf("ph_sb", [1, 16], I32)
    pf_sb = sbuf("pf_sb", [1, 16], I32)
    A_a = sbuf("A_a", [128, 128], F32)
    convw = sbuf("convw", [128, 72], F32)
    convb = sbuf("convb", [128, 18], F32)
    bdt = sbuf("bdt", [128, 8], F32)
    Da = sbuf("Da", [128, 8], F32)
    d64b = sbuf("d64b", [64, 2048], BF16)
    brawb = sbuf("brawb", [16, 2048], BF16)
    crawb = sbuf("crawb", [16, 2048], BF16)
    wpj = sbuf("wpj", [128, 1792], BF16)
    Ab_col = sbuf("Ab_col", [32, 1], F32)
    dtb_col = sbuf("dtb_col", [32, 1], F32)
    Dbr = sbuf("Dbr", [128, 16], F32)
    nwr = sbuf("nwr", [128, 32], F32)
    srow = sbuf("srow", [1, 64], F32)
    thr_c = sbuf("thr_c", [128, 2], F32)
    sc_ffn = sbuf("sc_ffn", [128, 1], F32)

    xr = sbuf("xr", [128, DT * TT], F32)
    ub = sbuf("ub", [128, DT * TT], BF16)
    cb = sbuf("cb", [128, 16384], BF16)
    ar = sbuf("ar", [128, 16384], BF16)
    ar2 = sbuf("ar2", [128, 4096], BF16)
    wouT = ar2[0:64, 3072:4096]
    wk = sbuf("wk", [128, 2048], F32)
    wk2 = sbuf("wk2", [128, 1024], F32)
    zb = sbuf("zb", [128, 2048], BF16)
    zb2 = sbuf("zb2", [128, 2048], BF16)
    rows = wk[0:1, :]
    mub = cb[:, 0:TT]
    rsb = cb[:, TT:2 * TT]
    st_a = wk2[:, 0:128]

    QS, QG, EV, EP, EA, ET, PO = "qs", "qg", "ev", "ep", "ea", "et", "po"

    p.inc(sy.dma_start(out=mp[:], in_=inp["meta"]), QS, 16)
    p.inc(sy.dma_start(out=bdt[:], in_=inp["bdt_own"]), QS, 16)
    p.inc(sy.dma_start(out=Da[:], in_=inp["Da_own"]), QS, 16)
    p.inc(sy.dma_start(out=wk[:, 0:128], in_=inp["alog_own"]), QS, 16)
    p.inc(sy.dma_start(out=convw[:], in_=inp["convw_own"]), QS, 16)
    p.inc(sy.dma_start(out=convb[:], in_=inp["convb_own"]), QS, 16)
    p.inc(sy.dma_start(out=wk[0:32, 256:257], in_=inp["alogb_own"]), QS, 16)
    p.inc(sy.dma_start(out=dtb_col[:], in_=inp["dtbias_own"]), QS, 16)
    p.inc(sy.dma_start(out=Dbr[:], in_=inp["Db_own"]), QS, 16)
    p.inc(sy.dma_start(out=nwr[:], in_=inp["normw_r"]), QS, 16)

    p.inc(g.memset(identB[:], 1.0), EP, 1)
    p.inc(g.memset(ident64[:], 1.0), EP, 1)
    p.wait(g, EP)
    p.inc(g.affine_select(identB[:], identB[:], [[1, 128]], ALU.is_equal, 0.0,
                          base=0, channel_multiplier=-1), EP, 1)
    p.inc(g.affine_select(ident64[:], ident64[:], [[1, 64]], ALU.is_equal,
                          0.0, base=0, channel_multiplier=-1), EP, 1)
    p.inc(v.memset(ph_sb[:].bitcast(F32), 0.0), EV, 1)
    p.inc(v.memset(ones_col[:], 1.0), EV, 1)
    p.inc(v.memset(ones_colF[:], 1.0), EV, 1)
    p.inc(v.memset(ones_row[:], 1.0), EV, 1)
    p.inc(v.memset(ones_rowF[:], 1.0), EV, 1)

    p.wait(sc, QS)
    p.inc(sc.activation(out=A_a[:], in_=wk[:, 0:128], func=AF.Exp), EA, 1)
    p.inc(sc.activation(out=wk2[0:32, 0:1], in_=wk[0:32, 256:257], func=AF.Exp),
          EA, 1)
    p.wait(v, EA)
    p.inc(v.tensor_scalar_mul(A_a[:], A_a[:], -1.0), EV, 1)
    p.inc(v.tensor_scalar_mul(Ab_col[:], wk2[0:32, 0:1], -1.0), EV, 1)

    p.wait(g, QS)
    r_me = g.alloc_register("r_me")
    r_peer = g.alloc_register("r_peer")
    r_phase = g.alloc_register("r_phase")
    r_want = g.alloc_register("r_want")
    r_poll = g.alloc_register("r_poll")
    r_fl = g.alloc_register("r_fl")
    g.reg_load(r_me, mp[0:1, 0:1])
    g.reg_load(r_peer, mp[0:1, 1:2])
    g.reg_load(r_phase, mp[0:1, 2:3])
    g.reg_load(r_want, mp[0:1, 2:3])
    g.reg_mov(r_poll, 16)
    me_v = g.snap(r_me, min_val=0, max_val=1)
    peer_v = g.snap(r_peer, min_val=0, max_val=1)

    spin_id = [0]
    n_sends = [0]
    flush_id = [0]

    def allwait():
        for e in (v, sc, te, g):
            for snm in (EV, EA, ET, EP, QG):
                p.wait(e, snm)

    def bb_flush():
        flush_id[0] += 1
        nm = f"flush{flush_id[0]}"
        g.br(nm)
        nc.switch_bb(nm)

    def flag_send():
        n_sends[0] += 1
        g.reg_add(r_phase, r_phase, 1)
        g.reg_save(ph_sb[0:1, 0:1], r_phase)
        p.inc(g.dma_start(out=sh_flag[bass.ds(me_v, 1), :], in_=ph_sb[:]), QG, 16)
        return n_sends[0]

    def flag_wait(expect=None):
        k = n_sends[0] if expect is None else expect
        g.reg_load(r_want, mp[0:1, 2:3])
        g.reg_add(r_want, r_want, k)
        sid = spin_id[0]
        spin_id[0] += 1
        lname, cname, ename = f"spin{sid}", f"spin{sid}_c", f"spin{sid}_end"
        g.br(lname)
        with nc.body(lname, valid_engines=None):
            g.dma_start(out=pf_sb[:], in_=sh_flag[bass.ds(peer_v, 1), :]).then_inc(
                p.sem(PO), 16)
            g.wait_ge(p.sem(PO), r_poll)
            g.reg_add(r_poll, r_poll, 16)
            g.reg_load(r_fl, pf_sb[0:1, 0:1])
            g.br_lt(r_fl, r_want, on_true=lname, on_false=cname)
        with nc.body(cname, valid_engines=None):
            g.reg_add(r_fl, r_fl, -99)
            g.br_lt(r_fl, r_want, on_true=ename, on_false=lname)
        nc.switch_bb(ename)

    def finish():
        p.wait(sy, EV)
        p.wait(sy, EA)
        p.wait(sy, EP)
        p.wait(sy, ET)
        for i in range(DT):
            p.inc(sy.dma_start(out=xout[i * 128:(i + 1) * 128, :],
                               in_=xr[:, i * TT:(i + 1) * TT]), QS, 16)
        p.wait(sy, QS)
        p.wait(g, QG)

    def tap_out(name, src_f32_fn, ntiles, tcols):
        if name not in tap:
            return
        for i in range(ntiles):
            p.wait(sy, EV)
            p.wait(sy, EA)
            p.wait(sy, ET)
            src = src_f32_fn(i)
            p.inc(sy.dma_start(out=tap[name][i * 128:(i + 1) * 128, :], in_=src),
                  QS, 16)
        p.wait(sy, QS)

    with nc.psum_tensor("ps_tz", [128, 512], F32) as ps:
        specs = [(inp["fcT_half"], sh_q2fc, D, FFH, 0),
                 (inp["projT_half"], sh_q2pj, FFH, D, 1)]
        p.wait(v, EV)
        p.inc(v.memset(wk2[:, 0:4], 0.0), EV, 1)
        chunks = []
        for wsrc, _, rn, cn, wi in specs:
            half = rn // 2
            for i in range(half // 128):
                for cc in range((cn + 2047) // 2048):
                    c0, c1 = cc * 2048, min(cn, (cc + 1) * 2048)
                    chunks.append((wsrc, i, c0, c1, wi))
        prev_incs = 0
        for k, (wsrc, i, c0, c1, wi) in enumerate(chunks):
            w_ = c1 - c0
            nsub = (w_ + 511) // 512
            buf = xr[:, (k % 2) * 2048:(k % 2) * 2048 + w_]
            p.wait_lag(g, EV, prev_incs if k > 0 else 0)
            p.inc(g.dma_start(out=buf, in_=wsrc[i * 128:(i + 1) * 128, c0:c1]),
                  QG, 16)
            p.wait(v, QG)
            for s8 in range(nsub):
                a, b = s8 * 512, min(w_, (s8 + 1) * 512)
                p.inc(v.tensor_reduce(wk2[:, 4 + s8:5 + s8], buf[:, a:b],
                                      AX.X, ALU.add,
                                      apply_absolute_value=True), EV, 1)
            p.wait(v, EV)
            p.inc(v.tensor_reduce(wk2[:, 2:3], wk2[:, 4:4 + nsub], AX.X,
                                  ALU.add), EV, 1)
            p.wait(v, EV)
            p.inc(v.tensor_tensor(wk2[:, wi:wi + 1], wk2[:, wi:wi + 1],
                                  wk2[:, 2:3], ALU.add), EV, 1)
            p.wait(v, EV)
            prev_incs = nsub + 2
        p.wait(te, EV)
        p.inc(te.matmul(ps[0:1, 0:2], ones_colF[:], wk2[:, 0:2], start=True,
                        stop=True), ET, 1)
        p.wait(v, ET)
        p.inc(v.tensor_copy(srow[0:1, 0:2], ps[0:1, 0:2]), EV, 1)
        p.wait(v, EV)
        p.inc(v.tensor_copy(ph_sb[0:1, 4:6].bitcast(F32), srow[0:1, 0:2]), EV, 1)
        p.wait(g, EV)
        flag_send()
        flag_wait()
        p.inc(g.engine_nop(), EP, 1)
        p.wait(v, EP)
        p.inc(v.tensor_tensor(srow[0:1, 4:6], srow[0:1, 0:2],
                              pf_sb[0:1, 4:6].bitcast(F32), ALU.add), EV, 1)
        p.wait(v, EV)
        p.inc(v.tensor_scalar_mul(srow[0:1, 6:7], srow[0:1, 4:5], 1.0 / (D * FFH)),
              EV, 1)
        p.inc(v.tensor_scalar_mul(srow[0:1, 7:8], srow[0:1, 5:6], 1.0 / (D * FFH)),
              EV, 1)
        p.wait(v, EV)
        p.inc(v.tensor_scalar_max(srow[0:1, 6:8], srow[0:1, 6:8], 1e-5), EV, 1)
        if "scal" in tap:
            p.wait(g, EV)
            p.inc(g.dma_start(out=tap["scal"][0:1, 0:8], in_=srow[0:1, 0:8]), QG, 16)
            p.inc(g.dma_start(out=tap["scal"][0:1, 8:24], in_=pf_sb[:].bitcast(F32)), QG, 16)
            p.inc(g.dma_start(out=tap["scal"][0:1, 24:32], in_=ph_sb[0:1, 0:8].bitcast(F32)), QG, 16)
            p.wait(g, QG)
        p.wait(v, EV)
        p.inc(v.tensor_scalar_mul(srow[0:1, 8:10], srow[0:1, 6:8], 0.7), EV, 1)
        p.wait(v, EV)
        p.inc(v.tensor_tensor(srow[0:1, 10:11], srow[0:1, 6:7], srow[0:1, 6:7],
                              ALU.mult), EV, 1)
        p.wait(v, EV)
        p.inc(v.tensor_tensor(srow[0:1, 10:11], srow[0:1, 10:11], srow[0:1, 7:8],
                              ALU.mult), EV, 1)
        p.wait(v, EV)
        p.inc(v.tensor_scalar_mul(srow[0:1, 10:11], srow[0:1, 10:11], 0.125), EV, 1)
        p.wait(te, EV)
        p.inc(te.matmul(ps[:, 0:4], ones_rowF[:], srow[0:1, 8:12], start=True,
                        stop=True), ET, 1)
        p.wait(v, ET)
        p.inc(v.tensor_copy(sc_ffn[:], ps[:, 2:3]), EV, 1)

        bounce = {0: q2fc_loc, 1: q2pj_loc}
        kk = 0
        for wsrc, wdst, rn, cn, wi in specs:
            p.wait(v, ET)
            p.wait(v, EA)
            p.inc(v.tensor_copy(thr_c[:, 0:1], ps[:, wi:wi + 1]), EV, 1)
            p.wait(v, EV)
            p.inc(v.tensor_scalar_mul(thr_c[:, 1:2], thr_c[:, 0:1], -1.0), EV, 1)
            half = rn // 2
            for i in range(half // 128):
                nch_c = (cn + 2047) // 2048
                for cc in range(nch_c):
                    c0, c1 = cc * 2048, min(cn, (cc + 1) * 2048)
                    w_ = c1 - c0
                    buf = xr[:, (kk % 2) * 2048:(kk % 2) * 2048 + w_]
                    p.wait_lag(g, EA, 2 if kk > 0 else 0)
                    p.inc(g.dma_start(out=buf,
                                      in_=wsrc[i * 128:(i + 1) * 128, c0:c1]),
                          QG, 16)
                    p.wait(sc, QG)
                    p.wait(sc, EV)
                    p.inc(sc.activation(out=zb[:, 0:w_], in_=buf,
                                        func=AF.Sign, bias=thr_c[:, 0:1]), EA, 1)
                    p.inc(sc.activation(out=zb2[:, 0:w_], in_=buf,
                                        func=AF.Sign, bias=thr_c[:, 1:2]), EA, 1)
                    p.wait(v, EA)
                    p.inc(v.tensor_tensor(zb[:, 0:w_], zb[:, 0:w_], zb2[:, 0:w_],
                                          ALU.add), EV, 1)
                    p.wait(g, EV)
                    p.inc(g.dma_start(out=bounce[wi][i * 128:(i + 1) * 128, c0:c1],
                                      in_=zb[:, 0:w_]), QG, 16)
                    kk += 1
        p.wait(g, QG)
        p.inc(g.dma_start(out=sh_q2fc[bass.ds(me_v, 1), :, :], in_=q2fc_loc),
              QG, 16)
        p.inc(g.dma_start(out=sh_q2pj[bass.ds(me_v, 1), :, :], in_=q2pj_loc),
              QG, 16)
        p.wait(g, QG)
        flag_send()

    if upto == "tern":
        if "q2fc" in tap:
            p.wait(sy, QG)
            flag_wait()
            for cc in range(2):
                p.wait(sy, EV)
                p.inc(sy.dma_start(out=zb[:, 0:2048],
                                   in_=sh_q2fc[0, 0:128, cc * 2048:(cc + 1) * 2048]),
                      QS, 16)
                p.wait(v, QS)
                p.inc(v.tensor_copy(wk[:, 0:2048], zb[:, 0:2048]), EV, 1)
                p.wait(sy, EV)
                p.inc(sy.dma_start(out=tap["q2fc"][:, cc * 2048:(cc + 1) * 2048],
                                   in_=wk[:, 0:2048]), QS, 16)
        if "scffn" in tap:
            p.wait(sy, EV)
            p.inc(sy.dma_start(out=tap["scffn"], in_=sc_ffn[:]), QS, 16)
        finish()
        return nc

    _ln_n = [0]

    def ln_to_u(x_tile_fn, u_arena, send_shared):
        _ln_n[0] += 1
        with nc.psum_tensor("ps_ln%d" % _ln_n[0], [128, 2048], F32) as ps_ln:
            p.wait(v, QG)
            for i in range(DT):
                xi = x_tile_fn(i)
                p.inc(v.tensor_copy(u_arena[:, i * TT:(i + 1) * TT], xi), EV, 1)
                p.wait(sc, ET)
                p.wait(sc, EV)
                p.inc(sc.activation(out=zb[:, 0:TT], in_=xi,
                                    func=AF.Square), EA, 1)
                p.wait(te, EV)
                p.wait(te, EA)
                sl = u_arena[:, i * TT:(i + 1) * TT]
                for tch in range(2):
                    c = slice(tch * 512, (tch + 1) * 512)
                    p.inc(te.matmul(ps_ln[0:1, tch * 512:(tch + 1) * 512],
                                    ones_col[:], sl[:, c], start=(i == 0),
                                    stop=(i == DT - 1)), ET, 1)
                    p.inc(te.matmul(ps_ln[0:1, 1024 + tch * 512:1536 + tch * 512],
                                    ones_col[:], zb[:, 0:TT][:, c],
                                    start=(i == 0), stop=(i == DT - 1)), ET, 1)
            p.wait(v, ET)
            p.inc(v.tensor_scalar_mul(rows[0:1, 0:TT], ps_ln[0:1, 0:TT], 1.0 / D),
                  EV, 1)
            p.inc(v.tensor_scalar_mul(rows[0:1, TT:2 * TT],
                                      ps_ln[0:1, 1024:1024 + TT], 1.0 / D), EV, 1)
            p.wait(v, EV)
            p.inc(v.tensor_tensor(wk2[0:1, 0:TT], rows[0:1, 0:TT],
                                  rows[0:1, 0:TT], ALU.mult), EV, 1)
            p.wait(v, EV)
            p.inc(v.tensor_tensor(rows[0:1, TT:2 * TT], rows[0:1, TT:2 * TT],
                                  wk2[0:1, 0:TT], ALU.subtract), EV, 1)
            p.wait(v, EV)
            p.inc(v.tensor_scalar_add(rows[0:1, TT:2 * TT], rows[0:1, TT:2 * TT],
                                      EPS), EV, 1)
            p.wait(v, EV)
            p.inc(v.reciprocal(wk2[0:1, 0:TT], rows[0:1, TT:2 * TT]), EV, 1)
            p.wait(sc, EV)
            p.inc(sc.activation(out=rows[0:1, TT:2 * TT], in_=wk2[0:1, 0:TT],
                                func=AF.Sqrt), EA, 1)
            p.wait(v, EA)
            p.inc(v.tensor_copy(zb[0:1, 0:TT], rows[0:1, 0:TT]), EV, 1)
            p.inc(v.tensor_copy(zb[0:1, TT:2 * TT], rows[0:1, TT:2 * TT]),
                  EV, 1)
            p.wait(te, EV)
            for tch in range(2):
                p.inc(te.matmul(ps_ln[:, tch * 512:(tch + 1) * 512], ones_row[:],
                                zb[0:1, tch * 512:(tch + 1) * 512], start=True,
                                stop=True), ET, 1)
                p.inc(te.matmul(ps_ln[:, 1024 + tch * 512:1536 + tch * 512],
                                ones_row[:],
                                zb[0:1, TT + tch * 512:TT + (tch + 1) * 512],
                                start=True, stop=True), ET, 1)
            p.wait(v, ET)
            p.inc(v.tensor_copy(mub[:], ps_ln[:, 0:TT]), EV, 1)
            p.inc(v.tensor_copy(rsb[:], ps_ln[:, 1024:1024 + TT]), EV, 1)
            for i in range(DT):
                sl = u_arena[:, i * TT:(i + 1) * TT]
                p.wait(v, EV)
                p.inc(v.tensor_tensor(zb2[:, 0:TT], sl, mub[:],
                                      ALU.subtract), EV, 1)
                p.wait(v, EV)
                p.inc(v.tensor_tensor(sl, zb2[:, 0:TT], rsb[:],
                                      ALU.mult), EV, 1)


    def xin_tile(i):
        p.wait(g, EV)
        p.wait(g, EA)
        slot = (i % 2) * TT
        p.inc(g.dma_start(out=wk[:, slot:slot + TT],
                          in_=inp["xin"][i * 128:(i + 1) * 128, :]), QG, 16)
        p.wait(v, QG)
        p.wait(sc, QG)
        return wk[:, slot:slot + TT]

    ln_to_u(xin_tile, ub, None)
    p.wait(g, EV)
    u_dst = sh_u[bass.ds(me_v, 1), :, :].rearrange("a (i q) t -> (a q) i t", q=128)
    p.inc(g.dma_start(out=u_dst, in_=ub[:, :].rearrange("q (i t) -> q i t", i=DT)),
          QG, 16)
    p.wait(g, QG)
    flag_send()
    flag_wait()

    if "u" in tap:
        p.wait(g, EV)
        for i in range(DT):
            p.inc(g.dma_start(out=tap["u"][i * 128:(i + 1) * 128, :],
                              in_=ub[:, i * TT:(i + 1) * TT]), QG, 16)
        p.wait(g, QG)

    if upto == "u":
        finish()
        return nc


    p.inc(g.dma_start(out=wpj[:, 0:DT * 112].rearrange("q (k r) -> q k r", k=DT),
                      in_=inp["wprj_cat"].rearrange("(k q) r -> q k r", q=128)),
          QG, 16)
    p.inc(g.dma_start(out=wouT[:], in_=inp["wdt_outT_own"]), QG, 16)

    with nc.psum_tensor("ps_pr", [128, 2048], F32) as ps_pr:
        for th in range(2):
            for i in range(DT):
                sl = (i % 2) * TT
                p.wait(g, ET)
                p.inc(g.dma_start(out=zb[:, sl:sl + TT],
                                  in_=sh_u[th, i * 128:(i + 1) * 128, :]), QG, 16)
                p.wait(te, QG)
                first, last = (i == 0), (i == DT - 1)
                for tc in range(2):
                    cc = slice(sl + tc * 512, sl + (tc + 1) * 512)
                    oc = slice(th * 1024 + tc * 512, th * 1024 + (tc + 1) * 512)
                    p.inc(te.matmul(ps_pr[0:112, oc],
                                    wpj[:, i * 112:(i + 1) * 112], zb[:, cc],
                                    start=first, stop=last), ET, 1)
        p.wait(v, ET)
        p.inc(v.tensor_copy(d64b[:], ps_pr[0:64, :]), EV, 1)
        p.inc(v.tensor_copy(brawb[:], ps_pr[64:80, :]), EV, 1)
        p.inc(v.tensor_copy(crawb[:], ps_pr[96:112, :]), EV, 1)
        p.wait(te, EV)

        for ht in range(8):
            p.wait(te, EV)
            p.wait(te, EA)
            for tc in range(4):
                cc = slice(tc * 512, (tc + 1) * 512)
                p.inc(te.matmul(ps_pr[:, cc], wouT[:, ht * 128:(ht + 1) * 128],
                                d64b[:, cc], start=True, stop=True), ET, 1)
            for th2 in range(2):
                hc = slice(th2 * TT, (th2 + 1) * TT)
                p.wait(sc, ET)
                p.wait(sc, EV)
                p.inc(sc.activation(out=wk[:, 0:TT], in_=ps_pr[:, hc], func=AF.Abs,
                                    bias=bdt[:, ht:ht + 1]), EA, 1)
                p.wait(sc, EA)
                p.inc(sc.activation(out=wk[:, TT:2 * TT], in_=wk[:, 0:TT],
                                    func=AF.Exp, scale=-1.0), EA, 1)
                p.wait(sc, EA)
                p.inc(sc.activation(out=wk[:, 0:TT], in_=wk[:, TT:2 * TT],
                                    func=AF.Ln, bias=1.0), EA, 1)
                p.wait(v, EA)
                p.inc(v.tensor_scalar(wk2[:, 0:TT], ps_pr[:, hc],
                                      bdt[:, ht:ht + 1], 0.0, ALU.add, ALU.max),
                      EV, 1)
                p.wait(v, EV)
                p.inc(v.tensor_tensor(cb[:, ht * 2048 + th2 * TT:
                                         ht * 2048 + (th2 + 1) * TT],
                                      wk2[:, 0:TT], wk[:, 0:TT], ALU.add), EV, 1)

    if upto == "dt":
        if "dt" in tap:
            p.wait(g, EV)
            for ht in range(8):
                p.inc(g.dma_start(out=tap["dt"][ht * 128:(ht + 1) * 128, :],
                                  in_=cb[:, ht * 2048:(ht + 1) * 2048]), QG, 16)
        if "braw" in tap:
            p.wait(g, EV)
            p.inc(g.dma_start(out=tap["braw"][0:16, :], in_=brawb[:]), QG, 16)
            p.inc(g.dma_start(out=tap["braw"][16:32, :], in_=crawb[:]), QG, 16)
        p.wait(g, QG)
        finish()
        return nc

    A_own = A_a
    ubb = xr[:, :].bitcast(BF16)
    with (nc.psum_tensor("ps_y", [128, 1024], F32) as ps_y,
          nc.psum_tensor("ps_w", [128, 1024], F32) as ps_w):
        for th in range(2):
            p.wait(g, EV)
            p.wait(g, EA)
            src = sh_u[th, :, :][bass.ds(me_v * 1024, 1024), :].rearrange(
                "(i q) t -> q i t", q=128)
            p.inc(g.dma_start(out=ar[:, th * 8 * TT:(th + 1) * 8 * TT].rearrange(
                "q (i t) -> q i t", i=8), in_=src), QG, 16)
            for n in range(NA):
                for tc in range(2):
                    cc = slice(th * TT + tc * 512, th * TT + (tc + 1) * 512)
                    p.wait(te, EA)
                    p.wait(te, EV)
                    p.inc(te.matmul(ps_w[:, tc * 512:(tc + 1) * 512],
                                    ident64[0:16, n:n + 1].broadcast_to([16, 128]),
                                    brawb[:, cc], start=True, stop=True), ET, 1)
                    p.inc(te.matmul(ps_w[:, 512 + tc * 512:512 + (tc + 1) * 512]
                                    if False else ps_y[:, tc * 512:(tc + 1) * 512],
                                    ident64[0:16, n:n + 1].broadcast_to([16, 128]),
                                    crawb[:, cc], start=True, stop=True), ET, 1)
                p.wait(sc, ET)
                p.inc(sc.copy(ubb[:, n * TT:(n + 1) * TT], ps_w[:, 0:TT]), EA, 1)
                p.inc(sc.copy(ubb[:, 16384 + n * TT:16384 + (n + 1) * TT],
                              ps_y[:, 0:TT]), EA, 1)
            p.wait(g, QG)
            for ht in range(8):
                dtsl = cb[:, ht * 2048 + th * TT:ht * 2048 + (th + 1) * TT]
                ulsl = ar[:, th * 8 * TT + ht * TT:th * 8 * TT + (ht + 1) * TT]
                p.wait(v, QG)
                p.wait(v, EA)
                p.inc(v.tensor_tensor(zb[:, 0:TT], dtsl, ulsl, ALU.mult), EV, 1)
                for n in range(NA):
                    col = ht * 16 + n
                    p.wait(sc, EV)
                    p.wait(sc, ET)
                    p.inc(sc.activation(out=wk[:, 0:TT], in_=dtsl, func=AF.Exp,
                                        scale=A_own[:, col:col + 1]), EA, 1)
                    p.wait(v, EA)
                    p.inc(v.tensor_tensor(zb[:, TT:2 * TT], zb[:, 0:TT],
                                          ubb[:, n * TT:(n + 1) * TT], ALU.mult),
                          EV, 1)
                    p.wait(v, EV)
                    if th == 0:
                        p.inc(v.tensor_tensor_scan(zb2[:, TT:2 * TT],
                                                   wk[:, 0:TT], zb[:, TT:2 * TT],
                                                   0.0, ALU.mult, ALU.add), EV, 1)
                    else:
                        p.inc(v.tensor_tensor_scan(zb2[:, TT:2 * TT],
                                                   wk[:, 0:TT], zb[:, TT:2 * TT],
                                                   st_a[:, col:col + 1],
                                                   ALU.mult, ALU.add), EV, 1)
                    p.wait(v, EV)
                    if th == 0:
                        p.inc(v.tensor_copy(st_a[:, col:col + 1],
                                            zb2[:, 2 * TT - 1:2 * TT]), EV, 1)
                    p.inc(v.tensor_tensor(zb2[:, 0:TT], zb2[:, TT:2 * TT],
                                          ubb[:, 16384 + n * TT:(n + 1) * TT
                                              + 16384], ALU.mult), EV, 1)
                    p.wait(te, EV)
                    for tc in range(2):
                        cc = slice(tc * 512, (tc + 1) * 512)
                        p.inc(te.matmul(ps_y[:, cc], identB[:], zb2[:, 0:TT][:, cc],
                                        start=(n == 0), stop=(n == NA - 1)), ET, 1)
                p.wait(v, ET)
                p.inc(v.scalar_tensor_tensor(
                    ub[:, ht * 2048 + th * TT:ht * 2048 + (th + 1) * TT], ulsl,
                    Da[:, ht:ht + 1], ps_y[:], ALU.mult, ALU.add), EV, 1)

    p.wait(g, EV)
    ya_dst = sh_ya[bass.ds(me_v, 1), :, :].rearrange("a (i q) t -> (a q) i t", q=128)
    p.inc(g.dma_start(out=ya_dst, in_=ub[:, :].rearrange("q (i t) -> q i t", i=8)),
          QG, 16)
    p.wait(g, QG)
    flag_send()
    flag_wait()

    for sl_i in range(2):
        src = sh_ya[sl_i, :, :][:, bass.ds(me_v * 1024, 1024)].rearrange(
            "(i q) t -> q i t", q=128)
        p.inc(g.dma_start(out=ar[:, sl_i * 8 * TT:(sl_i + 1) * 8 * TT].rearrange(
            "q (i t) -> q i t", i=8), in_=src), QG, 16)
    for i in range(DT):
        sl = (i % 2) * TT
        p.wait(sy, EV)
        p.wait(sy, EA)
        p.inc(sy.dma_start(out=wk[:, sl:sl + TT],
                           in_=inp["xin"][i * 128:(i + 1) * 128, :]), QS, 16)
        p.wait(v, QS)
        p.wait(v, QG)
        p.inc(v.tensor_tensor(xr[:, i * TT:(i + 1) * TT], wk[:, sl:sl + TT],
                              ar[:, i * TT:(i + 1) * TT], ALU.add), EV, 1)

    if "x1" in tap:
        p.wait(sy, EV)
        for i in range(DT):
            p.inc(sy.dma_start(out=tap["x1"][i * 128:(i + 1) * 128, :],
                               in_=xr[:, i * TT:(i + 1) * TT]), QS, 16)
    if upto == "x1":
        finish()
        return nc

    def ffn_block(first):
        def xr_tile(i):
            return xr[:, i * TT:(i + 1) * TT]
        ln_to_u(xr_tile, ub, None)
        if first:
            flag_wait(expect=2)
        sfx = "a" if first else "b"
        with (nc.psum_tensor("ps_f0" + sfx, [128, 1024], F32) as f0,
              nc.psum_tensor("ps_f1" + sfx, [128, 1024], F32) as f1,
              nc.psum_tensor("ps_f2" + sfx, [128, 1024], F32) as f2,
              nc.psum_tensor("ps_f3" + sfx, [128, 1024], F32) as f3):
            ps4 = [f0, f1, f2, f3]
            for mb in range(10):
                buf = (mb % 2) * 8192
                mc = slice(mb * 512, (mb + 1) * 512)
                p.wait_lag(g, ET, 128 if mb > 0 else 0)
                for sl_i in range(2):
                    src = sh_q2fc[sl_i, :, mc].rearrange("(k q) m -> q k m", q=128)
                    p.inc(g.dma_start(
                        out=ar[:, buf + sl_i * 4096:buf + (sl_i + 1) * 4096]
                        .rearrange("q (k m) -> q k m", k=8), in_=src), QG, 16)
                p.wait(te, QG)
                p.wait(te, EV)
                p.wait(te, EA)
                for m in range(4):
                    for k in range(DT):
                        lhs = ar[:, buf + k * 512 + m * 128:
                                 buf + k * 512 + (m + 1) * 128]
                        for tch in range(2):
                            p.inc(te.matmul(
                                ps4[m][:, tch * 512:(tch + 1) * 512], lhs,
                                ub[:, k * TT + tch * 512:k * TT + (tch + 1) * 512],
                                start=(k == 0), stop=(k == DT - 1)), ET, 1)
                for m in range(4):
                    p.wait(v, ET)
                    p.wait(v, QG)
                    p.wait(v, EA)
                    p.inc(v.tensor_scalar_max(wk[:, 0:TT], ps4[m][:, 0:TT], 0.0),
                          EV, 1)
                    p.wait(sc, EV)
                    p.inc(sc.activation(out=zb[:, TT:2 * TT], in_=wk[:, 0:TT],
                                        func=AF.Square), EA, 1)
                    p.wait(g, EA)
                    p.inc(g.dma_start(out=h_dram[(mb * 4 + m) * 128:
                                                 (mb * 4 + m + 1) * 128, :],
                                      in_=zb[:, TT:2 * TT]), QG, 16)
        with (nc.psum_tensor("ps_p0" + sfx, [128, 1024], F32) as p0,
              nc.psum_tensor("ps_p1" + sfx, [128, 1024], F32) as p1,
              nc.psum_tensor("ps_p2" + sfx, [128, 1024], F32) as p2,
              nc.psum_tensor("ps_p3" + sfx, [128, 1024], F32) as p3):
            pp = [p0, p1, p2, p3]
            p.wait(g, QG)
            for m2b in range(4):
                mc = slice(m2b * 512, (m2b + 1) * 512)
                for k2 in range(40):
                    wbuf = (k2 % 2) * 1024
                    hbuf = 2048 + (k2 % 2) * 1024
                    sl_i, row = k2 // 20, (k2 % 20) * 128
                    p.wait_lag(g, ET, 8)
                    p.inc(g.dma_start(out=ar[:, wbuf:wbuf + 512],
                                      in_=sh_q2pj[sl_i, row:row + 128, mc]),
                          QG, 16)
                    p.inc(g.dma_start(out=ar[:, hbuf:hbuf + 1024],
                                      in_=h_dram[k2 * 128:(k2 + 1) * 128, :]),
                          QG, 16)
                    p.wait(te, QG)
                    if k2 == 0:
                        p.wait(te, EV)
                    for m2 in range(4):
                        lhs = ar[:, wbuf + m2 * 128:wbuf + (m2 + 1) * 128]
                        for tch in range(2):
                            p.inc(te.matmul(
                                pp[m2][:, tch * 512:(tch + 1) * 512], lhs,
                                ar[:, hbuf + tch * 512:hbuf + (tch + 1) * 512],
                                start=(k2 == 0), stop=(k2 == 39)), ET, 1)
                for m2 in range(4):
                    d_i = m2b * 4 + m2
                    p.wait(v, ET)
                    p.inc(v.scalar_tensor_tensor(
                        xr[:, d_i * TT:(d_i + 1) * TT], pp[m2][:, 0:TT],
                        sc_ffn[:, 0:1], xr[:, d_i * TT:(d_i + 1) * TT],
                        ALU.mult, ALU.add), EV, 1)

    ffn_block(first=True)
    if "rows2" in tap:
        p.wait(sy, EV)
        p.wait(sy, EA)
        p.inc(sy.dma_start(out=tap["rows2"], in_=wk[0:1, :]), QS, 16)
    if "u2" in tap:
        p.wait(g, EV)
        for i in range(DT):
            p.inc(g.dma_start(out=tap["u2"][i * 128:(i + 1) * 128, :],
                              in_=ub[:, i * TT:(i + 1) * TT]), QG, 16)
    if "hff" in tap:
        p.wait(g, QG)
        for i in range(8):
            p.inc(g.dma_start(out=wk[:, 0:TT], in_=h_dram[i * 128:(i + 1) * 128, :]), QG, 16)
            p.wait(g, QG)
            p.inc(g.dma_start(out=tap["hff"][i * 128:(i + 1) * 128, :], in_=wk[:, 0:TT]), QG, 16)
            p.wait(g, QG)
    if "x2" in tap:
        p.wait(sy, EV)
        for i in range(DT):
            p.inc(sy.dma_start(out=tap["x2"][i * 128:(i + 1) * 128, :],
                               in_=xr[:, i * TT:(i + 1) * TT]), QS, 16)
    if upto == "x2":
        finish()
        return nc

    def xr_tile2(i):
        return xr[:, i * TT:(i + 1) * TT]

    ln_to_u(xr_tile2, ub, None)
    if "u3d" in tap:
        p.wait(g, EV)
        for i in range(DT):
            p.inc(g.dma_start(out=tap["u3d"][i * 128:(i + 1) * 128, :],
                              in_=ub[:, i * TT:(i + 1) * TT]), QG, 16)
        p.wait(g, QG)

    with (nc.psum_tensor("ps_i0", [128, 1024], F32) as i0,
          nc.psum_tensor("ps_i1", [128, 1024], F32) as i1,
          nc.psum_tensor("ps_i2", [128, 1024], F32) as i2,
          nc.psum_tensor("ps_i3", [128, 1024], F32) as i3):
        ps4b = [i0, i1, i2, i3]
        for mb in range(17):
            m0 = mb * 512
            mw = min(8512, (mb + 1) * 512) - m0
            buf = (mb % 2) * 8192
            p.wait_lag(g, ET, 128 if mb > 0 else 0)
            p.inc(g.dma_start(
                out=ar[:, buf:buf + 16 * mw].rearrange("q (k m) -> q k m", k=DT),
                in_=inp["inpT"][:, m0:m0 + mw].rearrange("(k q) m -> q k m", q=128)),
                QG, 16)
            p.wait(te, QG)
            p.wait(te, EV)
            p.wait(te, EA)
            ntile = (mw + 127) // 128
            for m in range(ntile):
                mrows = min(mw - m * 128, 128)
                for k in range(DT):
                    lhs = ar[:, buf + k * mw + m * 128:buf + k * mw + m * 128 + mrows]
                    for tch in range(2):
                        p.inc(te.matmul(
                            ps4b[m][0:mrows, tch * 512:(tch + 1) * 512], lhs,
                            ub[:, k * TT + tch * 512:k * TT + (tch + 1) * 512],
                            start=(k == 0), stop=(k == DT - 1)), ET, 1)
            for m in range(ntile):
                mt = mb * 4 + m
                mrows = min(mw - m * 128, 128)
                if mt < 32:
                    p.wait(sc, ET)
                    p.wait(sc, QG)
                    p.inc(sc.activation(out=zb[:, 0:TT], in_=ps4b[m][:, 0:TT],
                                        func=AF.Silu), EA, 1)
                    p.wait(g, EA)
                    p.inc(g.dma_start(out=gate_dram[mt * 128:(mt + 1) * 128, :],
                                      in_=zb[:, 0:TT]), QG, 16)
                else:
                    row = (mt - 32) * 128
                    p.wait(v, ET)
                    p.wait(v, QG)
                    p.inc(v.tensor_copy(zb2[0:mrows, 0:TT], ps4b[m][0:mrows, 0:TT]),
                          EV, 1)
                    p.wait(g, EV)
                    p.inc(g.dma_start(out=hbc_loc[row:row + mrows, :],
                                      in_=zb2[0:mrows, 0:TT]), QG, 16)
    if "sg2" in tap:
        p.wait(g, QG)
        for j in range(16):
            p.inc(g.dma_start(out=zb[:, 0:TT], in_=gate_dram[j * 128:(j + 1) * 128, :]), QG, 16)
            p.wait(g, QG)
            p.inc(g.dma_start(out=tap["sg2"][j * 128:(j + 1) * 128, :], in_=zb[:, 0:TT]), QG, 16)
            p.wait(g, QG)
    if "hbcl" in tap:
        p.wait(g, QG)
        for j in range(6):
            p.inc(g.dma_start(out=zb[:, 0:TT], in_=hbc_loc[j * 128:(j + 1) * 128, :]), QG, 16)
            p.wait(g, QG)
            p.inc(g.dma_start(out=tap["hbcl"][j * 128:(j + 1) * 128, :], in_=zb[:, 0:TT]), QG, 16)
            p.wait(g, QG)
    p.wait(g, QG)
    p.inc(g.dma_start(out=sh_hbc[bass.ds(me_v, 1), :, :], in_=hbc_loc), QG, 16)
    p.wait(g, QG)
    flag_send()
    flag_wait()

    p.inc(g.dma_start(out=d64b[0:32, :].rearrange("q (s t) -> q s t", s=2),
                      in_=sh_hbc[:, :, :][:, bass.ds(me_v * 32 + 4352, 32), :]
                      .rearrange("s q t -> q s t")), QG, 16)
    p.inc(g.dma_start(out=zconv_dram,
                      in_=sh_hbc[:, :, :][:, bass.ds(me_v * 2048, 2048), :]),
          QG, 16)
    adt = ar[0:64, 10240:14336].bitcast(F32)
    identF = ar[:, 14336:14592].bitcast(F32)
    BT0 = ar[:, 14592:14720]
    BT1 = ar[:, 14720:14848]
    zsv = ar[:, 14848:14896]
    Ssb = ar[:, 8192:10240]
    p.wait(v, QG)
    p.inc(v.tensor_copy(identF, identB[:]), EV, 1)
    p.inc(v.memset(Ssb, 0.0), EV, 1)
    for th2 in range(2):
        hc = slice(th2 * TT, (th2 + 1) * TT)
        p.wait(sc, EV)
        p.inc(sc.activation(out=wk[0:32, 0:TT], in_=d64b[0:32, hc], func=AF.Abs,
                            bias=dtb_col[:, 0:1]), EA, 1)
        p.wait(sc, EA)
        p.inc(sc.activation(out=wk[0:32, TT:2 * TT], in_=wk[0:32, 0:TT],
                            func=AF.Exp, scale=-1.0), EA, 1)
        p.wait(sc, EA)
        p.inc(sc.activation(out=wk[0:32, 0:TT], in_=wk[0:32, TT:2 * TT],
                            func=AF.Ln, bias=1.0), EA, 1)
        p.wait(v, EA)
        p.inc(v.tensor_scalar(wk2[0:32, 0:TT], d64b[0:32, hc], dtb_col[:, 0:1],
                              0.0, ALU.add, ALU.max), EV, 1)
        p.wait(v, EV)
        p.inc(v.tensor_tensor(wk2[0:32, 0:TT], wk2[0:32, 0:TT], wk[0:32, 0:TT],
                              ALU.add), EV, 1)
        p.wait(v, EV)
        p.inc(v.tensor_scalar_mul(adt[0:32, hc], wk2[0:32, 0:TT],
                                  Ab_col[:, 0:1]), EV, 1)
        p.inc(v.tensor_copy(d64b[0:32, hc], wk2[0:32, 0:TT]), EV, 1)

    p.wait(g, QG)
    p.wait(g, EV)
    p.inc(g.memset(ar2[0:32, 2048:4096], 1.0), EP, 1)
    p.wait(g, EP)
    p.inc(g.affine_select(ar2[0:32, 2048:4096].rearrange("q (a b) -> q a b", a=32),
                          ar2[0:32, 2048:4096].rearrange("q (a b) -> q a b", a=32),
                          [[1, 32], [0, 64]], ALU.is_equal, 0.0,
                          base=0, channel_multiplier=-1), EP, 1)

    p.wait(g, EV)
    p.wait(g, EA)
    p.inc(g.memset(zb[:, 1408:1920], 0.0), EP, 1)
    p.wait(g, EP)
    for sch in range(2):
        p.inc(g.affine_select(zb[:, 1408 + sch * 256:1664 + sch * 256],
                              zb[:, 1408 + sch * 256:1664 + sch * 256],
                              [[1, LCH]], ALU.is_ge, -1e30,
                              base=-(sch * 128), channel_multiplier=-1), EP, 1)

    with (nc.psum_tensor("ps_gm", [128, 1024], F32) as ps_gm,
          nc.psum_tensor("ps_y2", [128, 512], F32) as ps_y2,
          nc.psum_tensor("ps_sd", [128, 512], F32) as ps_sd,
          nc.psum_tensor("ps_t", [128, 512], F32) as ps_t,
          nc.psum_tensor("ps_tb", [128, 1024], BF16) as ps_tb,
          nc.psum_tensor("ps_x", [128, 512], F32) as ps_x,
          nc.psum_tensor("ps_m", [128, 512], F32) as ps_m):
        for th in range(2):
            p.wait(g, EV)
            p.wait(g, EA)
            p.wait(g, ET)
            src = zconv_dram[th, :, :].rearrange("(i q) t -> q i t", q=128)
            p.inc(g.dma_start(out=ub[:, 0:16 * TT].rearrange(
                "q (i t) -> q i t", i=16), in_=src), QG, 16)
            def conv_tile(zsrc, wj, dst, edge_src):
                p.wait(v, EA)
                p.inc(v.memset(wk[:, 0:3], 0.0), EV, 1)
                p.inc(v.tensor_single_scalar(wk[:, 3:TT], zsrc[:, 0:TT - 3],
                                             convw[:, wj * 4:wj * 4 + 1],
                                             ALU.mult), EV, 1)
                for kk in range(1, 4):
                    p.wait(v, EV)
                    p.inc(v.scalar_tensor_tensor(
                        wk[:, 3 - kk:TT], zsrc[:, 0:TT - 3 + kk],
                        convw[:, wj * 4 + kk:wj * 4 + kk + 1],
                        wk[:, 3 - kk:TT], ALU.mult, ALU.add), EV, 1)
                if edge_src is not None:
                    for kk in range(3):
                        p.wait(v, EV)
                        p.inc(v.scalar_tensor_tensor(
                            wk[:, 0:3 - kk], edge_src[:, kk:3],
                            convw[:, wj * 4 + kk:wj * 4 + kk + 1],
                            wk[:, 0:3 - kk], ALU.mult, ALU.add), EV, 1)
                p.wait(sc, EV)
                p.inc(sc.activation(out=dst, in_=wk[:, 0:TT], func=AF.Silu,
                                    bias=convb[:, wj:wj + 1]), EA, 1)

            p.wait(v, QG)
            for j in range(16):
                edge = zsv[:, j * 3:j * 3 + 3] if th == 1 else None
                conv_tile(ub[:, j * TT:(j + 1) * TT], j,
                          cb[:, j * TT:(j + 1) * TT], edge)
            for j in (16, 17):
                base = 4096 + (j - 16) * 128
                p.wait(g, EV)
                p.wait(g, ET)
                p.inc(g.dma_start(out=zb[:, 0:TT], in_=sh_hbc[th, base:base + 128, :]),
                      QG, 16)
                if th == 1:
                    p.inc(g.dma_start(out=zb[:, TT:TT + 3],
                                      in_=sh_hbc[0, base:base + 128, TT - 3:TT]),
                          QG, 16)
                p.wait(v, QG)
                edge = zb[:, TT:TT + 3] if th == 1 else None
                conv_tile(zb[:, 0:TT], j, ar2[:, (j - 16) * TT:(j - 15) * TT], edge)
            if th == 0:
                for j in range(16):
                    p.wait(v, EA)
                    p.inc(v.tensor_copy(zsv[:, j * 3:j * 3 + 3],
                                        ub[:, j * TT + TT - 3:(j + 1) * TT]), EV, 1)
            if "bc" in tap:
                p.wait(g, EA)
                p.inc(g.dma_start(out=tap["bc"][:, th * TT:(th + 1) * TT],
                                  in_=ar2[:, 0:TT]), QG, 16)
                p.inc(g.dma_start(out=tap["bc"][:, 2048 + th * TT:2048 + (th + 1) * TT]
                                  if False else
                                  tap["bc"][:, 2 * TT + th * TT:2 * TT + (th + 1) * TT],
                                  in_=ar2[:, TT:2 * TT]), QG, 16)
                p.wait(g, QG)
            if "hsd" in tap:
                p.wait(g, EA)
                for j in range(16):
                    p.inc(g.dma_start(out=tap["hsd"][j * 128:(j + 1) * 128,
                                                     th * TT:(th + 1) * TT],
                                      in_=cb[:, j * TT:(j + 1) * TT]), QG, 16)
                p.wait(g, QG)
            for j in range(16):
                lhs = ar2[0:32, 2048 + 2 * j * 64:2048 + 2 * j * 64 + 128]
                for tc in range(2):
                    p.wait(te, EV)
                    p.wait(te, EA)
                    p.inc(te.matmul(ps_m[:, 0:512], lhs,
                                    d64b[0:32, th * TT + tc * 512:
                                         th * TT + (tc + 1) * 512],
                                    start=True, stop=True), ET, 1)
                    p.wait(v, ET)
                    p.inc(v.tensor_tensor(
                        ub[:, j * TT + tc * 512:j * TT + (tc + 1) * 512],
                        cb[:, j * TT + tc * 512:j * TT + (tc + 1) * 512],
                        ps_m[:, 0:512], ALU.mult), EV, 1)
            for cl in range(4):
                c = th * 4 + cl
                for j in range(16):
                    for sch in range(2):
                        p.wait(te, EV)
                        p.wait(te, EA)
                        p.inc(te.transpose(
                            ps_tb[:, 0:128],
                            ub[:, j * TT + cl * LCH + sch * 128:
                               j * TT + cl * LCH + sch * 128 + 128],
                            identB[:]), ET, 1)
                        p.wait(sc, ET)
                        p.inc(sc.copy(ar[:, sch * 2048 + j * 128:
                                         sch * 2048 + (j + 1) * 128],
                                      ps_tb[:, 0:128]), EA, 1)
                acum = wk2[0:32, 128:384]
                ones_d0 = ones_colF[0:32, 0:1].broadcast_to([32, LCH])
                p.wait(v, EA)
                p.wait(v, ET)
                p.inc(v.tensor_tensor_scan(acum, ones_d0,
                                           adt[0:32, th * TT + cl * LCH:
                                               th * TT + (cl + 1) * LCH],
                                           0.0, ALU.mult, ALU.add), EV, 1)
                p.wait(g, EV)
                p.inc(g.dma_start(out=acum_dram[c, :, :], in_=acum), QG, 16)
                p.wait(g, QG)
                for sch in range(2):
                    p.wait(te, EV)
                    p.wait(te, EA)
                    p.inc(te.transpose(ps_t[:, 0:32],
                                       acum[:, sch * 128:(sch + 1) * 128],
                                       identF[0:32, 0:32]), ET, 1)
                    p.wait(v, ET)
                    p.inc(v.tensor_copy(wk2[:, 384 + sch * 32:416 + sch * 32],
                                        ps_t[:, 0:32]), EV, 1)
                p.wait(sc, EV)
                p.inc(sc.activation(out=zb[0:32, 512:768], in_=acum, func=AF.Exp),
                      EA, 1)
                p.wait(te, EV)
                p.wait(te, EA)
                p.inc(te.matmul(ps_sd[0:1, 0:32], wk2[0:32, 383:384],
                                identF[0:32, 0:32], start=True, stop=True), ET, 1)
                p.wait(sc, ET)
                p.inc(sc.copy(wk2[0:1, 448:480], ps_sd[0:1, 0:32]), EA, 1)
                p.wait(te, EA)
                p.inc(te.matmul(ps_sd[:, 64:96], ones_rowF[:], wk2[0:1, 448:480],
                                start=True, stop=True), ET, 1)
                for sch in range(2):
                    p.wait(v, ET)
                    p.inc(v.tensor_tensor(wk2[:, 512 + sch * 32:544 + sch * 32],
                                          ps_sd[:, 64:96],
                                          wk2[:, 384 + sch * 32:416 + sch * 32],
                                          ALU.subtract), EV, 1)
                    p.wait(sc, EV)
                    p.inc(sc.activation(out=zb[:, 1920 + sch * 32:1952 + sch * 32],
                                        in_=wk2[:, 512 + sch * 32:544 + sch * 32],
                                        func=AF.Exp), EA, 1)
                p.wait(sc, ET)
                p.inc(sc.activation(out=zb[:, 1984:2016], in_=ps_sd[:, 64:96],
                                    func=AF.Exp), EA, 1)
                if "dbg4" in tap and c == 0:
                    p.wait(g, EA)
                    p.inc(g.dma_start(out=tap["dbg4"][0:128, 0:64], in_=wk2[:, 384:448]), QG, 16)
                    p.inc(g.dma_start(out=tap["dbg4"][0:128, 64:128], in_=wk2[:, 512:576]), QG, 16)
                    p.inc(g.dma_start(out=tap["dbg4"][0:128, 128:224], in_=zb[:, 1920:2016]), QG, 16)
                    p.wait(g, QG)
                for grp in range(8):
                    bb_flush()
                    h4 = slice(grp * 4, grp * 4 + 4)
                    for sch in range(2):
                        src = acum_dram[c, grp * 4:grp * 4 + 4, :].rearrange(
                            "h l -> (h l)")[None, :].partition_broadcast(128)
                        p.wait(g, EV)
                        p.wait(g, EP)
                        p.wait(g, EA)
                        p.inc(g.dma_start(out=wk[:, 0:1024], in_=src), QG, 16)
                        acT = wk2[:, 384 + sch * 32:416 + sch * 32]
                        in1 = acT[:, h4].unsqueeze(2).broadcast_to([128, 4, LCH])
                        p.wait(v, QG)
                        p.wait(v, EA)
                        p.inc(v.tensor_tensor(
                            wk[:, 1024:2048].rearrange("q (h l) -> q h l", h=4),
                            wk[:, 0:1024].rearrange("q (h l) -> q h l", h=4),
                            in1, ALU.subtract), EV, 1)
                        p.wait(g, EV)
                        mskap = zb[:, 1408 + sch * 256:1664 + sch * 256] \
                            .unsqueeze(1).broadcast_to([128, 4, LCH])
                        p.inc(g.tensor_tensor(
                            wk[:, 1024:2048].rearrange("q (h l) -> q h l", h=4),
                            wk[:, 1024:2048].rearrange("q (h l) -> q h l", h=4),
                            mskap, ALU.add), EP, 1)
                        p.wait(sc, EP)
                        p.inc(sc.activation(out=wk[:, 0:1024],
                                            in_=wk[:, 1024:2048], func=AF.Exp),
                              EA, 1)
                        if "dbg" in tap and c == 2 and grp == 1:
                            p.wait(g, EA)
                            p.inc(g.dma_start(out=tap["dbg"][sch * 128:sch * 128 + 128, :],
                                              in_=wk[:, 1024:2048]), QG, 16)
                            p.inc(g.dma_start(out=tap["dbg"][256 + sch * 128:
                                                             384 + sch * 128, :],
                                              in_=wk[:, 0:1024]), QG, 16)
                            p.wait(g, QG)
                        p.wait(te, EV)
                        p.wait(te, EA)
                        for hh in range(4):
                            p.inc(te.matmul(
                                ps_gm[:, hh * 256:(hh + 1) * 256],
                                ar2[:, cl * LCH + sch * 128:
                                    cl * LCH + sch * 128 + 128],
                                ar2[:, TT + cl * LCH:TT + (cl + 1) * LCH],
                                start=True, stop=True), ET, 1)
                        p.wait(v, ET)
                        p.wait(v, EA)
                        p.inc(v.tensor_tensor(zb2[:, sch * 1024:(sch + 1) * 1024],
                                              ps_gm[:, 0:1024], wk[:, 0:1024],
                                              ALU.mult), EV, 1)

                    if "dbg2" in tap and c == 1 and grp == 0:
                        p.wait(g, EV)
                        p.inc(g.dma_start(out=tap["dbg2"][0:128, 0:1024], in_=zb2[:, 0:1024]), QG, 16)
                        p.inc(g.dma_start(out=tap["dbg2"][128:256, 0:1024], in_=zb2[:, 1024:2048]), QG, 16)
                        p.inc(g.dma_start(out=tap["dbg2"][256:384, 0:2048], in_=Ssb), QG, 16)
                        p.inc(g.dma_start(out=tap["dbg2"][384:416, 0:256], in_=zb[0:32, 512:768]), QG, 16)
                        p.wait(g, QG)
                    for hh in range(4):
                        hloc = grp * 4 + hh
                        lhs1 = ident64[0:32, hloc:hloc + 1].broadcast_to([32, 128])
                        p.wait(te, EV)
                        p.wait(te, EA)
                        p.inc(te.matmul(ps_sd[:, 0:256], lhs1, zb[0:32, 512:768],
                                        start=True, stop=True), ET, 1)
                        csd = zb[:, (hh % 2) * 256:(hh % 2) * 256 + 256]
                        p.wait(v, ET)
                        p.inc(v.tensor_tensor(csd, ar2[:, TT + cl * LCH:
                                                       TT + (cl + 1) * LCH],
                                              ps_sd[:, 0:256], ALU.mult), EV, 1)
                        pcol = (hh // 2) * 256
                        prow = 64 * (hh % 2)
                        tp = (0, 64) if hh % 2 else None
                        p.wait(te, EV)
                        for sch in range(2):
                            p.inc(te.matmul(
                                ps_y2[prow:prow + 64, pcol:pcol + 256],
                                ar[:, sch * 2048 + hloc * 64:
                                   sch * 2048 + hloc * 64 + 64],
                                zb2[:, sch * 1024 + hh * 256:
                                    sch * 1024 + (hh + 1) * 256],
                                start=(sch == 0), stop=False,
                                tile_position=tp), ET, 1)
                        p.inc(te.matmul(
                            ps_y2[prow:prow + 64, pcol:pcol + 256],
                            Ssb[:, hloc * 64:(hloc + 1) * 64], csd,
                            start=False, stop=True, tile_position=tp), ET, 1)
                        if hh % 2 == 1:
                            pr = grp * 2 + hh // 2
                            p.wait(v, ET)
                            p.wait(v, QG)
                            p.inc(v.scalar_tensor_tensor(
                                zb[:, 1120:1376],
                                cb[:, pr * TT + cl * LCH:pr * TT + (cl + 1) * LCH],
                                Dbr[:, pr:pr + 1],
                                ps_y2[:, pcol:pcol + 256], ALU.mult, ALU.add),
                                EV, 1)
                            p.wait(g, EV)
                            p.inc(g.dma_start(
                                out=yb_loc2[pr * 128:(pr + 1) * 128,
                                            th * TT + cl * LCH:
                                            th * TT + (cl + 1) * LCH],
                                in_=zb[:, 1120:1376]), QG, 16)
                for sch in range(2):
                    p.wait(te, EV)
                    p.wait(te, EA)
                    p.inc(te.transpose(ps_tb[:, 0:128],
                                       ar2[:, cl * LCH + sch * 128:
                                           cl * LCH + sch * 128 + 128],
                                       identB[:]), ET, 1)
                    p.wait(sc, ET)
                    p.inc(sc.copy(BT0 if sch == 0 else BT1, ps_tb[:, 0:128]), EA, 1)
                    dec = zb[:, 1920 + sch * 32:1952 + sch * 32]
                    in1b = dec.unsqueeze(2).broadcast_to([128, 32, 64])
                    p.wait(v, EA)
                    p.inc(v.tensor_tensor(
                        ar[:, 4096 + sch * 2048:4096 + (sch + 1) * 2048]
                        .rearrange("q (h pp) -> q h pp", h=32),
                        ar[:, sch * 2048:(sch + 1) * 2048]
                        .rearrange("q (h pp) -> q h pp", h=32),
                        in1b, ALU.mult), EV, 1)
                if "dbg3" in tap and c == 0:
                    p.wait(g, EV)
                    p.wait(g, EA)
                    p.inc(g.dma_start(out=tap["dbg3"][0:128, 0:32], in_=zb[:, 1920:1952]), QG, 16)
                    p.inc(g.dma_start(out=tap["dbg3"][0:128, 32:64], in_=zb[:, 1952:1984]), QG, 16)
                    p.inc(g.dma_start(out=tap["dbg3"][0:128, 64:96], in_=zb[:, 1984:2016]), QG, 16)
                    p.inc(g.dma_start(out=tap["dbg3"][128:256, 0:2048], in_=ar[:, 4096:6144]), QG, 16)
                    p.inc(g.dma_start(out=tap["dbg3"][256:384, 0:2048], in_=ar[:, 6144:8192]), QG, 16)
                    p.inc(g.dma_start(out=tap["dbg3"][384:512, 0:128], in_=BT0), QG, 16)
                    p.inc(g.dma_start(out=tap["dbg3"][384:512, 128:256], in_=BT1), QG, 16)
                    p.wait(g, QG)
                drep = zb[:, 1984:2016]
                for sl4 in range(4):
                    scol = slice(sl4 * 512, (sl4 + 1) * 512)
                    p.wait(te, EV)
                    p.wait(te, EA)
                    p.inc(te.matmul(ps_x[:, 0:512], BT0,
                                    ar[:, 4096:6144][:, scol],
                                    start=True, stop=False), ET, 1)
                    p.inc(te.matmul(ps_x[:, 0:512], BT1,
                                    ar[:, 6144:8192][:, scol],
                                    start=False, stop=True), ET, 1)
                    dr1 = drep[:, sl4 * 8:(sl4 + 1) * 8].unsqueeze(2) \
                        .broadcast_to([128, 8, 64])
                    p.wait(v, ET)
                    p.inc(v.tensor_tensor(
                        zb2[:, 0:512].rearrange("q (h pp) -> q h pp", h=8),
                        Ssb[:, scol].rearrange("q (h pp) -> q h pp", h=8),
                        dr1, ALU.mult), EV, 1)
                    p.wait(v, EV)
                    p.inc(v.tensor_tensor(Ssb[:, scol], zb2[:, 0:512],
                                          ps_x[:, 0:512], ALU.add), EV, 1)

    if "adt" in tap:
        p.wait(g, EV)
        p.inc(g.dma_start(out=tap["adt"][0:32, :], in_=adt[0:32, :]), QG, 16)
    if "acum" in tap:
        p.wait(g, QG)
        p.inc(g.dma_start(out=tap["acum"],
                          in_=acum_dram.rearrange("c h l -> (c h) l")), QG, 16)
    if "ybl" in tap:
        p.wait(g, QG)
        for j in range(16):
            p.inc(g.dma_start(out=zb[:, 0:2048],
                              in_=yb_loc2[j * 128:(j + 1) * 128, :]), QG, 16)
            p.wait(g, QG)
            p.inc(g.dma_start(out=tap["ybl"][j * 128:(j + 1) * 128, :],
                              in_=zb[:, 0:2048]), QG, 16)
            p.wait(g, QG)
    if "sg3" in tap:
        p.wait(g, QG)
        for j in range(16):
            p.inc(g.dma_start(out=zb[:, 0:TT], in_=gate_dram[j * 128:(j + 1) * 128, :]), QG, 16)
            p.wait(g, QG)
            p.inc(g.dma_start(out=tap["sg3"][j * 128:(j + 1) * 128, :], in_=zb[:, 0:TT]), QG, 16)
            p.wait(g, QG)
    p.wait(g, QG)
    p.inc(g.dma_start(out=sh_yb[bass.ds(me_v, 1), :, :], in_=yb_loc2), QG, 16)
    p.wait(g, QG)
    flag_send()
    flag_wait()

    p.inc(g.dma_start(out=ytok_dram, in_=sh_yb[:, :, bass.ds(me_v * TT, TT)]),
          QG, 16)
    p.wait(sy, QG)
    p.wait(sy, EV)
    for sl_i in range(2):
        dstar = ub if sl_i == 0 else ar
        src = ytok_dram[sl_i, :, :].rearrange("(i q) t -> q i t", q=128)
        p.inc(sy.dma_start(out=dstar[:, 0:16 * TT].rearrange(
            "q (i t) -> q i t", i=16), in_=src), QS, 16)
    p.wait(g, QS)

    if "yraw" in tap:
        p.wait(g, QS)
        p.wait(g, QG)
        for j in range(32):
            p.inc(g.dma_start(out=tap["yraw"][j * 128:(j + 1) * 128, :],
                              in_=ytile(j) if False else
                              (ub if j < 16 else ar)[:, (j % 16) * TT:(j % 16 + 1) * TT]),
                  QG, 16)
        p.wait(g, QG)
    if "argate" in tap:
        p.wait(g, QG)
        for j in range(2):
            p.inc(g.dma_start(out=tap["argate"][j * 128:(j + 1) * 128, :],
                              in_=ar[:, j * TT:(j + 1) * TT]), QG, 16)
            p.inc(g.dma_start(out=tap["argate"][256 + j * 128:256 + (j + 1) * 128, :],
                              in_=ub[:, j * TT:(j + 1) * TT]), QG, 16)
        p.wait(g, QG)
    if "ytk" in tap:
        p.wait(g, QG)
        for sl_i in range(2):
            for j in range(2):
                p.inc(g.dma_start(out=zb[:, 0:TT],
                                  in_=ytok_dram[sl_i, j * 128:(j + 1) * 128, :]), QG, 16)
                p.wait(g, QG)
                p.inc(g.dma_start(out=tap["ytk"][(sl_i * 2 + j) * 128:
                                                 (sl_i * 2 + j + 1) * 128, :],
                                  in_=zb[:, 0:TT]), QG, 16)
                p.wait(g, QG)

    def ytile(i):
        return (ub if i < 16 else ar)[:, (i % 16) * TT:(i % 16 + 1) * TT]

    with nc.psum_tensor("ps_r", [128, 2048], F32) as ps_r:
        for i in range(32):
            p.wait(g, EV)
            p.inc(g.dma_start(out=zb[:, 0:TT],
                              in_=gate_dram[i * 128:(i + 1) * 128, :]), QG, 16)
            p.wait(v, QG)
            p.inc(v.tensor_tensor(ytile(i), ytile(i), zb[:, 0:TT], ALU.mult),
                  EV, 1)
            p.wait(sc, EV)
            p.wait(sc, ET)
            p.inc(sc.activation(out=zb2[:, 0:TT], in_=ytile(i), func=AF.Square),
                  EA, 1)
            p.wait(te, EA)
            for tch in range(2):
                p.inc(te.matmul(ps_r[0:1, tch * 512:(tch + 1) * 512], ones_col[:],
                                zb2[:, tch * 512:(tch + 1) * 512],
                                start=(i == 0), stop=(i == 31)), ET, 1)
        p.wait(v, ET)
        p.inc(v.tensor_scalar_mul(wk[0:1, 0:TT], ps_r[0:1, 0:TT], 1.0 / INTER),
              EV, 1)
        p.wait(v, EV)
        p.inc(v.tensor_scalar_add(wk[0:1, 0:TT], wk[0:1, 0:TT], EPS), EV, 1)
        p.wait(v, EV)
        p.inc(v.reciprocal(wk[0:1, TT:2 * TT], wk[0:1, 0:TT]), EV, 1)
        p.wait(sc, EV)
        p.inc(sc.activation(out=wk[0:1, 0:TT], in_=wk[0:1, TT:2 * TT],
                            func=AF.Sqrt), EA, 1)
        p.wait(v, EA)
        p.inc(v.tensor_copy(zb[0:1, 0:TT], wk[0:1, 0:TT]), EV, 1)
        p.wait(te, EV)
        for tch in range(2):
            p.inc(te.matmul(ps_r[:, 1024 + tch * 512:1024 + (tch + 1) * 512],
                            ones_row[:], zb[0:1, tch * 512:(tch + 1) * 512],
                            start=True, stop=True), ET, 1)
        p.wait(v, ET)
        p.inc(v.tensor_copy(rsb, ps_r[:, 1024:2048]), EV, 1)
        for i in range(32):
            p.wait(v, EV)
            p.inc(v.scalar_tensor_tensor(ytile(i), ytile(i), nwr[:, i:i + 1],
                                         rsb, ALU.mult, ALU.mult), EV, 1)

    if "yn" in tap:
        p.wait(g, EV)
        for j in range(4):
            p.inc(g.dma_start(out=tap["yn"][j * 128:(j + 1) * 128, :],
                              in_=ub[:, j * TT:(j + 1) * TT]), QG, 16)
        p.wait(g, QG)
    if "sg" in tap:
        for j in range(16):
            p.inc(g.dma_start(out=zb[:, 0:TT], in_=gate_dram[j * 128:(j + 1) * 128, :]), QG, 16)
            p.wait(g, QG)
            p.inc(g.dma_start(out=tap["sg"][j * 128:(j + 1) * 128, :], in_=zb[:, 0:TT]), QG, 16)
            p.wait(g, QG)
    with (nc.psum_tensor("ps_o0", [128, 1024], F32) as o0,
          nc.psum_tensor("ps_o1", [128, 1024], F32) as o1,
          nc.psum_tensor("ps_o2", [128, 1024], F32) as o2,
          nc.psum_tensor("ps_o3", [128, 1024], F32) as o3):
        po4 = [o0, o1, o2, o3]
        for m2b in range(4):
            mc = slice(m2b * 512, (m2b + 1) * 512)
            for k2 in range(32):
                wbuf = 2048 + (k2 % 2) * 512
                p.wait(g, ET)
                p.inc(g.dma_start(out=cb[:, wbuf:wbuf + 512],
                                  in_=inp["outT"][k2 * 128:(k2 + 1) * 128, mc]),
                      QG, 16)
                p.wait(te, QG)
                if k2 == 0:
                    p.wait(te, EV)
                for m2 in range(4):
                    lhs = cb[:, wbuf + m2 * 128:wbuf + (m2 + 1) * 128]
                    for tch in range(2):
                        p.inc(te.matmul(
                            po4[m2][:, tch * 512:(tch + 1) * 512], lhs,
                            ytile(k2)[:, tch * 512:(tch + 1) * 512],
                            start=(k2 == 0), stop=(k2 == 31)), ET, 1)
            for m2 in range(4):
                d_i = m2b * 4 + m2
                p.wait(v, ET)
                p.inc(v.tensor_tensor(xr[:, d_i * TT:(d_i + 1) * TT],
                                      po4[m2][:, 0:TT],
                                      xr[:, d_i * TT:(d_i + 1) * TT], ALU.add),
                      EV, 1)

    if "ynpost" in tap:
        p.wait(g, ET)
        p.wait(g, EV)
        for j in range(8):
            p.inc(g.dma_start(out=tap["ynpost"][j * 128:(j + 1) * 128, :],
                              in_=ytile(j * 4)), QG, 16)
        p.wait(g, QG)
    if "x3" in tap:
        p.wait(sy, EV)
        for i in range(DT):
            p.inc(sy.dma_start(out=tap["x3"][i * 128:(i + 1) * 128, :],
                               in_=xr[:, i * TT:(i + 1) * TT]), QS, 16)
    if upto == "x3":
        finish()
        return nc

    ffn_block(first=False)

    finish()
    return nc


def _stage(inputs, core):
    b, half = core // 2, core % 2
    t0 = half * TT
    _ = inputs.get("L0_attn_b_mask")
    x = inputs["L0_attn_a_input"]
    d = {}
    d["xin"] = np.ascontiguousarray(np.asarray(x[b, t0:t0 + TT, :]).T)
    nonce = int(np.random.randint(1 << 20, 1 << 30))
    _stage.nonce = getattr(_stage, "nonce", nonce)
    d["meta"] = np.array([[half, 1 - half, _stage.nonce, 0, 0, 0, 0, 0]], np.int32)
    A = np.asarray
    d["wprj_cat"] = np.ascontiguousarray(np.concatenate(
        [A(inputs["L0_attn_a_W_dt_in"]).T,
         A(inputs["L0_attn_a_W_B"]).T,
         np.zeros((D, 16), np.float32),
         A(inputs["L0_attn_a_W_C"]).T], axis=1))
    d["wdt_outT_own"] = np.ascontiguousarray(
        A(inputs["L0_attn_a_W_dt_out"]).T[:, half * 1024:(half + 1) * 1024])
    d["bdt_own"] = np.ascontiguousarray(
        A(inputs["L0_attn_a_b_dt_out"]).reshape(16, 128).T[:, half * 8:half * 8 + 8])
    d["Da_own"] = np.ascontiguousarray(
        A(inputs["L0_attn_a_D"]).reshape(16, 128).T[:, half * 8:half * 8 + 8])
    d["alog_own"] = np.ascontiguousarray(
        A(inputs["L0_attn_a_A_log"]).reshape(16, 128, 16).transpose(1, 0, 2)
        .reshape(128, 256)[:, half * 128:(half + 1) * 128])
    d["inpT"] = np.ascontiguousarray(A(inputs["L0_attn_b_in_proj_w"]).T)
    cw = A(inputs["L0_attn_b_conv_w"]).reshape(34, 128, 4)
    cbv = A(inputs["L0_attn_b_conv_b"]).reshape(34, 128)
    own_cht = list(range(half * 16, half * 16 + 16)) + [32, 33]
    d["convw_own"] = np.ascontiguousarray(
        cw[own_cht].transpose(1, 0, 2).reshape(128, 72))
    d["convb_own"] = np.ascontiguousarray(cbv[own_cht].T)
    d["alogb_own"] = np.ascontiguousarray(
        A(inputs["L0_attn_b_A_log"]).reshape(NH, 1)[half * 32:half * 32 + 32])
    d["dtbias_own"] = np.ascontiguousarray(
        A(inputs["L0_attn_b_dt_bias"]).reshape(NH, 1)[half * 32:half * 32 + 32])
    d["Db_own"] = np.ascontiguousarray(
        np.repeat(A(inputs["L0_attn_b_D"]), PDIM).reshape(32, 128).T
        [:, half * 16:half * 16 + 16])
    d["normw_r"] = np.ascontiguousarray(
        A(inputs["L0_attn_b_norm_w"]).reshape(32, 128).T)
    d["outT"] = np.ascontiguousarray(A(inputs["L0_attn_b_out_proj_w"]).T)
    fcT = A(inputs["L0_ffn_w_fc"]).T
    pjT = A(inputs["L0_ffn_w_proj"]).T
    d["fcT_half"] = np.ascontiguousarray(fcT[half * (D // 2):(half + 1) * (D // 2)])
    d["projT_half"] = np.ascontiguousarray(
        pjT[half * (FFH // 2):(half + 1) * (FFH // 2)])
    return {k: np.asarray(v2) for k, v2 in d.items()}


def kernel(**inputs):
    _stage.nonce = int(np.random.randint(1 << 20, 1 << 30))
    nc = build_program()
    in_maps = [_stage(inputs, c) for c in range(8)]
    res = run_bass_kernel_spmd(nc, in_maps, core_ids=list(range(8)))
    out = np.zeros((B, S, D), np.float32)
    for c in range(8):
        b, half = c // 2, c % 2
        out[b, half * TT:(half + 1) * TT, :] = res.results[c]["xout"].T
    return out
